# version 19
# baseline (speedup 1.0000x reference)
import contextlib
import math

import numpy as np
import ml_dtypes

import concourse.bass as bass
import concourse.mybir as mybir
from concourse.bass_utils import run_bass_kernel_spmd

F32 = mybir.dt.float32
BF16 = mybir.dt.bfloat16
AF = mybir.ActivationFunctionType
ALU = mybir.AluOpType
AX = mybir.AxisListType
AP = bass.AP

B, S, D = 4, 2048, 2048
H, HD = 16, 128
INTER = 8192
NCTRL = 130156
NSAMP = INTER * D
NBLK = NSAMP // 128
GCH = 2048
NGCH = NBLK // GCH
DCH = 4096
NDCH = NBLK // DCH
TPC = 1024
SCALE = 1.0 / math.sqrt(HD)
INV_D = 1.0 / D
EPS = 1e-5
NCORES = 8
CTX = (8, 16)


_CONSTS = None


def _consts():
    global _CONSTS
    if _CONSTS is not None:
        return _CONSTS
    t = np.linspace(0.0, NCTRL - 1.0, NSAMP, dtype=np.float64)
    iarr = np.clip(np.floor(t).astype(np.int64), 0, NCTRL - 2)
    i_b = iarr[::128]
    tau = t[::128] - i_b
    s_b = (iarr.reshape(NBLK, 128) == i_b[:, None]).sum(1)
    R = (NCTRL - 1.0) / (NSAMP - 1)

    Cc = np.array([
        [0.0, -0.5, 1.0, -0.5],
        [1.0, 0.0, -2.5, 1.5],
        [0.0, 0.5, 2.0, -1.5],
        [0.0, 0.0, -0.5, 0.5]], dtype=np.float64)
    from math import comb
    qR = np.arange(128, dtype=np.float64) * R
    at = np.zeros((16, 128), dtype=np.float64)
    for k in range(4):
        for m in range(4):
            v = np.zeros_like(qR)
            for jj in range(m, 4):
                v += comb(jj, m) * Cc[k, jj] * qR ** (jj - m)
            at[4 * k + m] = v
    tp = np.zeros((8, NBLK), dtype=np.float64)
    for m in range(4):
        tp[m] = tau ** m
        tp[4 + m] = (tau - 1.0) ** m
    mask = (np.arange(128)[:, None] < s_b[None, :])
    tapidx = np.stack([np.clip(i_b + r - 1, 0, NCTRL - 1) for r in range(5)])

    cmask = {}
    for j in range(2):
        gk = [j, 1 - j, 2 + j, 3 - j]
        cm = np.zeros((16, 128, 512), dtype=np.float64)
        for cr in range(16):
            tslot = 0 if cr < 8 else 1
            gt = 2 * tslot + j
            tk = gk[cr // 4] * 512 + (cr % 4) * 128 + np.arange(128)[:, None]
            tq = gt * 512 + np.arange(512)[None, :]
            cm[cr] = (tk <= tq)
        cmask[j] = cm.astype(ml_dtypes.bfloat16)

    r0_tap = np.zeros(128, dtype=np.int64)
    r0_wt = np.zeros(128, dtype=np.int64)
    r0_tau = np.zeros(128, dtype=np.int64)
    for p in range(128):
        si, r32 = p // 32, p % 32
        k, m = (r32 % 16) // 4, r32 % 4
        r0_wt[p] = si // 2
        r0_tap[p] = k + (si % 2)
        r0_tau[p] = (si % 2) * 4 + m
    r1_tap = np.zeros(64, dtype=np.int64)
    r1_tau = np.zeros(64, dtype=np.int64)
    for p in range(64):
        si, r32 = p // 32, p % 32
        k, m = (r32 % 16) // 4, r32 % 4
        r1_tap[p] = k + si
        r1_tau[p] = si * 4 + m
    _CONSTS = dict(
        tapidx=tapidx, s_b=s_b,
        r0_tap=r0_tap, r0_wt=r0_wt, r0_tau=r0_tau,
        r1_tap=r1_tap, r1_tau=r1_tau,
        atilde=at.astype(np.float32),
        taupow=tp.astype(np.float32),
        wgmask=mask.astype(np.uint8),
        ident=np.eye(128, dtype=ml_dtypes.bfloat16),
        ones=np.ones((128, 128), dtype=ml_dtypes.bfloat16),
        cmask=cmask,
    )
    return _CONSTS


class Sched:
    ENGINES = ("T", "V", "S", "G", "Y")

    def __init__(self):
        self.recs = {e: [] for e in self.ENGINES}
        self.cnt = {e.lower(): 0 for e in self.ENGINES}
        self.pend = {e: [] for e in self.ENGINES}
        self.streams = {}
        self.issuer = {}
        self.open_streams = {e: set() for e in self.ENGINES}

    CHAINED = ("V", "S", "G")

    def op(self, e, fn):
        w = list(self.pend[e])
        self.pend[e] = []
        incs = []
        if e in self.CHAINED:
            sem = e.lower()
            if self.cnt[sem]:
                w.append((sem, self.cnt[sem]))
            self.cnt[sem] += 1
            incs.append((sem, 1, self.cnt[sem]))
        self.recs[e].append([fn, tuple(w), incs, False])

    def mark(self, e):
        rec = self.recs[e][-1]
        if rec[3]:
            sem, _n, tgt = rec[2][0]
            return (sem, tgt)
        if rec[2]:
            sem, _n, tgt = rec[2][0]
            return (sem, tgt)
        sem = e.lower()
        self.cnt[sem] += 1
        rec[2].append((sem, 1, self.cnt[sem]))
        return (sem, self.cnt[sem])

    def dma(self, e, out, in_, stream):
        sname = "q_" + stream
        assert self.issuer.setdefault(sname, e) == e, (sname, e)
        self.streams[sname] = self.streams.get(sname, 0) + 16
        tgt = self.streams[sname]
        self.open_streams[e].add(sname)
        w = tuple(self.pend[e])
        self.pend[e] = []
        self.recs[e].append(
            [lambda eng, o=out, i=in_: eng.dma_start(out=o, in_=i), w,
             [(sname, 16, tgt)], True])
        return (sname, tgt)

    def wait(self, e, tok):
        if tok is not None:
            self.pend[e].append(tok)

    def barrier(self):
        toks = []
        for e in self.ENGINES:
            if not self.recs[e] and not self.open_streams[e]:
                continue
            if self.open_streams[e]:
                for sname in sorted(self.open_streams[e]):
                    self.pend[e].append((sname, self.streams[sname]))
                self.open_streams[e] = set()
                self.op(e, lambda eng: eng.nop())
            toks.append(self.mark(e))
        for e in self.ENGINES:
            for t in toks:
                self.pend[e].append(t)

    def finish(self, e):
        for sname in sorted(self.open_streams[e]):
            self.pend[e].append((sname, self.streams[sname]))
        self.op(e, lambda eng: eng.nop())

    def emit(self, nc, ctx):
        sems = {}
        for sn in self.cnt:
            sems[sn] = ctx.enter_context(nc.semaphore("sem_" + sn))
        for sn in self.streams:
            sems[sn] = ctx.enter_context(nc.semaphore(sn))

        def replay(e):
            def f(eng):
                for fn, waits, incs, _isdma in self.recs[e]:
                    agg = {}
                    for (sn, v) in waits:
                        agg[sn] = max(agg.get(sn, 0), v)
                    for sn, v in agg.items():
                        eng.wait_ge(sems[sn], v)
                    inst = fn(eng)
                    for (sn, n, _tgt) in incs:
                        inst.then_inc(sems[sn], n)
            return f
        with nc.Block() as blk:
            blk.tensor(replay("T"))
            blk.vector(replay("V"))
            blk.scalar(replay("S"))
            blk.gpsimd(replay("G"))
            blk.sync(replay("Y"))


def _emit_sumav(sch, it, c, C, hb, tslot, pT, vT, prow, pav, ones, expmark,
                rmk, attn_emk):
    sch.wait("T", expmark[(it, c)])
    if c == 0:
        sch.wait("T", rmk.get(it - 1))
        sch.wait("T", attn_emk.get(it - 2))
    sch.op("T", lambda t, b=tslot, c_=c, C_=C:
           t.matmul(prow[:, :], ones[:, 0:1],
                    pT[b][:, c_ * 512:c_ * 512 + 512],
                    start=(c_ == 0), stop=(c_ == C_ - 1)))
    sch.op("T", lambda t, b=hb, bb=tslot, c_=c, C_=C, pa=pav[it % 2]:
           t.matmul(pa[:, :], vT[b][:, c_ * 128:c_ * 128 + 128],
                    pT[bb][:, c_ * 512:c_ * 512 + 512],
                    start=(c_ == 0), stop=(c_ == C_ - 1)))


def _own_tile(to):
    return to if to < 4 else to + 4


def _emit_ln(sch, col, c0, xap, sqscratch, out_bf=None, out_f32=None):
    sch.op("V", lambda v: v.reduce_sum(col[:, c0:c0 + 1], xap, axis=AX.X))
    sch.op("S", lambda s: s.activation(sqscratch[:, :], xap, AF.Square,
                                       bias=0.0, scale=1.0,
                                       accum_out=col[:, c0 + 1:c0 + 2]))
    sqm = sch.mark("S")
    sch.op("V", lambda v: v.tensor_scalar(col[:, c0 + 2:c0 + 3],
                                          col[:, c0:c0 + 1], INV_D, None,
                                          ALU.mult))
    sch.op("V", lambda v: v.tensor_mul(col[:, c0 + 3:c0 + 4],
                                       col[:, c0 + 2:c0 + 3],
                                       col[:, c0 + 2:c0 + 3]))
    sch.wait("V", sqm)
    sch.op("V", lambda v: v.tensor_scalar(col[:, c0 + 4:c0 + 5],
                                          col[:, c0 + 1:c0 + 2], INV_D,
                                          col[:, c0 + 3:c0 + 4],
                                          ALU.mult, ALU.subtract))
    vmk = sch.mark("V")
    sch.wait("S", vmk)
    sch.op("S", lambda s: s.activation(col[:, c0 + 5:c0 + 6],
                                       col[:, c0 + 4:c0 + 5], AF.Sqrt,
                                       bias=EPS, scale=1.0))
    smk = sch.mark("S")
    sch.wait("V", smk)
    sch.op("V", lambda v: v.reciprocal(col[:, c0 + 6:c0 + 7],
                                       col[:, c0 + 5:c0 + 6]))
    sch.op("V", lambda v: v.tensor_mul(col[:, c0 + 7:c0 + 8],
                                       col[:, c0 + 2:c0 + 3],
                                       col[:, c0 + 6:c0 + 7]))
    if out_bf is not None:
        sch.op("V", lambda v: v.tensor_scalar(out_bf[:, :], xap,
                                              col[:, c0 + 6:c0 + 7],
                                              col[:, c0 + 7:c0 + 8],
                                              ALU.mult, ALU.subtract))
    if out_f32 is not None:
        sch.op("V", lambda v: v.tensor_scalar(out_f32[:, :], xap,
                                              col[:, c0 + 6:c0 + 7],
                                              col[:, c0 + 7:c0 + 8],
                                              ALU.mult, ALU.subtract))


_PROG = None
PHASES = ("W", "B", "C", "D", "E")


def build_program():
    nc = bass.Bass("TRN2", target_bir_lowering=False)
    sch = Sched()

    x_row = nc.dram_tensor("x_row", [S, D], F32, kind="ExternalInput")
    wqkvT = nc.dram_tensor("wqkvT", [D, 3 * D], F32, kind="ExternalInput")
    bqkv_d = nc.dram_tensor("bqkv", [128, 48], F32, kind="ExternalInput")
    woutT = nc.dram_tensor("woutT", [D, D], F32, kind="ExternalInput")
    bout_d = nc.dram_tensor("bout_rep", [128, D], F32, kind="ExternalInput")
    tap_gu_d = nc.dram_tensor("tap_gu", [10, NBLK], F32, kind="ExternalInput")
    tau_gu_d = nc.dram_tensor("tau_gu", [32, NBLK], F32, kind="ExternalInput")
    tap_dn_d = nc.dram_tensor("tap_dn", [5, NBLK], F32, kind="ExternalInput")
    tau_dn_d = nc.dram_tensor("tau_dn", [32, NBLK], F32, kind="ExternalInput")
    at_d = nc.dram_tensor("atilde", [16, 128], F32, kind="ExternalInput")
    wgm_gu_d = nc.dram_tensor("wgmask_gu", [128, NBLK], mybir.dt.uint8,
                              kind="ExternalInput")
    wgm_dn_d = nc.dram_tensor("wgmask_dn", [128, NBLK], mybir.dt.uint8,
                              kind="ExternalInput")
    cm_d = nc.dram_tensor("cmask", [16, 128, 512], BF16, kind="ExternalInput")
    id_d = nc.dram_tensor("ident", [128, 128], BF16, kind="ExternalInput")
    on_d = nc.dram_tensor("ones", [128, 128], BF16, kind="ExternalInput")
    out_d = nc.dram_tensor("out", [TPC, D], F32, kind="ExternalOutput")

    wbuf_g = nc.dram_tensor("wbuf_g", [128, NBLK], BF16)
    wbuf_u = nc.dram_tensor("wbuf_u", [128, NBLK], BF16)
    wbuf_d = nc.dram_tensor("wbuf_d", [128, 64, D], BF16)
    kvbuf = nc.dram_tensor("kvbuf", [2 * H, 128, S], BF16)
    xmid = nc.dram_tensor("xmid", [TPC, D], F32)

    ctx = contextlib.ExitStack()
    with ctx:
        csb = ctx.enter_context(nc.sbuf_tensor("constsb", [128, 4], F32))
        for ci, cv in enumerate((0.0, EPS, 1.0, SCALE)):
            nc.const_aps.register(F32, float(cv), csb[:, ci:ci + 1])
            sch.op("V", lambda v, i=ci, val=float(cv):
                   v.memset(csb[:, i:i + 1], val))
        ident = ctx.enter_context(nc.sbuf_tensor("identsb", [128, 128], BF16))
        ones = ctx.enter_context(nc.sbuf_tensor("onessb", [128, 128], BF16))
        at_sb = ctx.enter_context(nc.sbuf_tensor("atsb", [128, 128], BF16))
        bqkv = ctx.enter_context(nc.sbuf_tensor("bqkvsb", [128, 48], F32))
        col = ctx.enter_context(nc.sbuf_tensor("colsb", [128, 16], F32))
        lnsq = ctx.enter_context(nc.sbuf_tensor("lnsqsb", [128, D], BF16))
        lnsq2 = ctx.enter_context(nc.sbuf_tensor("lnsqsb2", [128, D], BF16))
        lnsqs = [lnsq, lnsq2]

        sch.dma("Y", ident[:, :], id_d[:, :], "c0")
        sch.dma("Y", ones[:, :], on_d[:, :], "c0")
        sch.dma("Y", bqkv[:, :], bqkv_d[:, :], "c0")
        for u in (0, 32, 64, 96):
            sch.dma("G", at_sb[u:u + 16, :], at_d[:, :], "c1")
        sch.barrier()

        def make_gu_gen(wc, pA, pB, depth):
            CT = [wc.enter_context(nc.sbuf_tensor(f"gct{i}", [42, GCH], BF16))
                  for i in range(3)]
            U = [wc.enter_context(nc.sbuf_tensor(f"Ug{i}", [128, GCH], BF16))
                 for i in range(2)]
            IN0 = [wc.enter_context(nc.sbuf_tensor(f"gi0_{i}", [128, GCH], BF16))
                   for i in range(2)]
            IN1 = [wc.enter_context(nc.sbuf_tensor(f"gi1_{i}", [128, GCH], BF16))
                   for i in range(2)]
            MSK = [wc.enter_context(
                nc.sbuf_tensor(f"gm_{i}", [128, GCH], mybir.dt.uint8))
                for i in range(4)]
            WO = [wc.enter_context(nc.sbuf_tensor(f"gwo_{i}", [128, GCH], BF16))
                  for i in range(4)]
            streams = [(wbuf_g, 0, 32), (wbuf_u, 64, 96)]
            for t in IN0 + IN1:
                sch.op("V", lambda v, t_=t: v.memset(t_[:, :], 0.0))
            msmk_gu = sch.mark("V")

            def ldtoks(c):
                sch.dma("G", CT[c % 3][0:10, :],
                        AP(tap_gu_d, c * GCH, [[NBLK, 10], [1, GCH]]),
                        f"wg{c % 3}")
                return sch.dma("G", CT[c % 3][10:42, :],
                               AP(tau_gu_d, c * GCH, [[NBLK, 32], [1, GCH]]),
                               f"wg{c % 3}")

            def gen():
                umark, evmark, stok, mtoks = {}, {}, {}, {}
                ltok, etok = {}, {}

                def expand(c):
                    sch.wait("S", ltok[c])
                    if c >= 2:
                        sch.wait("S", umark[c - 2])
                    else:
                        sch.wait("S", msmk_gu)
                    for si in range(4):
                        wt, dlt = si // 2, si % 2
                        sch.dma("S", AP(IN0[c % 2], (si * 32) * GCH,
                                        [[GCH, 16], [1, GCH]]),
                                AP(CT[c % 3], (wt * 5 + dlt) * GCH,
                                   [[GCH, 4], [0, 4], [1, GCH]]),
                                f"we{c % 2}")
                        etok[c] = sch.dma(
                            "S", AP(IN1[c % 2], (si * 32) * GCH,
                                    [[GCH, 16], [1, GCH]]),
                            AP(CT[c % 3], (10 + dlt * 16) * GCH,
                               [[GCH, 16], [1, GCH]]),
                            f"we{c % 2}")

                def umult(c):
                    sch.wait("V", etok[c])
                    if c >= 2:
                        sch.wait("V", evmark[(c - 2) * 8 + 7])
                    sch.op("V", lambda v, b=c % 2: v.tensor_mul(
                        U[b][:, :], IN0[b][:, :], IN1[b][:, :]))
                    umark[c] = sch.mark("V")

                for c in range(NGCH):
                    yield
                    buf = c % 2
                    if c == 0:
                        for cpre in (0, 1, 2):
                            mtoks[cpre] = sch.dma(
                                "S", MSK[cpre % 4][:, :],
                                wgm_gu_d[:, cpre * GCH:(cpre + 1) * GCH],
                                f"gm{cpre % 4}")
                        ltok[0] = ldtoks(0)
                        ltok[1] = ldtoks(1)
                        expand(0)
                        umult(0)
                    if c + 2 < NGCH:
                        if c >= 1:
                            sch.wait("G", etok[c - 1])
                        ltok[c + 2] = ldtoks(c + 2)
                    sch.wait("T", umark[c])
                    for w in range(2):
                        wb, uA, uB = streams[w]
                        cw = c * 2 + w
                        first_ev = True
                        for q4 in range(4):
                            yield
                            i2 = c * 8 + w * 4 + q4
                            bk = i2 % depth
                            if i2 >= depth:
                                sch.wait("T", evmark[i2 - depth])
                            for u, po in ((uA, pA[bk]), (uB, pB[bk])):
                                sch.op("T", lambda t, u_=u, b=buf, pp=po,
                                       cc=q4 * 512:
                                       t.matmul(pp[:, :],
                                                at_sb[u_:u_ + 16, :],
                                                U[b][u_:u_ + 16, cc:cc + 512],
                                                start=True, stop=True,
                                                tile_position=(u_, 0)))
                            mm = sch.mark("T")
                            sch.wait("S", mm)
                            if first_ev:
                                if cw >= 4:
                                    sch.wait("S", stok[cw - 4])
                                if w == 0:
                                    sch.wait("V", mtoks[c])
                                first_ev = False
                            wo_ap = WO[cw % 4][:, q4 * 512:(q4 + 1) * 512]
                            mk_ap = MSK[c % 4][:, q4 * 512:(q4 + 1) * 512]
                            sch.op("S", lambda s, o_=wo_ap, pp=pB[bk]:
                                   s.copy(o_, pp[:, :]))
                            smk = sch.mark("S")
                            sch.wait("V", smk)
                            sch.op("V", lambda v, o_=wo_ap, m_=mk_ap,
                                   pp=pA[bk]:
                                   v.copy_predicated(o_, m_, pp[:, :]))
                            evmark[i2] = sch.mark("V")
                        sch.wait("Y", evmark[c * 8 + w * 4 + 3])
                        stok[cw] = sch.dma(
                            "Y", wb[:, c * GCH:(c + 1) * GCH],
                            WO[cw % 4][:, :],
                            f"ws{cw % 4}")
                        if w == 0 and c + 1 < NGCH:
                            expand(c + 1)
                    if c + 3 < NGCH:
                        if c >= 1:
                            sch.wait("S", evmark[(c - 1) * 8 + 7])
                        mtoks[c + 3] = sch.dma(
                            "S", MSK[(c + 3) % 4][:, :],
                            wgm_gu_d[:, (c + 3) * GCH:(c + 4) * GCH],
                            f"gm{(c + 3) % 4}")
                    if c + 1 < NGCH:
                        umult(c + 1)
                while True:
                    yield
            return gen()

        def make_down_gen(wc, pA, pB, depth):
            CT = [wc.enter_context(nc.sbuf_tensor(f"dct{i}", [37, DCH], BF16))
                  for i in range(3)]
            Ud = [wc.enter_context(nc.sbuf_tensor(f"Ud{i}", [64, DCH], BF16))
                  for i in range(2)]
            IN0 = [wc.enter_context(nc.sbuf_tensor(f"di0_{i}", [64, DCH], BF16))
                   for i in range(2)]
            IN1 = [wc.enter_context(nc.sbuf_tensor(f"di1_{i}", [64, DCH], BF16))
                   for i in range(2)]
            MSK = [wc.enter_context(
                nc.sbuf_tensor(f"dm_{i}", [128, DCH], mybir.dt.uint8))
                for i in range(4)]
            WO = [wc.enter_context(nc.sbuf_tensor(f"dwo_{i}", [128, DCH], BF16))
                  for i in range(4)]
            NCG = DCH // 512
            for t in IN0 + IN1:
                sch.op("V", lambda v, t_=t: v.memset(t_[:, :], 0.0))
            msmk_dn = sch.mark("V")

            def ldtoks(c):
                sch.dma("G", CT[c % 3][0:5, :],
                        AP(tap_dn_d, c * DCH, [[NBLK, 5], [1, DCH]]),
                        f"wd{c % 3}")
                return sch.dma("G", CT[c % 3][5:37, :],
                               AP(tau_dn_d, c * DCH, [[NBLK, 32], [1, DCH]]),
                               f"wd{c % 3}")

            def gen():
                umark, evmark, stok, mtoks = {}, {}, {}, {}
                ltok, etok = {}, {}

                def expand(c):
                    sch.wait("S", ltok[c])
                    if c >= 2:
                        sch.wait("S", umark[c - 2])
                    else:
                        sch.wait("S", msmk_dn)
                    for si in range(2):
                        sch.dma("S", AP(IN0[c % 2], (si * 32) * DCH,
                                        [[DCH, 16], [1, DCH]]),
                                AP(CT[c % 3], si * DCH,
                                   [[DCH, 4], [0, 4], [1, DCH]]),
                                f"de{c % 2}")
                        etok[c] = sch.dma(
                            "S", AP(IN1[c % 2], (si * 32) * DCH,
                                    [[DCH, 16], [1, DCH]]),
                            AP(CT[c % 3], (5 + si * 16) * DCH,
                               [[DCH, 16], [1, DCH]]),
                            f"de{c % 2}")

                def umult(c):
                    sch.wait("V", etok[c])
                    if c >= 2:
                        sch.wait("V", evmark[(c - 2) * NCG + NCG - 1])
                    sch.op("V", lambda v, b=c % 2: v.tensor_mul(
                        Ud[b][:, :], IN0[b][:, :], IN1[b][:, :]))
                    umark[c] = sch.mark("V")

                for c in range(NDCH):
                    yield
                    buf = c % 2
                    if c == 0:
                        for cpre in (0, 1, 2):
                            mtoks[cpre] = sch.dma(
                                "S", MSK[cpre % 4][:, :],
                                wgm_dn_d[:, cpre * DCH:(cpre + 1) * DCH],
                                f"dm{cpre % 4}")
                        ltok[0] = ldtoks(0)
                        ltok[1] = ldtoks(1)
                        expand(0)
                        umult(0)
                    if c + 2 < NDCH:
                        if c >= 1:
                            sch.wait("G", etok[c - 1])
                        ltok[c + 2] = ldtoks(c + 2)
                    sch.wait("T", umark[c])
                    first_ev = True
                    for cg in range(NCG):
                        yield
                        i2 = c * NCG + cg
                        bk = i2 % depth
                        if i2 >= depth:
                            sch.wait("T", evmark[i2 - depth])
                        for u, po in ((0, pA[bk]), (32, pB[bk])):
                            sch.op("T", lambda t, u_=u, b=buf, pp=po,
                                   cc=cg * 512:
                                   t.matmul(pp[:, :], at_sb[u_:u_ + 16, :],
                                            Ud[b][u_:u_ + 16, cc:cc + 512],
                                            start=True, stop=True,
                                            tile_position=(u_, 0)))
                        mm = sch.mark("T")
                        sch.wait("S", mm)
                        if first_ev:
                            if c >= 4:
                                sch.wait("S", stok[c - 4])
                            sch.wait("V", mtoks[c])
                            first_ev = False
                        wo_ap = WO[c % 4][:, cg * 512:(cg + 1) * 512]
                        mk_ap = MSK[c % 4][:, cg * 512:(cg + 1) * 512]
                        sch.op("S", lambda s, o_=wo_ap, pp=pB[bk]:
                               s.copy(o_, pp[:, :]))
                        smk = sch.mark("S")
                        sch.wait("V", smk)
                        sch.op("V", lambda v, o_=wo_ap, m_=mk_ap, pp=pA[bk]:
                               v.copy_predicated(o_, m_, pp[:, :]))
                        evmark[i2] = sch.mark("V")
                        if cg == NCG // 2 and c + 1 < NDCH:
                            expand(c + 1)
                    od = DCH // 64
                    sch.wait("Y", evmark[c * NCG + NCG - 1])
                    stok[c] = sch.dma(
                        "Y",
                        AP(wbuf_d, c * od,
                           [[64 * D, 128], [D, 64], [1, od]]),
                        AP(WO[c % 4], 0, [[DCH, 128], [od, 64], [1, od]]),
                        f"dws{c % 4}")
                    if c + 3 < NDCH:
                        if c >= 1:
                            sch.wait("S", evmark[(c - 1) * NCG + NCG - 1])
                        mtoks[c + 3] = sch.dma(
                            "S", MSK[(c + 3) % 4][:, :],
                            wgm_dn_d[:, (c + 3) * DCH:(c + 4) * DCH],
                            f"dm{(c + 3) % 4}")
                    if c + 1 < NDCH:
                        umult(c + 1)
                while True:
                    yield
            return gen()

        qscope = ctx.enter_context(contextlib.ExitStack())
        q_all = qscope.enter_context(
            nc.sbuf_tensor("qall", [128, H * TPC], BF16))
        if "W" in PHASES:
            with contextlib.ExitStack() as wsc:
                pguA = [wsc.enter_context(
                    nc.psum_tensor(f"pguA{i}", [128, 512], F32))
                    for i in range(4)]
                pguB = [wsc.enter_context(
                    nc.psum_tensor(f"pguB{i}", [128, 512], F32))
                    for i in range(4)]
                gugen = make_gu_gen(wsc, pguA, pguB, depth=4)
                for _ in range(NGCH * 9 + 20):
                    next(gugen)
            sch.barrier()

        if "B" in PHASES:
            with contextlib.ExitStack() as bc:

                x2T = bc.enter_context(nc.sbuf_tensor("x2T", [128, 16 * S], BF16))
                xt = [bc.enter_context(nc.sbuf_tensor(f"xt{i}", [128, D], F32))
                      for i in range(2)]
                wq = [bc.enter_context(nc.sbuf_tensor(f"wq{i}", [128, 4096], BF16))
                      for i in range(2)]
                kst = [bc.enter_context(nc.sbuf_tensor(f"kst{i}", [128, 512], BF16))
                       for i in range(4)]
                with contextlib.ExitStack() as tc:
                    ptr = [tc.enter_context(
                        nc.psum_tensor(f"ptr{i}", [128, 1024], BF16))
                        for i in range(2)]
                    ln_ev, nmks, trmk = {}, {}, {}

                    def b_transpose(ti):
                        for hc in range(16):
                            p = ptr[hc // 8]
                            if hc % 8 == 0:
                                sch.wait("T", nmks[ti])
                                if ti >= 1:
                                    sch.wait("T", ln_ev.get((ti - 1, hc // 8)))
                            sch.op("T", lambda t, p_=p, h=hc, b=ti % 2:
                                   t.matmul(p_[:, (h % 8) * 128:(h % 8) * 128 + 128],
                                            lnsqs[b][:, h * 128:h * 128 + 128],
                                            ident[:, :], is_transpose=True,
                                            start=(h % 8 == 0), stop=(h % 8 == 7)))
                            if hc % 8 == 7:
                                tmk = sch.mark("T")
                                trmk[ti] = tmk
                                sch.wait("V", tmk)
                                sch.op("V", lambda v, p_=p, t_=ti, g=hc // 8:
                                       v.tensor_copy(
                                           AP(x2T, (g * 8) * S + t_ * 128,
                                              [[16 * S, 128], [S, 8], [1, 128]]),
                                           p_[:, :]))
                                ln_ev[(ti, hc // 8)] = sch.mark("V")

                    for ti in range(16):
                        bufp = ti % 2
                        if ti >= 2:
                            sch.wait("Y", nmks[ti - 2])
                        xtok = sch.dma("Y", xt[bufp][:, :],
                                       x_row[ti * 128:(ti + 1) * 128, :],
                                       f"bx{ti % 2}")
                        sch.wait("V", xtok)
                        sch.wait("S", xtok)
                        if ti >= 2:
                            sch.wait("S", trmk[ti - 2])
                            sch.wait("V", trmk[ti - 2])
                        _emit_ln(sch, col, 8 * bufp, xt[bufp][:, :],
                                 lnsqs[bufp], out_bf=lnsqs[bufp])
                        nmks[ti] = sch.mark("V")
                        if ti >= 1:
                            b_transpose(ti - 1)
                    b_transpose(15)
                    sch.barrier()
                with contextlib.ExitStack() as qc:
                    pq = [qc.enter_context(
                        nc.psum_tensor(f"pq{i}", [128, 512], F32))
                        for i in range(4)]
                    evtix, wtok, kdma, qmmk = {}, {}, {}, {}
                    for oc in range(48):
                        if oc % 2 == 0:
                            if oc >= 4:
                                sch.wait("G", qmmk[oc - 3])
                            wtok[oc // 2] = sch.dma(
                                "G", wq[(oc // 2) % 2][:, :],
                                AP(wqkvT, oc * 128,
                                   [[3 * D, 128], [128 * 3 * D, 16], [1, 256]]),
                                f"wq{(oc // 2) % 2}")
                        tgs = [0, 2] if oc < 16 else [0, 1, 2, 3]
                        kind, hh = oc // 16, oc % 16
                        sch.wait("T", wtok[oc // 2])
                        for tix, tg in enumerate(tgs):
                            sch.wait("T", evtix.get((oc - 1, tix)))
                            for hc in range(16):
                                sch.op("T", lambda t, h=hc, o=oc, g=tg, bi=tix:
                                       t.matmul(
                                           pq[bi][:, :],
                                           wq[(o // 2) % 2][:, h * 256 + (o % 2) * 128:
                                                            h * 256 + (o % 2) * 128 + 128],
                                           x2T[:, h * S + g * 512:h * S + g * 512 + 512],
                                           start=(h == 0), stop=(h == 15)))
                            qmmk[oc] = qmm = sch.mark("T")
                            sch.wait("V", qmm)
                            if kind == 0:
                                dst = q_all[:, hh * TPC + (tg // 2) * 512:
                                            hh * TPC + (tg // 2) * 512 + 512]
                                sch.op("V", lambda v, d=dst, bi=tix, o=oc:
                                       v.tensor_scalar(
                                           d, pq[bi][:, :],
                                           bqkv[:, o:o + 1], None, ALU.add))
                            else:
                                if kdma.get(tix) is not None:
                                    sch.wait("V", kdma[tix])
                                sch.op("V", lambda v, d=kst[tix],
                                       bi=tix, o=oc:
                                       v.tensor_scalar(
                                           d[:, :], pq[bi][:, :],
                                           bqkv[:, o:o + 1], None, ALU.add))
                                vmk = sch.mark("V")
                                sch.wait("Y", vmk)
                                kdma[tix] = sch.dma(
                                    "Y",
                                    AP(kvbuf, ((kind - 1) * 16 + hh) * 128 * S
                                       + tg * 512, [[S, 128], [1, 512]]),
                                    kst[tix][:, :], f"kv{tix}")
                            evtix[(oc, tix)] = sch.mark("V")
                    if gugen is not None:
                        for _ in range(24):
                            next(gugen)
            sch.barrier()

        attnout = qscope.enter_context(
            nc.sbuf_tensor("attnout", [128, H * TPC], BF16))
        if "C" in PHASES:
            with contextlib.ExitStack() as cc:
                cmask = cc.enter_context(
                    nc.sbuf_tensor("cmasksb", [128, 16 * 512], BF16))
                cmtok = sch.dma(
                    "Y", AP(cmask, 0, [[16 * 512, 128], [512, 16], [1, 512]]),
                    AP(cm_d, 0, [[512, 128], [128 * 512, 16], [1, 512]]), "c2")
                sch.wait("V", cmtok)
                kb = [cc.enter_context(nc.sbuf_tensor(f"kb{i}", [128, S], BF16))
                      for i in range(2)]
                vb = [cc.enter_context(nc.sbuf_tensor(f"vb{i}", [128, S], BF16))
                      for i in range(2)]
                vT = [cc.enter_context(nc.sbuf_tensor(f"vT{i}", [128, S], BF16))
                      for i in range(2)]
                pT = [cc.enter_context(nc.sbuf_tensor(f"pT{i}", [128, 16 * 512], BF16))
                      for i in range(2)]
                rs = cc.enter_context(nc.sbuf_tensor("rs", [1, 512], BF16))
                bc_sb = cc.enter_context(nc.sbuf_tensor("bcsb", [128, 512], BF16))
                psc = [cc.enter_context(nc.psum_tensor(f"psc{i}", [128, 512], F32))
                       for i in range(4)]
                pvt = cc.enter_context(nc.psum_tensor("pvt", [128, 1024], BF16))
                pav = [cc.enter_context(nc.psum_tensor(f"pav{i}", [128, 512], F32))
                       for i in range(2)]
                prow = cc.enter_context(nc.psum_tensor("prow", [1, 512], F32))

                vtev = {}
                expmark = {}
                scmark = {}
                attn_emk = {}
                rmk = {}
                pending_tail = [None]

                def emit_tail(it_p):
                    h_p, ts_p = it_p // 2, it_p % 2
                    sch.wait("T", rmk[it_p])
                    if it_p >= 1:
                        sch.wait("T", attn_emk.get(it_p - 1))
                    sch.op("T", lambda t: t.matmul(
                        psc[2][:, :], ones[0:1, 0:128], rs[:, :],
                        start=True, stop=True))
                    bmk = sch.mark("T")
                    sch.wait("S", bmk)
                    sch.op("S", lambda s: s.copy(bc_sb[:, :], psc[2][:, :]))
                    csm = sch.mark("S")
                    sch.wait("V", csm)
                    sch.op("V", lambda v, ts=ts_p, hh=h_p, pa=pav[it_p % 2]:
                           v.tensor_mul(
                               attnout[:, hh * TPC + ts * 512:
                                       hh * TPC + ts * 512 + 512],
                               pa[:, :], bc_sb[:, :]))
                    attn_emk[it_p] = sch.mark("V")

                for h in range(H):
                    hb = h % 2
                    if h >= 2:
                        sch.wait("Y", attn_emk[2 * (h - 2) + 1])
                    ktok = sch.dma("Y", kb[hb][:, :],
                                   AP(kvbuf, h * 128 * S, [[S, 128], [1, S]]),
                                   f"ck{h % 2}")
                    vtok = sch.dma("Y", vb[hb][:, :],
                                   AP(kvbuf, (16 + h) * 128 * S, [[S, 128], [1, S]]),
                                   f"cv{h % 2}")
                    sch.wait("T", vtok)
                    for g2 in range(2):
                        if (h, g2) != (0, 0):
                            prevk = (h, g2 - 1) if g2 else (h - 1, 1)
                            sch.wait("T", vtev[prevk])
                        for tkc in range(8 * g2, 8 * g2 + 8):
                            sch.op("T", lambda t, c=tkc, b=hb:
                                   t.matmul(pvt[:, (c % 8) * 128:(c % 8) * 128 + 128],
                                            vb[b][:, c * 128:c * 128 + 128],
                                            ident[:, :], is_transpose=True,
                                            start=(c % 8 == 0), stop=(c % 8 == 7)))
                        tmk = sch.mark("T")
                        sch.wait("V", tmk)
                        sch.op("V", lambda v, b=hb, g=g2:
                               v.tensor_copy(vT[b][:, g * 1024:(g + 1) * 1024],
                                             pvt[:, :]))
                        vtev[(h, g2)] = sch.mark("V")
                    sch.wait("T", vtev[(h, 0)])
                    sch.wait("T", vtev[(h, 1)])
                    sch.wait("T", ktok)
                    for tslot in range(2):
                        C = CTX[tslot]
                        it = 2 * h + tslot
                        sch.wait("S", attn_emk.get(it - 2))
                        for tkc in range(C):
                            if tkc >= 4:
                                sch.wait("T", expmark[(it, tkc - 4)])
                            elif it >= 1:
                                last = [c for c in range(CTX[(it - 1) % 2])
                                        if c % 4 == tkc % 4]
                                if last:
                                    sch.wait("T", expmark[(it - 1, last[-1])])
                            if tkc % 4 == 2:
                                sch.wait("T", attn_emk.get(it - 1))
                            sch.op("T", lambda t, b=hb, c=tkc, ts=tslot, hh=h:
                                   t.matmul(psc[c % 4][:, :],
                                            kb[b][:, c * 128:c * 128 + 128],
                                            q_all[:, hh * TPC + ts * 512:
                                                  hh * TPC + ts * 512 + 512],
                                            start=True, stop=True))
                            scmark[(it, tkc)] = sch.mark("T")
                            if tkc == 1 and pending_tail[0] is not None:
                                emit_tail(pending_tail[0])
                                pending_tail[0] = None
                            sch.wait("S", scmark[(it, tkc)])
                            sch.op("S", lambda s, b=tslot, c=tkc:
                                   s.activation(pT[b][:, c * 512:c * 512 + 512],
                                                psc[c % 4][:, :], AF.Exp,
                                                bias=0.0, scale=SCALE))
                            expmark[(it, tkc)] = sch.mark("S")
                            if (tslot == 0) or (tkc >= 8):
                                sch.wait("V", expmark[(it, tkc)])
                                sch.op("V", lambda v, b=tslot, c=tkc:
                                       v.tensor_mul(
                                           pT[b][:, c * 512:c * 512 + 512],
                                           pT[b][:, c * 512:c * 512 + 512],
                                           cmask[:, c * 512:c * 512 + 512]))
                                expmark[(it, tkc)] = sch.mark("V")
                            if tkc >= 2:
                                _emit_sumav(sch, it, tkc - 2, C, hb, tslot,
                                            pT, vT, prow, pav, ones, expmark,
                                            rmk, attn_emk)
                        for c2 in (C - 2, C - 1):
                            _emit_sumav(sch, it, c2, C, hb, tslot, pT, vT,
                                        prow, pav, ones, expmark, rmk,
                                        attn_emk)
                        avm = sch.mark("T")
                        sch.wait("V", avm)
                        def _recip(v):
                            with nc.allow_low_precision(reason="softmax 1/s"):
                                return v.reciprocal(rs[:, :], prow[:, :])
                        sch.op("V", _recip)
                        rmk[it] = sch.mark("V")
                        pending_tail[0] = it
                emit_tail(pending_tail[0])
                pending_tail[0] = None
            sch.barrier()
        if "D" in PHASES:
            with contextlib.ExitStack() as dc:
                def pumpd(n=1):
                    pass
                yfeat = dc.enter_context(
                    nc.sbuf_tensor("yfeat", [128, 16 * TPC], BF16))
                wo = [dc.enter_context(nc.sbuf_tensor(f"wosb{i}", [128, 4096], BF16))
                      for i in range(2)]
                ytok = dc.enter_context(nc.sbuf_tensor("ytok", [128, D], F32))
                lno = ytok
                bout = dc.enter_context(nc.sbuf_tensor("boutsb", [128, D], F32))
                xti = [dc.enter_context(nc.sbuf_tensor(f"xtiD{i}", [128, D], F32))
                       for i in range(2)]
                botok = sch.dma("Y", bout[:, :], bout_d[:, :], "c3")
                sch.wait("V", botok)
                with contextlib.ExitStack() as pc:
                    po = [pc.enter_context(
                        nc.psum_tensor(f"po{i}", [128, 512], F32))
                        for i in range(4)]
                    ev, wtok, ommk = {}, {}, {}
                    for oc in range(16):
                        if oc % 2 == 0:
                            if oc >= 4:
                                sch.wait("G", ommk[oc - 3])
                            wtok[oc // 2] = sch.dma(
                                "G", wo[(oc // 2) % 2][:, :],
                                AP(woutT, oc * 128,
                                   [[D, 128], [128 * D, 16], [1, 256]]),
                                f"wo{(oc // 2) % 2}")
                        bks = (oc % 2) * 2
                        sch.wait("T", wtok[oc // 2])
                        if oc >= 2:
                            sch.wait("T", ev[oc - 2])
                        for dcx in range(16):
                            pumpd(1)
                            for tg in range(2):
                                sch.op("T", lambda t, d_=dcx, o=oc, g=tg, bi=bks + tg:
                                       t.matmul(
                                           po[bi][:, :],
                                           wo[(o // 2) % 2][:, d_ * 256 + (o % 2) * 128:
                                                            d_ * 256 + (o % 2) * 128 + 128],
                                           attnout[:, d_ * TPC + g * 512:
                                                   d_ * TPC + g * 512 + 512],
                                           start=(d_ == 0), stop=(d_ == 15)))
                        ommk[oc] = omm = sch.mark("T")
                        sch.wait("S", omm)
                        for tg in range(2):
                            sch.op("S", lambda s, o=oc, g=tg, bi=bks + tg:
                                   s.copy(yfeat[:, o * TPC + g * 512:
                                                o * TPC + g * 512 + 512],
                                          po[bi][:, :]))
                        ev[oc] = sch.mark("S")
                    sch.barrier()
                with contextlib.ExitStack() as pc:
                    ptb = [pc.enter_context(
                        nc.psum_tensor(f"ptb{i}", [128, 1024], BF16))
                        for i in range(2)]
                    prev = {}
                    xst = {}
                    for to in range(8):
                        pumpd(4)
                        if to >= 2:
                            sch.wait("Y", xst[to - 2])
                        xtok = sch.dma("Y", xti[to % 2][:, :],
                                       x_row[_own_tile(to) * 128:
                                             _own_tile(to) * 128 + 128, :],
                                       f"dx{to % 2}")
                        tg, tl = to // 4, to % 4
                        for ocx in range(16):
                            p = ptb[ocx // 8]
                            if ocx % 8 == 0 and to >= 1:
                                sch.wait("T", prev.get((to - 1, ocx // 8)))
                            sch.op("T", lambda t, p_=p, o=ocx, g=tg, l=tl:
                                   t.matmul(p_[:, (o % 8) * 128:(o % 8) * 128 + 128],
                                            yfeat[:, o * TPC + g * 512 + l * 128:
                                                  o * TPC + g * 512 + l * 128 + 128],
                                            ident[:, :], is_transpose=True,
                                            start=(o % 8 == 0), stop=(o % 8 == 7)))
                            if ocx % 8 == 7:
                                tmk = sch.mark("T")
                                sch.wait("V", tmk)
                                if to >= 1:
                                    sch.wait("V", prev[to - 1])
                                sch.op("V", lambda v, p_=p, g=ocx // 8:
                                       v.scalar_tensor_tensor(
                                           ytok[:, g * 1024:(g + 1) * 1024],
                                           p_[:, :], 0.0,
                                           bout[:, g * 1024:(g + 1) * 1024],
                                           ALU.add, ALU.add))
                                prev[(to, ocx // 8)] = sch.mark("V")
                        sch.wait("S", prev[(to, 0)])
                        sch.wait("S", prev[(to, 1)])
                        _emit_ln(sch, col, 8 * (to % 2), ytok[:, :], lnsq,
                                 out_f32=lno)
                        sch.wait("V", xtok)
                        sch.op("V", lambda v, o=to:
                               v.tensor_add(xti[o % 2][:, :], xti[o % 2][:, :],
                                            lno[:, :]))
                        prev[to] = sch.mark("V")
                        sch.wait("Y", prev[to])
                        xst[to] = sch.dma("Y",
                                          xmid[to * 128:(to + 1) * 128, :],
                                          xti[to % 2][:, :], f"dxs{to % 2}")
                    pumpd(400)
            sch.barrier()
        qscope.close()

        if "W" in PHASES:
            with contextlib.ExitStack() as wdc:
                pdnA = [wdc.enter_context(
                    nc.psum_tensor(f"pdnA{i}", [128, 512], F32))
                    for i in range(4)]
                pdnB = [wdc.enter_context(
                    nc.psum_tensor(f"pdnB{i}", [128, 512], F32))
                    for i in range(4)]
                dgen = make_down_gen(wdc, pdnA, pdnB, depth=4)
                for _ in range(NDCH * 9 + 20):
                    next(dgen)
            sch.barrier()

        if "E" in PHASES:
            with contextlib.ExitStack() as ec:
                x2f = ec.enter_context(nc.sbuf_tensor("x2f", [128, 16 * 512], BF16))
                act = ec.enter_context(nc.sbuf_tensor("actsb", [128, 64 * 512], BF16))
                wg = [ec.enter_context(nc.sbuf_tensor(f"wgsb{i}", [128, 2048], BF16))
                      for i in range(4)]
                wu = [ec.enter_context(nc.sbuf_tensor(f"wusb{i}", [128, 2048], BF16))
                      for i in range(4)]
                wd = [ec.enter_context(nc.sbuf_tensor(f"wdsb{i}", [128, 2048], BF16))
                      for i in range(2)]
                xtie = [ec.enter_context(nc.sbuf_tensor(f"xtiE{i}", [128, D], F32))
                       for i in range(2)]
                sil = [ec.enter_context(nc.sbuf_tensor(f"silsb{i}", [128, 512], BF16))
                       for i in range(4)]
                y2f = ec.enter_context(nc.sbuf_tensor("y2f", [128, 16 * 512], BF16))
                ytoke = ec.enter_context(nc.sbuf_tensor("ytokE", [128, D], F32))
                lnoe = ytoke

                for g in range(2):
                    sch.barrier()
                    with contextlib.ExitStack() as pc:
                        ptr = [pc.enter_context(
                            nc.psum_tensor(f"ptrE{g}{i}", [128, 1024], BF16))
                            for i in range(2)]
                        pev, nmks, trmk = {}, {}, {}

                        def e_transpose(tl):
                            for hc in range(16):
                                p = ptr[hc // 8]
                                if hc % 8 == 0:
                                    sch.wait("T", nmks[tl])
                                    if tl >= 1:
                                        sch.wait("T", pev.get((tl - 1, hc // 8)))
                                sch.op("T", lambda t, p_=p, h=hc, b=tl % 2:
                                       t.matmul(p_[:, (h % 8) * 128:(h % 8) * 128 + 128],
                                                lnsqs[b][:, h * 128:h * 128 + 128],
                                                ident[:, :], is_transpose=True,
                                                start=(h % 8 == 0), stop=(h % 8 == 7)))
                                if hc % 8 == 7:
                                    tmk = sch.mark("T")
                                    trmk[tl] = tmk
                                    sch.wait("V", tmk)
                                    sch.op("V", lambda v, p_=p, t_=tl, gg=hc // 8:
                                           v.tensor_copy(
                                               AP(x2f, (gg * 8) * 512 + t_ * 128,
                                                  [[16 * 512, 128], [512, 8], [1, 128]]),
                                               p_[:, :]))
                                    pev[(tl, hc // 8)] = sch.mark("V")

                        for tl in range(4):
                            to = g * 4 + tl
                            if tl >= 2:
                                sch.wait("Y", nmks[tl - 2])
                            xtok = sch.dma("Y", xtie[tl % 2][:, :],
                                           xmid[to * 128:(to + 1) * 128, :],
                                           f"ex{tl % 2}")
                            sch.wait("V", xtok)
                            sch.wait("S", xtok)
                            if tl >= 2:
                                sch.wait("S", trmk[tl - 2])
                                sch.wait("V", trmk[tl - 2])
                            _emit_ln(sch, col, 8 * (tl % 2),
                                     xtie[tl % 2][:, :], lnsqs[tl % 2],
                                     out_bf=lnsqs[tl % 2])
                            nmks[tl] = sch.mark("V")
                            if tl >= 1:
                                e_transpose(tl - 1)
                        e_transpose(3)
                        sch.barrier()
                    with contextlib.ExitStack() as pc:
                        pg = [pc.enter_context(
                            nc.psum_tensor(f"pgE{g}{i}", [128, 512], F32))
                            for i in range(4)]
                        pu = [pc.enter_context(
                            nc.psum_tensor(f"puE{g}{i}", [128, 512], F32))
                            for i in range(4)]
                        ev, wtokg, wtoku, gmmk = {}, {}, {}, {}
                        for oc in range(64):
                            if oc >= 4:
                                sch.wait("Y", gmmk[oc - 4])
                            wtokg[oc] = sch.dma(
                                "Y", wg[oc % 4][:, :],
                                AP(wbuf_g, oc * GCH, [[NBLK, 128], [1, GCH]]),
                                f"eg{oc % 4}")
                            wtoku[oc] = sch.dma(
                                "Y", wu[oc % 4][:, :],
                                AP(wbuf_u, oc * GCH, [[NBLK, 128], [1, GCH]]),
                                f"eu{oc % 4}")
                            bkg = oc % 4
                            sch.wait("T", wtokg[oc])
                            sch.wait("T", wtoku[oc])
                            if oc >= 4:
                                sch.wait("T", ev[oc - 4])
                            for hc in range(16):
                                sch.op("T", lambda t, h=hc, o=oc, bi=bkg:
                                       t.matmul(
                                           pg[bi][:, :],
                                           wg[o % 4][:, h * 128:h * 128 + 128],
                                           x2f[:, h * 512:h * 512 + 512],
                                           start=(h == 0), stop=(h == 15)))
                                sch.op("T", lambda t, h=hc, o=oc, bi=bkg:
                                       t.matmul(
                                           pu[bi][:, :],
                                           wu[o % 4][:, h * 128:h * 128 + 128],
                                           x2f[:, h * 512:h * 512 + 512],
                                           start=(h == 0), stop=(h == 15)))
                            gmmk[oc] = gmm = sch.mark("T")
                            sch.wait("S", gmm)
                            if oc >= 4:
                                sch.wait("S", ev[oc - 4])
                            sch.op("S", lambda s, bi=bkg, o=oc:
                                   s.activation(sil[o % 4][:, :], pg[bi][:, :],
                                                AF.Silu, bias=0.0, scale=1.0))
                            smk = sch.mark("S")
                            sch.wait("V", smk)
                            sch.op("V", lambda v, o=oc, bi=bkg:
                                   v.tensor_mul(act[:, o * 512:o * 512 + 512],
                                                sil[o % 4][:, :], pu[bi][:, :]))
                            ev[oc] = sch.mark("V")
                        sch.barrier()
                    with contextlib.ExitStack() as pc:
                        pd = [pc.enter_context(
                            nc.psum_tensor(f"pdE{g}{i}", [128, 512], F32))
                            for i in range(4)]
                        ev, wtokd, dmmk = {}, {}, {}
                        for hcp in range(8):
                            bks = (hcp % 2) * 2
                            if hcp >= 2:
                                sch.wait("T", ev[hcp - 2])
                            for ic in range(64):
                                if ic % 8 == 0:
                                    icq = ic // 8
                                    prevq = (hcp, icq - 2) if icq >= 2 else \
                                            ((hcp - 1, 6 + icq) if hcp >= 1 else None)
                                    if prevq is not None:
                                        sch.wait("Y", dmmk.get(prevq))
                                    wtokd[(hcp, icq)] = sch.dma(
                                        "Y", wd[icq % 2][:, :],
                                        AP(wbuf_d, icq * 8 * D + hcp * 256,
                                           [[64 * D, 128], [D, 8], [1, 256]]),
                                        f"ed{icq % 2}")
                                    sch.wait("T", wtokd[(hcp, icq)])
                                for o2 in range(2):
                                    sch.op("T", lambda t, i_=ic, o=o2, bi=bks:
                                           t.matmul(
                                               pd[bi + o][:, :],
                                               wd[(i_ // 8) % 2][:, (i_ % 8) * 256 + o * 128:
                                                            (i_ % 8) * 256 + o * 128 + 128],
                                               act[:, i_ * 512:i_ * 512 + 512],
                                               start=(i_ == 0), stop=(i_ == 63)))
                                if ic % 8 == 7:
                                    dmmk[(hcp, ic // 8)] = sch.mark("T")
                            dmm = sch.mark("T")
                            sch.wait("S", dmm)
                            if hcp >= 2:
                                sch.wait("S", ev[hcp - 2])
                            for o2 in range(2):
                                sch.op("S", lambda s, h=hcp, o=o2, bi=bks:
                                       s.copy(y2f[:, (2 * h + o) * 512:
                                                  (2 * h + o) * 512 + 512],
                                              pd[bi + o][:, :]))
                            ev[hcp] = sch.mark("S")
                        sch.barrier()
                    with contextlib.ExitStack() as pc:
                        ptb = [pc.enter_context(
                            nc.psum_tensor(f"ptbE{g}{i}", [128, 1024], BF16))
                            for i in range(2)]
                        prev = {}
                        ost = {}
                        for tl in range(4):
                            to = g * 4 + tl
                            if tl >= 2:
                                sch.wait("Y", ost[tl - 2])
                            xtok = sch.dma("Y", xtie[tl % 2][:, :],
                                           xmid[to * 128:(to + 1) * 128, :],
                                           f"e4x{tl % 2}")
                            for ocx in range(16):
                                p = ptb[ocx // 8]
                                if ocx % 8 == 0 and tl >= 1:
                                    sch.wait("T", prev.get((tl - 1, ocx // 8)))
                                sch.op("T", lambda t, p_=p, o=ocx, l=tl:
                                       t.matmul(p_[:, (o % 8) * 128:(o % 8) * 128 + 128],
                                                y2f[:, o * 512 + l * 128:
                                                    o * 512 + l * 128 + 128],
                                                ident[:, :], is_transpose=True,
                                                start=(o % 8 == 0), stop=(o % 8 == 7)))
                                if ocx % 8 == 7:
                                    tmk = sch.mark("T")
                                    sch.wait("S", tmk)
                                    if tl >= 1:
                                        sch.wait("S", prev[tl - 1])
                                    sch.op("S", lambda s, p_=p, gg=ocx // 8:
                                           s.copy(ytoke[:, gg * 1024:(gg + 1) * 1024],
                                                  p_[:, :]))
                                    prev[(tl, ocx // 8)] = sch.mark("S")
                            sch.wait("V", prev[(tl, 0)])
                            sch.wait("V", prev[(tl, 1)])
                            _emit_ln(sch, col, 8 * (tl % 2), ytoke[:, :], lnsq,
                                     out_f32=lnoe)
                            sch.wait("V", xtok)
                            sch.op("V", lambda v, l=tl:
                                   v.tensor_add(xtie[l % 2][:, :],
                                                xtie[l % 2][:, :], lnoe[:, :]))
                            fmk = sch.mark("V")
                            prev[tl] = fmk
                            sch.wait("Y", fmk)
                            ost[tl] = sch.dma(
                                "Y", out_d[to * 128:(to + 1) * 128, :],
                                xtie[tl % 2][:, :], f"out{tl % 2}")

        sch.finish("Y")
        sch.emit(nc, ctx)

    return nc


_TPR = {}


def _perm_gu():
    if "gu" not in _TPR:
        p = np.arange(GCH).reshape(GCH // 16, 16).T.ravel()
        _TPR["gu"] = (np.arange(NBLK).reshape(-1, GCH)[:, p]).ravel()
    return _TPR["gu"]


def _perm_dn():
    if "dn" not in _TPR:
        p = np.arange(DCH).reshape(DCH // 64, 64).T.ravel()
        _TPR["dn"] = (np.arange(NBLK).reshape(-1, DCH)[:, p]).ravel()
    return _TPR["dn"]


def _tau_tiled():
    cst = _consts()
    tp = cst["taupow"]
    return np.concatenate([np.tile(tp[0:4], (4, 1)), np.tile(tp[4:8], (4, 1))])


def _tpr0():
    if 0 not in _TPR:
        _TPR[0] = np.ascontiguousarray(_tau_tiled()[:, _perm_gu()])
    return _TPR[0]


def _tpr1():
    if 1 not in _TPR:
        _TPR[1] = np.ascontiguousarray(_tau_tiled()[:, _perm_dn()])
    return _TPR[1]


def _get_program():
    global _PROG
    if _PROG is None:
        _PROG = build_program()
    return _PROG


def kernel(L0_attn_x, L0_attn_w_qkv, L0_attn_b_qkv, L0_attn_w_out,
           L0_attn_b_out, L0_ffn_gate_cp, L0_ffn_up_cp, L0_ffn_down_cp):
    cst = _consts()
    nc = _get_program()

    x = np.ascontiguousarray(np.asarray(L0_attn_x, dtype=np.float32))
    wqkvT = np.ascontiguousarray(np.asarray(L0_attn_w_qkv, np.float32).T)
    woutT = np.ascontiguousarray(np.asarray(L0_attn_w_out, np.float32).T)
    bqkv = np.ascontiguousarray(
        np.asarray(L0_attn_b_qkv, np.float32).reshape(48, 128).T)
    bout_rep = np.ascontiguousarray(
        np.broadcast_to(np.asarray(L0_attn_b_out, np.float32)[None, :],
                        (128, D)))
    cps = [np.asarray(c, np.float32) for c in
           (L0_ffn_gate_cp, L0_ffn_up_cp, L0_ffn_down_cp)]
    taps = [c[cst["tapidx"]] for c in cps]
    tap_gu = np.ascontiguousarray(
        np.concatenate([taps[0], taps[1]])[:, _perm_gu()])
    tap_dn = np.ascontiguousarray(taps[2][:, _perm_dn()])
    wgm_gu = np.ascontiguousarray(cst["wgmask"][:, _perm_gu()])
    wgm_dn = np.ascontiguousarray(cst["wgmask"][:, _perm_dn()])

    in_maps = []
    for c in range(NCORES):
        b, j = c // 2, c % 2
        gk = [j, 1 - j, 2 + j, 3 - j]
        xr = np.ascontiguousarray(x[b].reshape(4, 512, D)[gk].reshape(S, D))
        in_maps.append(dict(
            x_row=xr, wqkvT=wqkvT, bqkv=bqkv, woutT=woutT,
            bout_rep=bout_rep, tap_gu=tap_gu, tap_dn=tap_dn,
            tau_gu=_tpr0(), tau_dn=_tpr1(),
            atilde=cst["atilde"], wgmask_gu=wgm_gu, wgmask_dn=wgm_dn,
            cmask=np.ascontiguousarray(cst["cmask"][j]),
            ident=cst["ident"], ones=cst["ones"],
        ))

    res = run_bass_kernel_spmd(nc, in_maps, core_ids=list(range(NCORES)))

    out = np.empty((B, S, D), dtype=np.float32)
    for c in range(NCORES):
        b, j = c // 2, c % 2
        o = res.results[c]["out"]
        out[b, j * 512:(j + 1) * 512] = o[0:512]
        out[b, (2 + j) * 512:(3 + j) * 512] = o[512:1024]
    return out


# version 20
# speedup vs baseline: 1.1743x; 1.1743x over previous
import contextlib
import math

import numpy as np
import ml_dtypes

import concourse.bass as bass
import concourse.mybir as mybir
from concourse.bass_utils import run_bass_kernel_spmd

F32 = mybir.dt.float32
BF16 = mybir.dt.bfloat16
AF = mybir.ActivationFunctionType
ALU = mybir.AluOpType
AX = mybir.AxisListType
AP = bass.AP

B, S, D = 4, 2048, 2048
H, HD = 16, 128
INTER = 8192
NCTRL = 130156
NSAMP = INTER * D
NBLK = NSAMP // 128
GCH = 2048
NGCH = NBLK // GCH
DCH = 4096
NDCH = NBLK // DCH
TPC = 1024
SCALE = 1.0 / math.sqrt(HD)
INV_D = 1.0 / D
EPS = 1e-5
NCORES = 8
CTX = (8, 16)


_CONSTS = None


def _consts():
    global _CONSTS
    if _CONSTS is not None:
        return _CONSTS
    t = np.linspace(0.0, NCTRL - 1.0, NSAMP, dtype=np.float64)
    iarr = np.clip(np.floor(t).astype(np.int64), 0, NCTRL - 2)
    i_b = iarr[::128]
    tau = t[::128] - i_b
    s_b = (iarr.reshape(NBLK, 128) == i_b[:, None]).sum(1)
    R = (NCTRL - 1.0) / (NSAMP - 1)

    Cc = np.array([
        [0.0, -0.5, 1.0, -0.5],
        [1.0, 0.0, -2.5, 1.5],
        [0.0, 0.5, 2.0, -1.5],
        [0.0, 0.0, -0.5, 0.5]], dtype=np.float64)
    from math import comb
    qR = np.arange(128, dtype=np.float64) * R
    at = np.zeros((16, 128), dtype=np.float64)
    for k in range(4):
        for m in range(4):
            v = np.zeros_like(qR)
            for jj in range(m, 4):
                v += comb(jj, m) * Cc[k, jj] * qR ** (jj - m)
            at[4 * k + m] = v
    tp = np.zeros((8, NBLK), dtype=np.float64)
    for m in range(4):
        tp[m] = tau ** m
        tp[4 + m] = (tau - 1.0) ** m
    mask = (np.arange(128)[:, None] < s_b[None, :])
    tapidx = np.stack([np.clip(i_b + r - 1, 0, NCTRL - 1) for r in range(5)])

    cmask = {}
    for j in range(2):
        gk = [j, 1 - j, 2 + j, 3 - j]
        cm = np.zeros((16, 128, 512), dtype=np.float64)
        for cr in range(16):
            tslot = 0 if cr < 8 else 1
            gt = 2 * tslot + j
            tk = gk[cr // 4] * 512 + (cr % 4) * 128 + np.arange(128)[:, None]
            tq = gt * 512 + np.arange(512)[None, :]
            cm[cr] = (tk <= tq)
        cmask[j] = cm.astype(ml_dtypes.bfloat16)

    r0_tap = np.zeros(128, dtype=np.int64)
    r0_wt = np.zeros(128, dtype=np.int64)
    r0_tau = np.zeros(128, dtype=np.int64)
    for p in range(128):
        si, r32 = p // 32, p % 32
        k, m = (r32 % 16) // 4, r32 % 4
        r0_wt[p] = si // 2
        r0_tap[p] = k + (si % 2)
        r0_tau[p] = (si % 2) * 4 + m
    r1_tap = np.zeros(64, dtype=np.int64)
    r1_tau = np.zeros(64, dtype=np.int64)
    for p in range(64):
        si, r32 = p // 32, p % 32
        k, m = (r32 % 16) // 4, r32 % 4
        r1_tap[p] = k + si
        r1_tau[p] = si * 4 + m
    _CONSTS = dict(
        tapidx=tapidx, s_b=s_b,
        r0_tap=r0_tap, r0_wt=r0_wt, r0_tau=r0_tau,
        r1_tap=r1_tap, r1_tau=r1_tau,
        atilde=at.astype(np.float32),
        taupow=tp.astype(np.float32),
        wgmask=mask.astype(np.uint8),
        ident=np.eye(128, dtype=ml_dtypes.bfloat16),
        ones=np.ones((128, 128), dtype=ml_dtypes.bfloat16),
        cmask=cmask,
    )
    return _CONSTS


class Sched:
    ENGINES = ("T", "V", "S", "G", "Y")

    def __init__(self):
        self.recs = {e: [] for e in self.ENGINES}
        self.cnt = {e.lower(): 0 for e in self.ENGINES}
        self.pend = {e: [] for e in self.ENGINES}
        self.streams = {}
        self.issuer = {}
        self.open_streams = {e: set() for e in self.ENGINES}

    CHAINED = ("V", "S", "G")

    def op(self, e, fn):
        w = list(self.pend[e])
        self.pend[e] = []
        incs = []
        if e in self.CHAINED:
            sem = e.lower()
            if self.cnt[sem]:
                w.append((sem, self.cnt[sem]))
            self.cnt[sem] += 1
            incs.append((sem, 1, self.cnt[sem]))
        self.recs[e].append([fn, tuple(w), incs, False])

    def mark(self, e):
        rec = self.recs[e][-1]
        if rec[3]:
            sem, _n, tgt = rec[2][0]
            return (sem, tgt)
        if rec[2]:
            sem, _n, tgt = rec[2][0]
            return (sem, tgt)
        sem = e.lower()
        self.cnt[sem] += 1
        rec[2].append((sem, 1, self.cnt[sem]))
        return (sem, self.cnt[sem])

    def dma(self, e, out, in_, stream):
        sname = "q_" + stream
        assert self.issuer.setdefault(sname, e) == e, (sname, e)
        self.streams[sname] = self.streams.get(sname, 0) + 16
        tgt = self.streams[sname]
        self.open_streams[e].add(sname)
        w = tuple(self.pend[e])
        self.pend[e] = []
        self.recs[e].append(
            [lambda eng, o=out, i=in_: eng.dma_start(out=o, in_=i), w,
             [(sname, 16, tgt)], True])
        return (sname, tgt)

    def wait(self, e, tok):
        if tok is not None:
            self.pend[e].append(tok)

    def barrier(self):
        toks = []
        for e in self.ENGINES:
            if not self.recs[e] and not self.open_streams[e]:
                continue
            if self.open_streams[e]:
                for sname in sorted(self.open_streams[e]):
                    self.pend[e].append((sname, self.streams[sname]))
                self.open_streams[e] = set()
                self.op(e, lambda eng: eng.nop())
            toks.append(self.mark(e))
        for e in self.ENGINES:
            for t in toks:
                self.pend[e].append(t)

    def finish(self, e):
        for sname in sorted(self.open_streams[e]):
            self.pend[e].append((sname, self.streams[sname]))
        self.op(e, lambda eng: eng.nop())

    def emit(self, nc, ctx):
        sems = {}
        for sn in self.cnt:
            sems[sn] = ctx.enter_context(nc.semaphore("sem_" + sn))
        for sn in self.streams:
            sems[sn] = ctx.enter_context(nc.semaphore(sn))

        def replay(e):
            def f(eng):
                for fn, waits, incs, _isdma in self.recs[e]:
                    agg = {}
                    for (sn, v) in waits:
                        agg[sn] = max(agg.get(sn, 0), v)
                    for sn, v in agg.items():
                        eng.wait_ge(sems[sn], v)
                    inst = fn(eng)
                    for (sn, n, _tgt) in incs:
                        inst.then_inc(sems[sn], n)
            return f
        with nc.Block() as blk:
            blk.tensor(replay("T"))
            blk.vector(replay("V"))
            blk.scalar(replay("S"))
            blk.gpsimd(replay("G"))
            blk.sync(replay("Y"))


def _emit_sumav(sch, it, c, C, hb, tslot, pT, vT, prow, pav, ones, expmark,
                rmk, attn_emk):
    sch.wait("T", expmark[(it, c)])
    if c == 0:
        sch.wait("T", rmk.get(it - 1))
        sch.wait("T", attn_emk.get(it - 2))
    sch.op("T", lambda t, b=tslot, c_=c, C_=C:
           t.matmul(prow[:, :], ones[:, 0:1],
                    pT[b][:, c_ * 512:c_ * 512 + 512],
                    start=(c_ == 0), stop=(c_ == C_ - 1)))
    sch.op("T", lambda t, b=hb, bb=tslot, c_=c, C_=C, pa=pav[it % 2]:
           t.matmul(pa[:, :], vT[b][:, c_ * 128:c_ * 128 + 128],
                    pT[bb][:, c_ * 512:c_ * 512 + 512],
                    start=(c_ == 0), stop=(c_ == C_ - 1)))


def _own_tile(to):
    return to if to < 4 else to + 4


def _emit_ln(sch, col, c0, xap, sqscratch, out_bf=None, out_f32=None):
    sch.op("V", lambda v: v.reduce_sum(col[:, c0:c0 + 1], xap, axis=AX.X))
    sch.op("S", lambda s: s.activation(sqscratch[:, :], xap, AF.Square,
                                       bias=0.0, scale=1.0,
                                       accum_out=col[:, c0 + 1:c0 + 2]))
    sqm = sch.mark("S")
    sch.op("V", lambda v: v.tensor_scalar(col[:, c0 + 2:c0 + 3],
                                          col[:, c0:c0 + 1], INV_D, None,
                                          ALU.mult))
    sch.op("V", lambda v: v.tensor_mul(col[:, c0 + 3:c0 + 4],
                                       col[:, c0 + 2:c0 + 3],
                                       col[:, c0 + 2:c0 + 3]))
    sch.wait("V", sqm)
    sch.op("V", lambda v: v.tensor_scalar(col[:, c0 + 4:c0 + 5],
                                          col[:, c0 + 1:c0 + 2], INV_D,
                                          col[:, c0 + 3:c0 + 4],
                                          ALU.mult, ALU.subtract))
    vmk = sch.mark("V")
    sch.wait("S", vmk)
    sch.op("S", lambda s: s.activation(col[:, c0 + 5:c0 + 6],
                                       col[:, c0 + 4:c0 + 5], AF.Sqrt,
                                       bias=EPS, scale=1.0))
    smk = sch.mark("S")
    sch.wait("V", smk)
    sch.op("V", lambda v: v.reciprocal(col[:, c0 + 6:c0 + 7],
                                       col[:, c0 + 5:c0 + 6]))
    sch.op("V", lambda v: v.tensor_mul(col[:, c0 + 7:c0 + 8],
                                       col[:, c0 + 2:c0 + 3],
                                       col[:, c0 + 6:c0 + 7]))
    if out_bf is not None:
        sch.op("V", lambda v: v.tensor_scalar(out_bf[:, :], xap,
                                              col[:, c0 + 6:c0 + 7],
                                              col[:, c0 + 7:c0 + 8],
                                              ALU.mult, ALU.subtract))
    if out_f32 is not None:
        sch.op("V", lambda v: v.tensor_scalar(out_f32[:, :], xap,
                                              col[:, c0 + 6:c0 + 7],
                                              col[:, c0 + 7:c0 + 8],
                                              ALU.mult, ALU.subtract))


_PROG = None
PHASES = ("W", "B", "C", "D", "E")


def build_program():
    nc = bass.Bass("TRN2", target_bir_lowering=False)
    sch = Sched()

    x_row = nc.dram_tensor("x_row", [S, D], F32, kind="ExternalInput")
    wqkvT = nc.dram_tensor("wqkvT", [D, 3 * D], F32, kind="ExternalInput")
    bqkv_d = nc.dram_tensor("bqkv", [128, 48], F32, kind="ExternalInput")
    woutT = nc.dram_tensor("woutT", [D, D], F32, kind="ExternalInput")
    bout_d = nc.dram_tensor("bout_rep", [128, D], F32, kind="ExternalInput")
    tap_gu_d = nc.dram_tensor("tap_gu", [10, NBLK], F32, kind="ExternalInput")
    tau_gu_d = nc.dram_tensor("tau_gu", [32, NBLK], F32, kind="ExternalInput")
    tap_dn_d = nc.dram_tensor("tap_dn", [5, NBLK], F32, kind="ExternalInput")
    tau_dn_d = nc.dram_tensor("tau_dn", [32, NBLK], F32, kind="ExternalInput")
    at_d = nc.dram_tensor("atilde", [16, 128], F32, kind="ExternalInput")
    wgm_gu_d = nc.dram_tensor("wgmask_gu", [128, NBLK], mybir.dt.uint8,
                              kind="ExternalInput")
    wgm_dn_d = nc.dram_tensor("wgmask_dn", [128, NBLK], mybir.dt.uint8,
                              kind="ExternalInput")
    cm_d = nc.dram_tensor("cmask", [16, 128, 512], BF16, kind="ExternalInput")
    id_d = nc.dram_tensor("ident", [128, 128], BF16, kind="ExternalInput")
    on_d = nc.dram_tensor("ones", [128, 128], BF16, kind="ExternalInput")
    out_d = nc.dram_tensor("out", [TPC, D], F32, kind="ExternalOutput")

    wbuf_g = nc.dram_tensor("wbuf_g", [128, NBLK], BF16)
    wbuf_u = nc.dram_tensor("wbuf_u", [128, NBLK], BF16)
    wbuf_d = nc.dram_tensor("wbuf_d", [128, 64, D], BF16)
    kvbuf = nc.dram_tensor("kvbuf", [2 * H, 128, S], BF16)
    xmid = nc.dram_tensor("xmid", [TPC, D], F32)

    ctx = contextlib.ExitStack()
    with ctx:
        csb = ctx.enter_context(nc.sbuf_tensor("constsb", [128, 4], F32))
        for ci, cv in enumerate((0.0, EPS, 1.0, SCALE)):
            nc.const_aps.register(F32, float(cv), csb[:, ci:ci + 1])
            sch.op("V", lambda v, i=ci, val=float(cv):
                   v.memset(csb[:, i:i + 1], val))
        ident = ctx.enter_context(nc.sbuf_tensor("identsb", [128, 128], BF16))
        ones = ctx.enter_context(nc.sbuf_tensor("onessb", [128, 128], BF16))
        at_sb = ctx.enter_context(nc.sbuf_tensor("atsb", [128, 128], BF16))
        bqkv = ctx.enter_context(nc.sbuf_tensor("bqkvsb", [128, 48], F32))
        col = ctx.enter_context(nc.sbuf_tensor("colsb", [128, 16], F32))
        lnsq = ctx.enter_context(nc.sbuf_tensor("lnsqsb", [128, D], BF16))
        lnsq2 = ctx.enter_context(nc.sbuf_tensor("lnsqsb2", [128, D], BF16))
        lnsqs = [lnsq, lnsq2]

        sch.dma("Y", ident[:, :], id_d[:, :], "c0")
        sch.dma("Y", ones[:, :], on_d[:, :], "c0")
        sch.dma("Y", bqkv[:, :], bqkv_d[:, :], "c0")
        for u in (0, 32, 64, 96):
            sch.dma("G", at_sb[u:u + 16, :], at_d[:, :], "c1")
        sch.barrier()

        def make_gu_gen(wc, pA, pB, depth):
            CT = [wc.enter_context(nc.sbuf_tensor(f"gct{i}", [42, GCH], BF16))
                  for i in range(3)]
            U = [wc.enter_context(nc.sbuf_tensor(f"Ug{i}", [128, GCH], BF16))
                 for i in range(2)]
            IN0 = [wc.enter_context(nc.sbuf_tensor(f"gi0_{i}", [128, GCH], BF16))
                   for i in range(2)]
            IN1 = [wc.enter_context(nc.sbuf_tensor(f"gi1_{i}", [128, GCH], BF16))
                   for i in range(2)]
            MSK = [wc.enter_context(
                nc.sbuf_tensor(f"gm_{i}", [128, GCH], mybir.dt.uint8))
                for i in range(4)]
            WO = [wc.enter_context(nc.sbuf_tensor(f"gwo_{i}", [128, GCH], BF16))
                  for i in range(4)]
            streams = [(wbuf_g, 0, 32), (wbuf_u, 64, 96)]
            for t in IN0 + IN1:
                sch.op("V", lambda v, t_=t: v.memset(t_[:, :], 0.0))
            msmk_gu = sch.mark("V")

            def ldtoks(c):
                sch.dma("G", CT[c % 3][0:10, :],
                        AP(tap_gu_d, c * GCH, [[NBLK, 10], [1, GCH]]),
                        f"wg{c % 3}")
                return sch.dma("G", CT[c % 3][10:42, :],
                               AP(tau_gu_d, c * GCH, [[NBLK, 32], [1, GCH]]),
                               f"wg{c % 3}")

            def gen():
                umark, evmark, stok, mtoks = {}, {}, {}, {}
                ltok, etok = {}, {}

                def expand(c):
                    sch.wait("S", ltok[c])
                    if c >= 2:
                        sch.wait("S", umark[c - 2])
                    else:
                        sch.wait("S", msmk_gu)
                    for si in range(4):
                        wt, dlt = si // 2, si % 2
                        sch.dma("S", AP(IN0[c % 2], (si * 32) * GCH,
                                        [[GCH, 16], [1, GCH]]),
                                AP(CT[c % 3], (wt * 5 + dlt) * GCH,
                                   [[GCH, 4], [0, 4], [1, GCH]]),
                                f"we{c % 2}")
                        etok[c] = sch.dma(
                            "S", AP(IN1[c % 2], (si * 32) * GCH,
                                    [[GCH, 16], [1, GCH]]),
                            AP(CT[c % 3], (10 + dlt * 16) * GCH,
                               [[GCH, 16], [1, GCH]]),
                            f"we{c % 2}")

                def umult(c):
                    sch.wait("V", etok[c])
                    if c >= 2:
                        sch.wait("V", evmark[(c - 2) * 8 + 7])
                    sch.op("V", lambda v, b=c % 2: v.tensor_mul(
                        U[b][:, :], IN0[b][:, :], IN1[b][:, :]))
                    umark[c] = sch.mark("V")

                for c in range(NGCH):
                    yield
                    buf = c % 2
                    if c == 0:
                        for cpre in (0, 1):
                            mtoks[cpre] = sch.dma(
                                "S", MSK[cpre % 4][:, :],
                                wgm_gu_d[:, cpre * GCH:(cpre + 1) * GCH],
                                f"gm{cpre % 4}")
                        ltok[0] = ldtoks(0)
                        ltok[1] = ldtoks(1)
                        expand(0)
                        umult(0)
                    if c + 2 < NGCH:
                        if c >= 1:
                            sch.wait("G", etok[c - 1])
                        ltok[c + 2] = ldtoks(c + 2)
                    sch.wait("T", umark[c])
                    for w in range(2):
                        wb, uA, uB = streams[w]
                        cw = c * 2 + w
                        first_ev = True
                        for q4 in range(4):
                            yield
                            i2 = c * 8 + w * 4 + q4
                            bk = i2 % depth
                            if i2 >= depth:
                                sch.wait("T", evmark[i2 - depth])
                            for u, po in ((uA, pA[bk]), (uB, pB[bk])):
                                sch.op("T", lambda t, u_=u, b=buf, pp=po,
                                       cc=q4 * 512:
                                       t.matmul(pp[:, :],
                                                at_sb[u_:u_ + 16, :],
                                                U[b][u_:u_ + 16, cc:cc + 512],
                                                start=True, stop=True,
                                                tile_position=(u_, 0)))
                            mm = sch.mark("T")
                            sch.wait("S", mm)
                            if first_ev:
                                if cw >= 4:
                                    sch.wait("S", stok[cw - 4])
                                if w == 0:
                                    sch.wait("V", mtoks[c])
                                first_ev = False
                            wo_ap = WO[cw % 4][:, q4 * 512:(q4 + 1) * 512]
                            mk_ap = MSK[c % 4][:, q4 * 512:(q4 + 1) * 512]
                            sch.op("S", lambda s, o_=wo_ap, pp=pB[bk]:
                                   s.copy(o_, pp[:, :]))
                            smk = sch.mark("S")
                            sch.wait("V", smk)
                            sch.op("V", lambda v, o_=wo_ap, m_=mk_ap,
                                   pp=pA[bk]:
                                   v.copy_predicated(o_, m_, pp[:, :]))
                            evmark[i2] = sch.mark("V")
                        sch.wait("Y", evmark[c * 8 + w * 4 + 3])
                        stok[cw] = sch.dma(
                            "Y", wb[:, c * GCH:(c + 1) * GCH],
                            WO[cw % 4][:, :],
                            f"ws{cw % 4}")
                        if w == 0 and c + 1 < NGCH:
                            expand(c + 1)
                    if c + 2 < NGCH:
                        if c >= 2:
                            sch.wait("S", evmark[(c - 2) * 8 + 7])
                        mtoks[c + 2] = sch.dma(
                            "S", MSK[(c + 2) % 4][:, :],
                            wgm_gu_d[:, (c + 2) * GCH:(c + 3) * GCH],
                            f"gm{(c + 2) % 4}")
                    if c + 1 < NGCH:
                        umult(c + 1)
                while True:
                    yield
            return gen()

        def make_down_gen(wc, pA, pB, depth):
            CT = [wc.enter_context(nc.sbuf_tensor(f"dct{i}", [37, DCH], BF16))
                  for i in range(3)]
            Ud = [wc.enter_context(nc.sbuf_tensor(f"Ud{i}", [64, DCH], BF16))
                  for i in range(2)]
            IN0 = [wc.enter_context(nc.sbuf_tensor(f"di0_{i}", [64, DCH], BF16))
                   for i in range(2)]
            IN1 = [wc.enter_context(nc.sbuf_tensor(f"di1_{i}", [64, DCH], BF16))
                   for i in range(2)]
            MSK = [wc.enter_context(
                nc.sbuf_tensor(f"dm_{i}", [128, DCH], mybir.dt.uint8))
                for i in range(4)]
            WO = [wc.enter_context(nc.sbuf_tensor(f"dwo_{i}", [128, DCH], BF16))
                  for i in range(4)]
            NCG = DCH // 512
            for t in IN0 + IN1:
                sch.op("V", lambda v, t_=t: v.memset(t_[:, :], 0.0))
            msmk_dn = sch.mark("V")

            def ldtoks(c):
                sch.dma("G", CT[c % 3][0:5, :],
                        AP(tap_dn_d, c * DCH, [[NBLK, 5], [1, DCH]]),
                        f"wd{c % 3}")
                return sch.dma("G", CT[c % 3][5:37, :],
                               AP(tau_dn_d, c * DCH, [[NBLK, 32], [1, DCH]]),
                               f"wd{c % 3}")

            def gen():
                umark, evmark, stok, mtoks = {}, {}, {}, {}
                ltok, etok = {}, {}

                def expand(c):
                    sch.wait("S", ltok[c])
                    if c >= 2:
                        sch.wait("S", umark[c - 2])
                    else:
                        sch.wait("S", msmk_dn)
                    for si in range(2):
                        sch.dma("S", AP(IN0[c % 2], (si * 32) * DCH,
                                        [[DCH, 16], [1, DCH]]),
                                AP(CT[c % 3], si * DCH,
                                   [[DCH, 4], [0, 4], [1, DCH]]),
                                f"de{c % 2}")
                        etok[c] = sch.dma(
                            "S", AP(IN1[c % 2], (si * 32) * DCH,
                                    [[DCH, 16], [1, DCH]]),
                            AP(CT[c % 3], (5 + si * 16) * DCH,
                               [[DCH, 16], [1, DCH]]),
                            f"de{c % 2}")

                def umult(c):
                    sch.wait("V", etok[c])
                    if c >= 2:
                        sch.wait("V", evmark[(c - 2) * NCG + NCG - 1])
                    sch.op("V", lambda v, b=c % 2: v.tensor_mul(
                        Ud[b][:, :], IN0[b][:, :], IN1[b][:, :]))
                    umark[c] = sch.mark("V")

                for c in range(NDCH):
                    yield
                    buf = c % 2
                    if c == 0:
                        for cpre in (0, 1):
                            mtoks[cpre] = sch.dma(
                                "S", MSK[cpre % 4][:, :],
                                wgm_dn_d[:, cpre * DCH:(cpre + 1) * DCH],
                                f"dm{cpre % 4}")
                        ltok[0] = ldtoks(0)
                        ltok[1] = ldtoks(1)
                        expand(0)
                        umult(0)
                    if c + 2 < NDCH:
                        if c >= 1:
                            sch.wait("G", etok[c - 1])
                        ltok[c + 2] = ldtoks(c + 2)
                    sch.wait("T", umark[c])
                    first_ev = True
                    for cg in range(NCG):
                        yield
                        i2 = c * NCG + cg
                        bk = i2 % depth
                        if i2 >= depth:
                            sch.wait("T", evmark[i2 - depth])
                        for u, po in ((0, pA[bk]), (32, pB[bk])):
                            sch.op("T", lambda t, u_=u, b=buf, pp=po,
                                   cc=cg * 512:
                                   t.matmul(pp[:, :], at_sb[u_:u_ + 16, :],
                                            Ud[b][u_:u_ + 16, cc:cc + 512],
                                            start=True, stop=True,
                                            tile_position=(u_, 0)))
                        mm = sch.mark("T")
                        sch.wait("S", mm)
                        if first_ev:
                            if c >= 4:
                                sch.wait("S", stok[c - 4])
                            sch.wait("V", mtoks[c])
                            first_ev = False
                        wo_ap = WO[c % 4][:, cg * 512:(cg + 1) * 512]
                        mk_ap = MSK[c % 4][:, cg * 512:(cg + 1) * 512]
                        sch.op("S", lambda s, o_=wo_ap, pp=pB[bk]:
                               s.copy(o_, pp[:, :]))
                        smk = sch.mark("S")
                        sch.wait("V", smk)
                        sch.op("V", lambda v, o_=wo_ap, m_=mk_ap, pp=pA[bk]:
                               v.copy_predicated(o_, m_, pp[:, :]))
                        evmark[i2] = sch.mark("V")
                        if cg == NCG // 2 and c + 1 < NDCH:
                            expand(c + 1)
                    od = DCH // 64
                    sch.wait("Y", evmark[c * NCG + NCG - 1])
                    stok[c] = sch.dma(
                        "Y",
                        AP(wbuf_d, c * od,
                           [[64 * D, 128], [D, 64], [1, od]]),
                        AP(WO[c % 4], 0, [[DCH, 128], [od, 64], [1, od]]),
                        f"dws{c % 4}")
                    if c + 2 < NDCH:
                        if c >= 2:
                            sch.wait("S", evmark[(c - 2) * NCG + NCG - 1])
                        mtoks[c + 2] = sch.dma(
                            "S", MSK[(c + 2) % 4][:, :],
                            wgm_dn_d[:, (c + 2) * DCH:(c + 3) * DCH],
                            f"dm{(c + 2) % 4}")
                    if c + 1 < NDCH:
                        umult(c + 1)
                while True:
                    yield
            return gen()

        qscope = ctx.enter_context(contextlib.ExitStack())
        q_all = qscope.enter_context(
            nc.sbuf_tensor("qall", [128, H * TPC], BF16))
        if "W" in PHASES:
            with contextlib.ExitStack() as wsc:
                pguA = [wsc.enter_context(
                    nc.psum_tensor(f"pguA{i}", [128, 512], F32))
                    for i in range(4)]
                pguB = [wsc.enter_context(
                    nc.psum_tensor(f"pguB{i}", [128, 512], F32))
                    for i in range(4)]
                gugen = make_gu_gen(wsc, pguA, pguB, depth=4)
                for _ in range(NGCH * 9 + 20):
                    next(gugen)
            sch.barrier()

        if "B" in PHASES:
            with contextlib.ExitStack() as bc:

                x2T = bc.enter_context(nc.sbuf_tensor("x2T", [128, 16 * S], BF16))
                xt = [bc.enter_context(nc.sbuf_tensor(f"xt{i}", [128, D], F32))
                      for i in range(2)]
                wq = [bc.enter_context(nc.sbuf_tensor(f"wq{i}", [128, 4096], BF16))
                      for i in range(2)]
                kst = [bc.enter_context(nc.sbuf_tensor(f"kst{i}", [128, 512], BF16))
                       for i in range(4)]
                with contextlib.ExitStack() as tc:
                    ptr = [tc.enter_context(
                        nc.psum_tensor(f"ptr{i}", [128, 1024], BF16))
                        for i in range(2)]
                    ln_ev, nmks, trmk = {}, {}, {}

                    def b_transpose(ti):
                        for hc in range(16):
                            p = ptr[hc // 8]
                            if hc % 8 == 0:
                                sch.wait("T", nmks[ti])
                                if ti >= 1:
                                    sch.wait("T", ln_ev.get((ti - 1, hc // 8)))
                            sch.op("T", lambda t, p_=p, h=hc, b=ti % 2:
                                   t.matmul(p_[:, (h % 8) * 128:(h % 8) * 128 + 128],
                                            lnsqs[b][:, h * 128:h * 128 + 128],
                                            ident[:, :], is_transpose=True,
                                            start=(h % 8 == 0), stop=(h % 8 == 7)))
                            if hc % 8 == 7:
                                tmk = sch.mark("T")
                                trmk[ti] = tmk
                                sch.wait("V", tmk)
                                sch.op("V", lambda v, p_=p, t_=ti, g=hc // 8:
                                       v.tensor_copy(
                                           AP(x2T, (g * 8) * S + t_ * 128,
                                              [[16 * S, 128], [S, 8], [1, 128]]),
                                           p_[:, :]))
                                ln_ev[(ti, hc // 8)] = sch.mark("V")

                    for ti in range(16):
                        bufp = ti % 2
                        if ti >= 2:
                            sch.wait("Y", nmks[ti - 2])
                        xtok = sch.dma("Y", xt[bufp][:, :],
                                       x_row[ti * 128:(ti + 1) * 128, :],
                                       f"bx{ti % 2}")
                        sch.wait("V", xtok)
                        sch.wait("S", xtok)
                        if ti >= 2:
                            sch.wait("S", trmk[ti - 2])
                            sch.wait("V", trmk[ti - 2])
                        _emit_ln(sch, col, 8 * bufp, xt[bufp][:, :],
                                 lnsqs[bufp], out_bf=lnsqs[bufp])
                        nmks[ti] = sch.mark("V")
                        if ti >= 1:
                            b_transpose(ti - 1)
                    b_transpose(15)
                    sch.barrier()
                with contextlib.ExitStack() as qc:
                    pq = [qc.enter_context(
                        nc.psum_tensor(f"pq{i}", [128, 512], F32))
                        for i in range(4)]
                    evtix, wtok, kdma, qmmk = {}, {}, {}, {}
                    for oc in range(48):
                        if oc % 2 == 0:
                            if oc >= 4:
                                sch.wait("G", qmmk[oc - 3])
                            wtok[oc // 2] = sch.dma(
                                "G", wq[(oc // 2) % 2][:, :],
                                AP(wqkvT, oc * 128,
                                   [[3 * D, 128], [128 * 3 * D, 16], [1, 256]]),
                                f"wq{(oc // 2) % 2}")
                        tgs = [0, 2] if oc < 16 else [0, 1, 2, 3]
                        kind, hh = oc // 16, oc % 16
                        sch.wait("T", wtok[oc // 2])
                        for tix, tg in enumerate(tgs):
                            sch.wait("T", evtix.get((oc - 1, tix)))
                            for hc in range(16):
                                sch.op("T", lambda t, h=hc, o=oc, g=tg, bi=tix:
                                       t.matmul(
                                           pq[bi][:, :],
                                           wq[(o // 2) % 2][:, h * 256 + (o % 2) * 128:
                                                            h * 256 + (o % 2) * 128 + 128],
                                           x2T[:, h * S + g * 512:h * S + g * 512 + 512],
                                           start=(h == 0), stop=(h == 15)))
                            qmmk[oc] = qmm = sch.mark("T")
                            sch.wait("V", qmm)
                            if kind == 0:
                                dst = q_all[:, hh * TPC + (tg // 2) * 512:
                                            hh * TPC + (tg // 2) * 512 + 512]
                                sch.op("V", lambda v, d=dst, bi=tix, o=oc:
                                       v.tensor_scalar(
                                           d, pq[bi][:, :],
                                           bqkv[:, o:o + 1], None, ALU.add))
                            else:
                                if kdma.get(tix) is not None:
                                    sch.wait("V", kdma[tix])
                                sch.op("V", lambda v, d=kst[tix],
                                       bi=tix, o=oc:
                                       v.tensor_scalar(
                                           d[:, :], pq[bi][:, :],
                                           bqkv[:, o:o + 1], None, ALU.add))
                                vmk = sch.mark("V")
                                sch.wait("Y", vmk)
                                kdma[tix] = sch.dma(
                                    "Y",
                                    AP(kvbuf, ((kind - 1) * 16 + hh) * 128 * S
                                       + tg * 512, [[S, 128], [1, 512]]),
                                    kst[tix][:, :], f"kv{tix}")
                            evtix[(oc, tix)] = sch.mark("V")
                    if gugen is not None:
                        for _ in range(24):
                            next(gugen)
            sch.barrier()

        attnout = qscope.enter_context(
            nc.sbuf_tensor("attnout", [128, H * TPC], BF16))
        if "C" in PHASES:
            with contextlib.ExitStack() as cc:
                cmask = cc.enter_context(
                    nc.sbuf_tensor("cmasksb", [128, 16 * 512], BF16))
                cmtok = sch.dma(
                    "Y", AP(cmask, 0, [[16 * 512, 128], [512, 16], [1, 512]]),
                    AP(cm_d, 0, [[512, 128], [128 * 512, 16], [1, 512]]), "c2")
                sch.wait("V", cmtok)
                kb = [cc.enter_context(nc.sbuf_tensor(f"kb{i}", [128, S], BF16))
                      for i in range(2)]
                vb = [cc.enter_context(nc.sbuf_tensor(f"vb{i}", [128, S], BF16))
                      for i in range(2)]
                vT = [cc.enter_context(nc.sbuf_tensor(f"vT{i}", [128, S], BF16))
                      for i in range(2)]
                pT = [cc.enter_context(nc.sbuf_tensor(f"pT{i}", [128, 16 * 512], BF16))
                      for i in range(2)]
                rs = cc.enter_context(nc.sbuf_tensor("rs", [1, 512], BF16))
                bc_sb = cc.enter_context(nc.sbuf_tensor("bcsb", [128, 512], BF16))
                psc = [cc.enter_context(nc.psum_tensor(f"psc{i}", [128, 512], F32))
                       for i in range(4)]
                pvt = cc.enter_context(nc.psum_tensor("pvt", [128, 1024], BF16))
                pav = [cc.enter_context(nc.psum_tensor(f"pav{i}", [128, 512], F32))
                       for i in range(2)]
                prow = cc.enter_context(nc.psum_tensor("prow", [1, 512], F32))

                vtev = {}
                expmark = {}
                scmark = {}
                attn_emk = {}
                rmk = {}
                pending_tail = [None]

                def emit_tail(it_p):
                    h_p, ts_p = it_p // 2, it_p % 2
                    sch.wait("T", rmk[it_p])
                    if it_p >= 1:
                        sch.wait("T", attn_emk.get(it_p - 1))
                    sch.op("T", lambda t: t.matmul(
                        psc[2][:, :], ones[0:1, 0:128], rs[:, :],
                        start=True, stop=True))
                    bmk = sch.mark("T")
                    sch.wait("S", bmk)
                    sch.op("S", lambda s: s.copy(bc_sb[:, :], psc[2][:, :]))
                    csm = sch.mark("S")
                    sch.wait("V", csm)
                    sch.op("V", lambda v, ts=ts_p, hh=h_p, pa=pav[it_p % 2]:
                           v.tensor_mul(
                               attnout[:, hh * TPC + ts * 512:
                                       hh * TPC + ts * 512 + 512],
                               pa[:, :], bc_sb[:, :]))
                    attn_emk[it_p] = sch.mark("V")

                for h in range(H):
                    hb = h % 2
                    if h >= 2:
                        sch.wait("Y", attn_emk[2 * (h - 2) + 1])
                    ktok = sch.dma("Y", kb[hb][:, :],
                                   AP(kvbuf, h * 128 * S, [[S, 128], [1, S]]),
                                   f"ck{h % 2}")
                    vtok = sch.dma("Y", vb[hb][:, :],
                                   AP(kvbuf, (16 + h) * 128 * S, [[S, 128], [1, S]]),
                                   f"cv{h % 2}")
                    sch.wait("T", vtok)
                    for g2 in range(2):
                        if (h, g2) != (0, 0):
                            prevk = (h, g2 - 1) if g2 else (h - 1, 1)
                            sch.wait("T", vtev[prevk])
                        for tkc in range(8 * g2, 8 * g2 + 8):
                            sch.op("T", lambda t, c=tkc, b=hb:
                                   t.matmul(pvt[:, (c % 8) * 128:(c % 8) * 128 + 128],
                                            vb[b][:, c * 128:c * 128 + 128],
                                            ident[:, :], is_transpose=True,
                                            start=(c % 8 == 0), stop=(c % 8 == 7)))
                        tmk = sch.mark("T")
                        sch.wait("V", tmk)
                        sch.op("V", lambda v, b=hb, g=g2:
                               v.tensor_copy(vT[b][:, g * 1024:(g + 1) * 1024],
                                             pvt[:, :]))
                        vtev[(h, g2)] = sch.mark("V")
                    sch.wait("T", vtev[(h, 0)])
                    sch.wait("T", vtev[(h, 1)])
                    sch.wait("T", ktok)
                    for tslot in range(2):
                        C = CTX[tslot]
                        it = 2 * h + tslot
                        sch.wait("S", attn_emk.get(it - 2))
                        for tkc in range(C):
                            if tkc >= 4:
                                sch.wait("T", expmark[(it, tkc - 4)])
                            elif it >= 1:
                                last = [c for c in range(CTX[(it - 1) % 2])
                                        if c % 4 == tkc % 4]
                                if last:
                                    sch.wait("T", expmark[(it - 1, last[-1])])
                            if tkc % 4 == 2:
                                sch.wait("T", attn_emk.get(it - 1))
                            sch.op("T", lambda t, b=hb, c=tkc, ts=tslot, hh=h:
                                   t.matmul(psc[c % 4][:, :],
                                            kb[b][:, c * 128:c * 128 + 128],
                                            q_all[:, hh * TPC + ts * 512:
                                                  hh * TPC + ts * 512 + 512],
                                            start=True, stop=True))
                            scmark[(it, tkc)] = sch.mark("T")
                            if tkc == 1 and pending_tail[0] is not None:
                                emit_tail(pending_tail[0])
                                pending_tail[0] = None
                            sch.wait("S", scmark[(it, tkc)])
                            sch.op("S", lambda s, b=tslot, c=tkc:
                                   s.activation(pT[b][:, c * 512:c * 512 + 512],
                                                psc[c % 4][:, :], AF.Exp,
                                                bias=0.0, scale=SCALE))
                            expmark[(it, tkc)] = sch.mark("S")
                            if (tslot == 0) or (tkc >= 8):
                                sch.wait("V", expmark[(it, tkc)])
                                sch.op("V", lambda v, b=tslot, c=tkc:
                                       v.tensor_mul(
                                           pT[b][:, c * 512:c * 512 + 512],
                                           pT[b][:, c * 512:c * 512 + 512],
                                           cmask[:, c * 512:c * 512 + 512]))
                                expmark[(it, tkc)] = sch.mark("V")
                            if tkc >= 2:
                                _emit_sumav(sch, it, tkc - 2, C, hb, tslot,
                                            pT, vT, prow, pav, ones, expmark,
                                            rmk, attn_emk)
                        for c2 in (C - 2, C - 1):
                            _emit_sumav(sch, it, c2, C, hb, tslot, pT, vT,
                                        prow, pav, ones, expmark, rmk,
                                        attn_emk)
                        avm = sch.mark("T")
                        sch.wait("V", avm)
                        def _recip(v):
                            with nc.allow_low_precision(reason="softmax 1/s"):
                                return v.reciprocal(rs[:, :], prow[:, :])
                        sch.op("V", _recip)
                        rmk[it] = sch.mark("V")
                        pending_tail[0] = it
                emit_tail(pending_tail[0])
                pending_tail[0] = None
            sch.barrier()
        if "D" in PHASES:
            with contextlib.ExitStack() as dc:
                def pumpd(n=1):
                    pass
                yfeat = dc.enter_context(
                    nc.sbuf_tensor("yfeat", [128, 16 * TPC], BF16))
                wo = [dc.enter_context(nc.sbuf_tensor(f"wosb{i}", [128, 4096], BF16))
                      for i in range(2)]
                ytok = dc.enter_context(nc.sbuf_tensor("ytok", [128, D], F32))
                lno = ytok
                bout = dc.enter_context(nc.sbuf_tensor("boutsb", [128, D], F32))
                xti = [dc.enter_context(nc.sbuf_tensor(f"xtiD{i}", [128, D], F32))
                       for i in range(2)]
                botok = sch.dma("Y", bout[:, :], bout_d[:, :], "c3")
                sch.wait("V", botok)
                with contextlib.ExitStack() as pc:
                    po = [pc.enter_context(
                        nc.psum_tensor(f"po{i}", [128, 512], F32))
                        for i in range(4)]
                    ev, wtok, ommk = {}, {}, {}
                    for oc in range(16):
                        if oc % 2 == 0:
                            if oc >= 4:
                                sch.wait("G", ommk[oc - 3])
                            wtok[oc // 2] = sch.dma(
                                "G", wo[(oc // 2) % 2][:, :],
                                AP(woutT, oc * 128,
                                   [[D, 128], [128 * D, 16], [1, 256]]),
                                f"wo{(oc // 2) % 2}")
                        bks = (oc % 2) * 2
                        sch.wait("T", wtok[oc // 2])
                        if oc >= 2:
                            sch.wait("T", ev[oc - 2])
                        for dcx in range(16):
                            pumpd(1)
                            for tg in range(2):
                                sch.op("T", lambda t, d_=dcx, o=oc, g=tg, bi=bks + tg:
                                       t.matmul(
                                           po[bi][:, :],
                                           wo[(o // 2) % 2][:, d_ * 256 + (o % 2) * 128:
                                                            d_ * 256 + (o % 2) * 128 + 128],
                                           attnout[:, d_ * TPC + g * 512:
                                                   d_ * TPC + g * 512 + 512],
                                           start=(d_ == 0), stop=(d_ == 15)))
                        ommk[oc] = omm = sch.mark("T")
                        sch.wait("S", omm)
                        for tg in range(2):
                            sch.op("S", lambda s, o=oc, g=tg, bi=bks + tg:
                                   s.copy(yfeat[:, o * TPC + g * 512:
                                                o * TPC + g * 512 + 512],
                                          po[bi][:, :]))
                        ev[oc] = sch.mark("S")
                    sch.barrier()
                with contextlib.ExitStack() as pc:
                    ptb = [pc.enter_context(
                        nc.psum_tensor(f"ptb{i}", [128, 1024], BF16))
                        for i in range(2)]
                    prev = {}
                    xst = {}
                    for to in range(8):
                        pumpd(4)
                        if to >= 2:
                            sch.wait("Y", xst[to - 2])
                        xtok = sch.dma("Y", xti[to % 2][:, :],
                                       x_row[_own_tile(to) * 128:
                                             _own_tile(to) * 128 + 128, :],
                                       f"dx{to % 2}")
                        tg, tl = to // 4, to % 4
                        for ocx in range(16):
                            p = ptb[ocx // 8]
                            if ocx % 8 == 0 and to >= 1:
                                sch.wait("T", prev.get((to - 1, ocx // 8)))
                            sch.op("T", lambda t, p_=p, o=ocx, g=tg, l=tl:
                                   t.matmul(p_[:, (o % 8) * 128:(o % 8) * 128 + 128],
                                            yfeat[:, o * TPC + g * 512 + l * 128:
                                                  o * TPC + g * 512 + l * 128 + 128],
                                            ident[:, :], is_transpose=True,
                                            start=(o % 8 == 0), stop=(o % 8 == 7)))
                            if ocx % 8 == 7:
                                tmk = sch.mark("T")
                                sch.wait("V", tmk)
                                if to >= 1:
                                    sch.wait("V", prev[to - 1])
                                sch.op("V", lambda v, p_=p, g=ocx // 8:
                                       v.scalar_tensor_tensor(
                                           ytok[:, g * 1024:(g + 1) * 1024],
                                           p_[:, :], 0.0,
                                           bout[:, g * 1024:(g + 1) * 1024],
                                           ALU.add, ALU.add))
                                prev[(to, ocx // 8)] = sch.mark("V")
                        sch.wait("S", prev[(to, 0)])
                        sch.wait("S", prev[(to, 1)])
                        _emit_ln(sch, col, 8 * (to % 2), ytok[:, :], lnsq,
                                 out_f32=lno)
                        sch.wait("V", xtok)
                        sch.op("V", lambda v, o=to:
                               v.tensor_add(xti[o % 2][:, :], xti[o % 2][:, :],
                                            lno[:, :]))
                        prev[to] = sch.mark("V")
                        sch.wait("Y", prev[to])
                        xst[to] = sch.dma("Y",
                                          xmid[to * 128:(to + 1) * 128, :],
                                          xti[to % 2][:, :], f"dxs{to % 2}")
                    pumpd(400)
            sch.barrier()
        qscope.close()

        if "W" in PHASES:
            with contextlib.ExitStack() as wdc:
                pdnA = [wdc.enter_context(
                    nc.psum_tensor(f"pdnA{i}", [128, 512], F32))
                    for i in range(4)]
                pdnB = [wdc.enter_context(
                    nc.psum_tensor(f"pdnB{i}", [128, 512], F32))
                    for i in range(4)]
                dgen = make_down_gen(wdc, pdnA, pdnB, depth=4)
                for _ in range(NDCH * 9 + 20):
                    next(dgen)
            sch.barrier()

        if "E" in PHASES:
            with contextlib.ExitStack() as ec:
                x2f = ec.enter_context(nc.sbuf_tensor("x2f", [128, 16 * 512], BF16))
                act = ec.enter_context(nc.sbuf_tensor("actsb", [128, 64 * 512], BF16))
                wg = [ec.enter_context(nc.sbuf_tensor(f"wgsb{i}", [128, 2048], BF16))
                      for i in range(4)]
                wu = [ec.enter_context(nc.sbuf_tensor(f"wusb{i}", [128, 2048], BF16))
                      for i in range(4)]
                wd = [ec.enter_context(nc.sbuf_tensor(f"wdsb{i}", [128, 2048], BF16))
                      for i in range(2)]
                xtie = [ec.enter_context(nc.sbuf_tensor(f"xtiE{i}", [128, D], F32))
                       for i in range(2)]
                sil = [ec.enter_context(nc.sbuf_tensor(f"silsb{i}", [128, 512], BF16))
                       for i in range(4)]
                y2f = ec.enter_context(nc.sbuf_tensor("y2f", [128, 16 * 512], BF16))
                ytoke = ec.enter_context(nc.sbuf_tensor("ytokE", [128, D], F32))
                lnoe = ytoke

                for g in range(2):
                    sch.barrier()
                    with contextlib.ExitStack() as pc:
                        ptr = [pc.enter_context(
                            nc.psum_tensor(f"ptrE{g}{i}", [128, 1024], BF16))
                            for i in range(2)]
                        pev, nmks, trmk = {}, {}, {}

                        def e_transpose(tl):
                            for hc in range(16):
                                p = ptr[hc // 8]
                                if hc % 8 == 0:
                                    sch.wait("T", nmks[tl])
                                    if tl >= 1:
                                        sch.wait("T", pev.get((tl - 1, hc // 8)))
                                sch.op("T", lambda t, p_=p, h=hc, b=tl % 2:
                                       t.matmul(p_[:, (h % 8) * 128:(h % 8) * 128 + 128],
                                                lnsqs[b][:, h * 128:h * 128 + 128],
                                                ident[:, :], is_transpose=True,
                                                start=(h % 8 == 0), stop=(h % 8 == 7)))
                                if hc % 8 == 7:
                                    tmk = sch.mark("T")
                                    trmk[tl] = tmk
                                    sch.wait("V", tmk)
                                    sch.op("V", lambda v, p_=p, t_=tl, gg=hc // 8:
                                           v.tensor_copy(
                                               AP(x2f, (gg * 8) * 512 + t_ * 128,
                                                  [[16 * 512, 128], [512, 8], [1, 128]]),
                                               p_[:, :]))
                                    pev[(tl, hc // 8)] = sch.mark("V")

                        for tl in range(4):
                            to = g * 4 + tl
                            if tl >= 2:
                                sch.wait("Y", nmks[tl - 2])
                            xtok = sch.dma("Y", xtie[tl % 2][:, :],
                                           xmid[to * 128:(to + 1) * 128, :],
                                           f"ex{tl % 2}")
                            sch.wait("V", xtok)
                            sch.wait("S", xtok)
                            if tl >= 2:
                                sch.wait("S", trmk[tl - 2])
                                sch.wait("V", trmk[tl - 2])
                            _emit_ln(sch, col, 8 * (tl % 2),
                                     xtie[tl % 2][:, :], lnsqs[tl % 2],
                                     out_bf=lnsqs[tl % 2])
                            nmks[tl] = sch.mark("V")
                            if tl >= 1:
                                e_transpose(tl - 1)
                        e_transpose(3)
                        sch.barrier()
                    with contextlib.ExitStack() as pc:
                        pg = [pc.enter_context(
                            nc.psum_tensor(f"pgE{g}{i}", [128, 512], F32))
                            for i in range(4)]
                        pu = [pc.enter_context(
                            nc.psum_tensor(f"puE{g}{i}", [128, 512], F32))
                            for i in range(4)]
                        ev, wtokg, wtoku, gmmk = {}, {}, {}, {}
                        for oc in range(64):
                            if oc >= 4:
                                sch.wait("Y", gmmk[oc - 4])
                            wtokg[oc] = sch.dma(
                                "Y", wg[oc % 4][:, :],
                                AP(wbuf_g, oc * GCH, [[NBLK, 128], [1, GCH]]),
                                f"eg{oc % 4}")
                            wtoku[oc] = sch.dma(
                                "Y", wu[oc % 4][:, :],
                                AP(wbuf_u, oc * GCH, [[NBLK, 128], [1, GCH]]),
                                f"eu{oc % 4}")
                            bkg = oc % 4
                            sch.wait("T", wtokg[oc])
                            sch.wait("T", wtoku[oc])
                            if oc >= 4:
                                sch.wait("T", ev[oc - 4])
                            for hc in range(16):
                                sch.op("T", lambda t, h=hc, o=oc, bi=bkg:
                                       t.matmul(
                                           pg[bi][:, :],
                                           wg[o % 4][:, h * 128:h * 128 + 128],
                                           x2f[:, h * 512:h * 512 + 512],
                                           start=(h == 0), stop=(h == 15)))
                                sch.op("T", lambda t, h=hc, o=oc, bi=bkg:
                                       t.matmul(
                                           pu[bi][:, :],
                                           wu[o % 4][:, h * 128:h * 128 + 128],
                                           x2f[:, h * 512:h * 512 + 512],
                                           start=(h == 0), stop=(h == 15)))
                            gmmk[oc] = gmm = sch.mark("T")
                            sch.wait("S", gmm)
                            if oc >= 4:
                                sch.wait("S", ev[oc - 4])
                            sch.op("S", lambda s, bi=bkg, o=oc:
                                   s.activation(sil[o % 4][:, :], pg[bi][:, :],
                                                AF.Silu, bias=0.0, scale=1.0))
                            smk = sch.mark("S")
                            sch.wait("V", smk)
                            sch.op("V", lambda v, o=oc, bi=bkg:
                                   v.tensor_mul(act[:, o * 512:o * 512 + 512],
                                                sil[o % 4][:, :], pu[bi][:, :]))
                            ev[oc] = sch.mark("V")
                        sch.barrier()
                    with contextlib.ExitStack() as pc:
                        pd = [pc.enter_context(
                            nc.psum_tensor(f"pdE{g}{i}", [128, 512], F32))
                            for i in range(4)]
                        ev, wtokd, dmmk = {}, {}, {}
                        for hcp in range(8):
                            bks = (hcp % 2) * 2
                            if hcp >= 2:
                                sch.wait("T", ev[hcp - 2])
                            for ic in range(64):
                                if ic % 8 == 0:
                                    icq = ic // 8
                                    prevq = (hcp, icq - 2) if icq >= 2 else \
                                            ((hcp - 1, 6 + icq) if hcp >= 1 else None)
                                    if prevq is not None:
                                        sch.wait("Y", dmmk.get(prevq))
                                    wtokd[(hcp, icq)] = sch.dma(
                                        "Y", wd[icq % 2][:, :],
                                        AP(wbuf_d, icq * 8 * D + hcp * 256,
                                           [[64 * D, 128], [D, 8], [1, 256]]),
                                        f"ed{icq % 2}")
                                    sch.wait("T", wtokd[(hcp, icq)])
                                for o2 in range(2):
                                    sch.op("T", lambda t, i_=ic, o=o2, bi=bks:
                                           t.matmul(
                                               pd[bi + o][:, :],
                                               wd[(i_ // 8) % 2][:, (i_ % 8) * 256 + o * 128:
                                                            (i_ % 8) * 256 + o * 128 + 128],
                                               act[:, i_ * 512:i_ * 512 + 512],
                                               start=(i_ == 0), stop=(i_ == 63)))
                                if ic % 8 == 7:
                                    dmmk[(hcp, ic // 8)] = sch.mark("T")
                            dmm = sch.mark("T")
                            sch.wait("S", dmm)
                            if hcp >= 2:
                                sch.wait("S", ev[hcp - 2])
                            for o2 in range(2):
                                sch.op("S", lambda s, h=hcp, o=o2, bi=bks:
                                       s.copy(y2f[:, (2 * h + o) * 512:
                                                  (2 * h + o) * 512 + 512],
                                              pd[bi + o][:, :]))
                            ev[hcp] = sch.mark("S")
                        sch.barrier()
                    with contextlib.ExitStack() as pc:
                        ptb = [pc.enter_context(
                            nc.psum_tensor(f"ptbE{g}{i}", [128, 1024], BF16))
                            for i in range(2)]
                        prev = {}
                        ost = {}
                        for tl in range(4):
                            to = g * 4 + tl
                            if tl >= 2:
                                sch.wait("Y", ost[tl - 2])
                            xtok = sch.dma("Y", xtie[tl % 2][:, :],
                                           xmid[to * 128:(to + 1) * 128, :],
                                           f"e4x{tl % 2}")
                            for ocx in range(16):
                                p = ptb[ocx // 8]
                                if ocx % 8 == 0 and tl >= 1:
                                    sch.wait("T", prev.get((tl - 1, ocx // 8)))
                                sch.op("T", lambda t, p_=p, o=ocx, l=tl:
                                       t.matmul(p_[:, (o % 8) * 128:(o % 8) * 128 + 128],
                                                y2f[:, o * 512 + l * 128:
                                                    o * 512 + l * 128 + 128],
                                                ident[:, :], is_transpose=True,
                                                start=(o % 8 == 0), stop=(o % 8 == 7)))
                                if ocx % 8 == 7:
                                    tmk = sch.mark("T")
                                    sch.wait("S", tmk)
                                    if tl >= 1:
                                        sch.wait("S", prev[tl - 1])
                                    sch.op("S", lambda s, p_=p, gg=ocx // 8:
                                           s.copy(ytoke[:, gg * 1024:(gg + 1) * 1024],
                                                  p_[:, :]))
                                    prev[(tl, ocx // 8)] = sch.mark("S")
                            sch.wait("V", prev[(tl, 0)])
                            sch.wait("V", prev[(tl, 1)])
                            _emit_ln(sch, col, 8 * (tl % 2), ytoke[:, :], lnsq,
                                     out_f32=lnoe)
                            sch.wait("V", xtok)
                            sch.op("V", lambda v, l=tl:
                                   v.tensor_add(xtie[l % 2][:, :],
                                                xtie[l % 2][:, :], lnoe[:, :]))
                            fmk = sch.mark("V")
                            prev[tl] = fmk
                            sch.wait("Y", fmk)
                            ost[tl] = sch.dma(
                                "Y", out_d[to * 128:(to + 1) * 128, :],
                                xtie[tl % 2][:, :], f"out{tl % 2}")

        sch.finish("Y")
        sch.emit(nc, ctx)

    return nc


_TPR = {}


def _perm_gu():
    if "gu" not in _TPR:
        p = np.arange(GCH).reshape(GCH // 16, 16).T.ravel()
        _TPR["gu"] = (np.arange(NBLK).reshape(-1, GCH)[:, p]).ravel()
    return _TPR["gu"]


def _perm_dn():
    if "dn" not in _TPR:
        p = np.arange(DCH).reshape(DCH // 64, 64).T.ravel()
        _TPR["dn"] = (np.arange(NBLK).reshape(-1, DCH)[:, p]).ravel()
    return _TPR["dn"]


def _tau_tiled():
    cst = _consts()
    tp = cst["taupow"]
    return np.concatenate([np.tile(tp[0:4], (4, 1)), np.tile(tp[4:8], (4, 1))])


def _tpr0():
    if 0 not in _TPR:
        _TPR[0] = np.ascontiguousarray(_tau_tiled()[:, _perm_gu()])
    return _TPR[0]


def _tpr1():
    if 1 not in _TPR:
        _TPR[1] = np.ascontiguousarray(_tau_tiled()[:, _perm_dn()])
    return _TPR[1]


def _get_program():
    global _PROG
    if _PROG is None:
        _PROG = build_program()
    return _PROG


def kernel(L0_attn_x, L0_attn_w_qkv, L0_attn_b_qkv, L0_attn_w_out,
           L0_attn_b_out, L0_ffn_gate_cp, L0_ffn_up_cp, L0_ffn_down_cp):
    cst = _consts()
    nc = _get_program()

    x = np.ascontiguousarray(np.asarray(L0_attn_x, dtype=np.float32))
    wqkvT = np.ascontiguousarray(np.asarray(L0_attn_w_qkv, np.float32).T)
    woutT = np.ascontiguousarray(np.asarray(L0_attn_w_out, np.float32).T)
    bqkv = np.ascontiguousarray(
        np.asarray(L0_attn_b_qkv, np.float32).reshape(48, 128).T)
    bout_rep = np.ascontiguousarray(
        np.broadcast_to(np.asarray(L0_attn_b_out, np.float32)[None, :],
                        (128, D)))
    cps = [np.asarray(c, np.float32) for c in
           (L0_ffn_gate_cp, L0_ffn_up_cp, L0_ffn_down_cp)]
    taps = [c[cst["tapidx"]] for c in cps]
    tap_gu = np.ascontiguousarray(
        np.concatenate([taps[0], taps[1]])[:, _perm_gu()])
    tap_dn = np.ascontiguousarray(taps[2][:, _perm_dn()])
    wgm_gu = np.ascontiguousarray(cst["wgmask"][:, _perm_gu()])
    wgm_dn = np.ascontiguousarray(cst["wgmask"][:, _perm_dn()])

    in_maps = []
    for c in range(NCORES):
        b, j = c // 2, c % 2
        gk = [j, 1 - j, 2 + j, 3 - j]
        xr = np.ascontiguousarray(x[b].reshape(4, 512, D)[gk].reshape(S, D))
        in_maps.append(dict(
            x_row=xr, wqkvT=wqkvT, bqkv=bqkv, woutT=woutT,
            bout_rep=bout_rep, tap_gu=tap_gu, tap_dn=tap_dn,
            tau_gu=_tpr0(), tau_dn=_tpr1(),
            atilde=cst["atilde"], wgmask_gu=wgm_gu, wgmask_dn=wgm_dn,
            cmask=np.ascontiguousarray(cst["cmask"][j]),
            ident=cst["ident"], ones=cst["ones"],
        ))

    res = run_bass_kernel_spmd(nc, in_maps, core_ids=list(range(NCORES)))

    out = np.empty((B, S, D), dtype=np.float32)
    for c in range(NCORES):
        b, j = c // 2, c % 2
        o = res.results[c]["out"]
        out[b, j * 512:(j + 1) * 512] = o[0:512]
        out[b, (2 + j) * 512:(3 + j) * 512] = o[512:1024]
    return out


# version 21
# speedup vs baseline: 1.3512x; 1.1507x over previous
import contextlib
import math

import numpy as np
import ml_dtypes

import concourse.bass as bass
import concourse.mybir as mybir
from concourse.bass_utils import run_bass_kernel_spmd

F32 = mybir.dt.float32
BF16 = mybir.dt.bfloat16
AF = mybir.ActivationFunctionType
ALU = mybir.AluOpType
AX = mybir.AxisListType
AP = bass.AP

B, S, D = 4, 2048, 2048
H, HD = 16, 128
INTER = 8192
NCTRL = 130156
NSAMP = INTER * D
NBLK = NSAMP // 128
GCH = 2048
NGCH = NBLK // GCH
DCH = 4096
NDCH = NBLK // DCH
TPC = 1024
SCALE = 1.0 / math.sqrt(HD)
INV_D = 1.0 / D
EPS = 1e-5
NCORES = 8
CTX = (8, 16)


_CONSTS = None


def _consts():
    global _CONSTS
    if _CONSTS is not None:
        return _CONSTS
    t = np.linspace(0.0, NCTRL - 1.0, NSAMP, dtype=np.float64)
    iarr = np.clip(np.floor(t).astype(np.int64), 0, NCTRL - 2)
    i_b = iarr[::128]
    tau = t[::128] - i_b
    s_b = (iarr.reshape(NBLK, 128) == i_b[:, None]).sum(1)
    R = (NCTRL - 1.0) / (NSAMP - 1)

    Cc = np.array([
        [0.0, -0.5, 1.0, -0.5],
        [1.0, 0.0, -2.5, 1.5],
        [0.0, 0.5, 2.0, -1.5],
        [0.0, 0.0, -0.5, 0.5]], dtype=np.float64)
    from math import comb
    qR = np.arange(128, dtype=np.float64) * R
    at = np.zeros((16, 128), dtype=np.float64)
    for k in range(4):
        for m in range(4):
            v = np.zeros_like(qR)
            for jj in range(m, 4):
                v += comb(jj, m) * Cc[k, jj] * qR ** (jj - m)
            at[4 * k + m] = v
    tp = np.zeros((8, NBLK), dtype=np.float64)
    for m in range(4):
        tp[m] = tau ** m
        tp[4 + m] = (tau - 1.0) ** m
    mask = (np.arange(128)[:, None] < s_b[None, :])
    tapidx = np.stack([np.clip(i_b + r - 1, 0, NCTRL - 1) for r in range(5)])

    cmask = {}
    for j in range(2):
        gk = [j, 1 - j, 2 + j, 3 - j]
        cm = np.zeros((16, 128, 512), dtype=np.float64)
        for cr in range(16):
            tslot = 0 if cr < 8 else 1
            gt = 2 * tslot + j
            tk = gk[cr // 4] * 512 + (cr % 4) * 128 + np.arange(128)[:, None]
            tq = gt * 512 + np.arange(512)[None, :]
            cm[cr] = (tk <= tq)
        cmask[j] = cm.astype(ml_dtypes.bfloat16)

    r0_tap = np.zeros(128, dtype=np.int64)
    r0_wt = np.zeros(128, dtype=np.int64)
    r0_tau = np.zeros(128, dtype=np.int64)
    for p in range(128):
        si, r32 = p // 32, p % 32
        k, m = (r32 % 16) // 4, r32 % 4
        r0_wt[p] = si // 2
        r0_tap[p] = k + (si % 2)
        r0_tau[p] = (si % 2) * 4 + m
    r1_tap = np.zeros(64, dtype=np.int64)
    r1_tau = np.zeros(64, dtype=np.int64)
    for p in range(64):
        si, r32 = p // 32, p % 32
        k, m = (r32 % 16) // 4, r32 % 4
        r1_tap[p] = k + si
        r1_tau[p] = si * 4 + m
    _CONSTS = dict(
        tapidx=tapidx, s_b=s_b,
        r0_tap=r0_tap, r0_wt=r0_wt, r0_tau=r0_tau,
        r1_tap=r1_tap, r1_tau=r1_tau,
        atilde=at.astype(np.float32),
        taupow=tp.astype(np.float32),
        wgmask=mask.astype(np.uint8),
        ident=np.eye(128, dtype=ml_dtypes.bfloat16),
        ones=np.ones((128, 128), dtype=ml_dtypes.bfloat16),
        cmask=cmask,
    )
    return _CONSTS


class Sched:
    ENGINES = ("T", "V", "S", "G", "Y")

    def __init__(self):
        self.recs = {e: [] for e in self.ENGINES}
        self.cnt = {e.lower(): 0 for e in self.ENGINES}
        self.pend = {e: [] for e in self.ENGINES}
        self.streams = {}
        self.issuer = {}
        self.open_streams = {e: set() for e in self.ENGINES}

    CHAINED = ("V", "S", "G")

    def op(self, e, fn):
        w = list(self.pend[e])
        self.pend[e] = []
        incs = []
        if e in self.CHAINED:
            sem = e.lower()
            if self.cnt[sem]:
                w.append((sem, self.cnt[sem]))
            self.cnt[sem] += 1
            incs.append((sem, 1, self.cnt[sem]))
        self.recs[e].append([fn, tuple(w), incs, False])

    def mark(self, e):
        rec = self.recs[e][-1]
        if rec[3]:
            sem, _n, tgt = rec[2][0]
            return (sem, tgt)
        if rec[2]:
            sem, _n, tgt = rec[2][0]
            return (sem, tgt)
        sem = e.lower()
        self.cnt[sem] += 1
        rec[2].append((sem, 1, self.cnt[sem]))
        return (sem, self.cnt[sem])

    def dma(self, e, out, in_, stream):
        sname = "q_" + stream
        assert self.issuer.setdefault(sname, e) == e, (sname, e)
        self.streams[sname] = self.streams.get(sname, 0) + 16
        tgt = self.streams[sname]
        self.open_streams[e].add(sname)
        w = tuple(self.pend[e])
        self.pend[e] = []
        self.recs[e].append(
            [lambda eng, o=out, i=in_: eng.dma_start(out=o, in_=i), w,
             [(sname, 16, tgt)], True])
        return (sname, tgt)

    def wait(self, e, tok):
        if tok is not None:
            self.pend[e].append(tok)

    def barrier(self):
        toks = []
        for e in self.ENGINES:
            if not self.recs[e] and not self.open_streams[e]:
                continue
            if self.open_streams[e]:
                for sname in sorted(self.open_streams[e]):
                    self.pend[e].append((sname, self.streams[sname]))
                self.open_streams[e] = set()
                self.op(e, lambda eng: eng.nop())
            toks.append(self.mark(e))
        for e in self.ENGINES:
            for t in toks:
                self.pend[e].append(t)

    def finish(self, e):
        for sname in sorted(self.open_streams[e]):
            self.pend[e].append((sname, self.streams[sname]))
        self.op(e, lambda eng: eng.nop())

    def emit(self, nc, ctx):
        sems = {}
        for sn in self.cnt:
            sems[sn] = ctx.enter_context(nc.semaphore("sem_" + sn))
        for sn in self.streams:
            sems[sn] = ctx.enter_context(nc.semaphore(sn))

        def replay(e):
            def f(eng):
                for fn, waits, incs, _isdma in self.recs[e]:
                    agg = {}
                    for (sn, v) in waits:
                        agg[sn] = max(agg.get(sn, 0), v)
                    for sn, v in agg.items():
                        eng.wait_ge(sems[sn], v)
                    inst = fn(eng)
                    for (sn, n, _tgt) in incs:
                        inst.then_inc(sems[sn], n)
            return f
        with nc.Block() as blk:
            blk.tensor(replay("T"))
            blk.vector(replay("V"))
            blk.scalar(replay("S"))
            blk.gpsimd(replay("G"))
            blk.sync(replay("Y"))


def _emit_sumav(sch, it, c, C, hb, tslot, pT, vT, prow, pav, ones, expmark,
                rmk, attn_emk):
    sch.wait("T", expmark[(it, c)])
    if c == 0:
        sch.wait("T", rmk.get(it - 1))
        sch.wait("T", attn_emk.get(it - 2))
    sch.op("T", lambda t, b=tslot, c_=c, C_=C:
           t.matmul(prow[:, :], ones[:, 0:1],
                    pT[b][:, c_ * 512:c_ * 512 + 512],
                    start=(c_ == 0), stop=(c_ == C_ - 1)))
    sch.op("T", lambda t, b=hb, bb=tslot, c_=c, C_=C, pa=pav[it % 2]:
           t.matmul(pa[:, :], vT[b][:, c_ * 128:c_ * 128 + 128],
                    pT[bb][:, c_ * 512:c_ * 512 + 512],
                    start=(c_ == 0), stop=(c_ == C_ - 1)))


def _own_tile(to):
    return to if to < 4 else to + 4


def _emit_ln(sch, col, c0, xap, sqscratch, out_bf=None, out_f32=None):
    sch.op("V", lambda v: v.reduce_sum(col[:, c0:c0 + 1], xap, axis=AX.X))
    sch.op("S", lambda s: s.activation(sqscratch[:, :], xap, AF.Square,
                                       bias=0.0, scale=1.0,
                                       accum_out=col[:, c0 + 1:c0 + 2]))
    sqm = sch.mark("S")
    sch.op("V", lambda v: v.tensor_scalar(col[:, c0 + 2:c0 + 3],
                                          col[:, c0:c0 + 1], INV_D, None,
                                          ALU.mult))
    sch.op("V", lambda v: v.tensor_mul(col[:, c0 + 3:c0 + 4],
                                       col[:, c0 + 2:c0 + 3],
                                       col[:, c0 + 2:c0 + 3]))
    sch.wait("V", sqm)
    sch.op("V", lambda v: v.tensor_scalar(col[:, c0 + 4:c0 + 5],
                                          col[:, c0 + 1:c0 + 2], INV_D,
                                          col[:, c0 + 3:c0 + 4],
                                          ALU.mult, ALU.subtract))
    vmk = sch.mark("V")
    sch.wait("S", vmk)
    sch.op("S", lambda s: s.activation(col[:, c0 + 5:c0 + 6],
                                       col[:, c0 + 4:c0 + 5], AF.Sqrt,
                                       bias=EPS, scale=1.0))
    smk = sch.mark("S")
    sch.wait("V", smk)
    sch.op("V", lambda v: v.reciprocal(col[:, c0 + 6:c0 + 7],
                                       col[:, c0 + 5:c0 + 6]))
    sch.op("V", lambda v: v.tensor_mul(col[:, c0 + 7:c0 + 8],
                                       col[:, c0 + 2:c0 + 3],
                                       col[:, c0 + 6:c0 + 7]))
    if out_bf is not None:
        sch.op("V", lambda v: v.tensor_scalar(out_bf[:, :], xap,
                                              col[:, c0 + 6:c0 + 7],
                                              col[:, c0 + 7:c0 + 8],
                                              ALU.mult, ALU.subtract))
    if out_f32 is not None:
        sch.op("V", lambda v: v.tensor_scalar(out_f32[:, :], xap,
                                              col[:, c0 + 6:c0 + 7],
                                              col[:, c0 + 7:c0 + 8],
                                              ALU.mult, ALU.subtract))


_PROG = None
PHASES = ("W", "B", "C", "D", "E")


def build_program():
    nc = bass.Bass("TRN2", target_bir_lowering=False)
    sch = Sched()

    x_row = nc.dram_tensor("x_row", [S, D], F32, kind="ExternalInput")
    wqkvT = nc.dram_tensor("wqkvT", [D, 3 * D], F32, kind="ExternalInput")
    bqkv_d = nc.dram_tensor("bqkv", [128, 48], F32, kind="ExternalInput")
    woutT = nc.dram_tensor("woutT", [D, D], F32, kind="ExternalInput")
    bout_d = nc.dram_tensor("bout_rep", [128, D], F32, kind="ExternalInput")
    tap_gu_d = nc.dram_tensor("tap_gu", [10, NBLK], F32, kind="ExternalInput")
    tau_gu_d = nc.dram_tensor("tau_gu", [32, NBLK], F32, kind="ExternalInput")
    tap_dn_d = nc.dram_tensor("tap_dn", [5, NBLK], F32, kind="ExternalInput")
    tau_dn_d = nc.dram_tensor("tau_dn", [32, NBLK], F32, kind="ExternalInput")
    at_d = nc.dram_tensor("atilde", [16, 128], F32, kind="ExternalInput")
    wgm_gu_d = nc.dram_tensor("wgmask_gu", [128, NBLK], mybir.dt.uint8,
                              kind="ExternalInput")
    wgm_dn_d = nc.dram_tensor("wgmask_dn", [128, NBLK], mybir.dt.uint8,
                              kind="ExternalInput")
    cm_d = nc.dram_tensor("cmask", [16, 128, 512], BF16, kind="ExternalInput")
    id_d = nc.dram_tensor("ident", [128, 128], BF16, kind="ExternalInput")
    on_d = nc.dram_tensor("ones", [128, 128], BF16, kind="ExternalInput")
    out_d = nc.dram_tensor("out", [TPC, D], F32, kind="ExternalOutput")

    wbuf_g = nc.dram_tensor("wbuf_g", [128, NBLK], BF16)
    wbuf_u = nc.dram_tensor("wbuf_u", [128, NBLK], BF16)
    wbuf_d = nc.dram_tensor("wbuf_d", [128, 64, D], BF16)
    kvbuf = nc.dram_tensor("kvbuf", [2 * H, 128, S], BF16)
    xmid = nc.dram_tensor("xmid", [TPC, D], F32)

    ctx = contextlib.ExitStack()
    with ctx:
        csb = ctx.enter_context(nc.sbuf_tensor("constsb", [128, 4], F32))
        for ci, cv in enumerate((0.0, EPS, 1.0, SCALE)):
            nc.const_aps.register(F32, float(cv), csb[:, ci:ci + 1])
            sch.op("V", lambda v, i=ci, val=float(cv):
                   v.memset(csb[:, i:i + 1], val))
        ident = ctx.enter_context(nc.sbuf_tensor("identsb", [128, 128], BF16))
        ones = ctx.enter_context(nc.sbuf_tensor("onessb", [128, 128], BF16))
        at_sb = ctx.enter_context(nc.sbuf_tensor("atsb", [128, 128], BF16))
        bqkv = ctx.enter_context(nc.sbuf_tensor("bqkvsb", [128, 48], F32))
        col = ctx.enter_context(nc.sbuf_tensor("colsb", [128, 16], F32))
        lnsq = ctx.enter_context(nc.sbuf_tensor("lnsqsb", [128, D], BF16))
        lnsq2 = ctx.enter_context(nc.sbuf_tensor("lnsqsb2", [128, D], BF16))
        lnsqs = [lnsq, lnsq2]

        sch.dma("Y", ident[:, :], id_d[:, :], "c0")
        sch.dma("Y", ones[:, :], on_d[:, :], "c0")
        sch.dma("Y", bqkv[:, :], bqkv_d[:, :], "c0")
        for u in (0, 32, 64, 96):
            sch.dma("G", at_sb[u:u + 16, :], at_d[:, :], "c1")
        sch.barrier()

        def make_gu_gen(wc, pA, pB, depth):
            CT = [wc.enter_context(nc.sbuf_tensor(f"gct{i}", [42, GCH], BF16))
                  for i in range(3)]
            U = [wc.enter_context(nc.sbuf_tensor(f"Ug{i}", [128, GCH], BF16))
                 for i in range(2)]
            IN0 = [wc.enter_context(nc.sbuf_tensor(f"gi0_{i}", [128, GCH], BF16))
                   for i in range(2)]
            IN1 = [wc.enter_context(nc.sbuf_tensor(f"gi1_{i}", [128, GCH], BF16))
                   for i in range(2)]
            MSK = [wc.enter_context(
                nc.sbuf_tensor(f"gm_{i}", [128, GCH], mybir.dt.uint8))
                for i in range(4)]
            WO = [wc.enter_context(nc.sbuf_tensor(f"gwo_{i}", [128, GCH], BF16))
                  for i in range(4)]
            streams = [(wbuf_g, 0, 32), (wbuf_u, 64, 96)]
            for t in IN0 + IN1:
                sch.op("V", lambda v, t_=t: v.memset(t_[:, :], 0.0))
            msmk_gu = sch.mark("V")

            def ldtoks(c):
                sch.dma("G", CT[c % 3][0:10, :],
                        AP(tap_gu_d, c * GCH, [[NBLK, 10], [1, GCH]]),
                        f"wg{c % 3}")
                return sch.dma("G", CT[c % 3][10:42, :],
                               AP(tau_gu_d, c * GCH, [[NBLK, 32], [1, GCH]]),
                               f"wg{c % 3}")

            def gen():
                umark, evmark, stok, mtoks = {}, {}, {}, {}
                ltok, etok = {}, {}

                def expand(c):
                    sch.wait("G", ltok[c])
                    if c >= 2:
                        sch.wait("G", umark[c - 2])
                    else:
                        sch.wait("G", msmk_gu)
                    for si in range(4):
                        wt, dlt = si // 2, si % 2
                        sch.dma("G", AP(IN0[c % 2], (si * 32) * GCH,
                                        [[GCH, 16], [1, GCH]]),
                                AP(CT[c % 3], (wt * 5 + dlt) * GCH,
                                   [[GCH, 4], [0, 4], [1, GCH]]),
                                f"we{c % 2}")
                        etok[c] = sch.dma(
                            "G", AP(IN1[c % 2], (si * 32) * GCH,
                                    [[GCH, 16], [1, GCH]]),
                            AP(CT[c % 3], (10 + dlt * 16) * GCH,
                               [[GCH, 16], [1, GCH]]),
                            f"we{c % 2}")

                def umult(c):
                    sch.wait("V", etok[c])
                    if c >= 2:
                        sch.wait("V", evmark[(c - 2) * 8 + 7])
                    sch.op("V", lambda v, b=c % 2: v.tensor_mul(
                        U[b][:, :], IN0[b][:, :], IN1[b][:, :]))
                    umark[c] = sch.mark("V")

                for c in range(NGCH):
                    yield
                    buf = c % 2
                    if c == 0:
                        for cpre in (0, 1):
                            mtoks[cpre] = sch.dma(
                                "G", MSK[cpre % 4][:, :],
                                wgm_gu_d[:, cpre * GCH:(cpre + 1) * GCH],
                                f"gm{cpre % 4}")
                        ltok[0] = ldtoks(0)
                        ltok[1] = ldtoks(1)
                        expand(0)
                        umult(0)
                    if c + 2 < NGCH:
                        if c >= 1:
                            sch.wait("G", etok[c - 1])
                        ltok[c + 2] = ldtoks(c + 2)
                    sch.wait("T", umark[c])
                    for w in range(2):
                        wb, uA, uB = streams[w]
                        cw = c * 2 + w
                        first_ev = True
                        for q4 in range(4):
                            yield
                            i2 = c * 8 + w * 4 + q4
                            bk = i2 % depth
                            if i2 >= depth:
                                sch.wait("T", evmark[i2 - depth])
                            for u, po in ((uA, pA[bk]), (uB, pB[bk])):
                                sch.op("T", lambda t, u_=u, b=buf, pp=po,
                                       cc=q4 * 512:
                                       t.matmul(pp[:, :],
                                                at_sb[u_:u_ + 16, :],
                                                U[b][u_:u_ + 16, cc:cc + 512],
                                                start=True, stop=True,
                                                tile_position=(u_, 0)))
                            mm = sch.mark("T")
                            sch.wait("S", mm)
                            if first_ev:
                                if cw >= 4:
                                    sch.wait("S", stok[cw - 4])
                                if w == 0:
                                    sch.wait("V", mtoks[c])
                                first_ev = False
                            wo_ap = WO[cw % 4][:, q4 * 512:(q4 + 1) * 512]
                            mk_ap = MSK[c % 4][:, q4 * 512:(q4 + 1) * 512]
                            sch.op("S", lambda s, o_=wo_ap, pp=pB[bk]:
                                   s.copy(o_, pp[:, :]))
                            smk = sch.mark("S")
                            sch.wait("V", smk)
                            sch.op("V", lambda v, o_=wo_ap, m_=mk_ap,
                                   pp=pA[bk]:
                                   v.copy_predicated(o_, m_, pp[:, :]))
                            evmark[i2] = sch.mark("V")
                        sch.wait("Y", evmark[c * 8 + w * 4 + 3])
                        stok[cw] = sch.dma(
                            "Y", wb[:, c * GCH:(c + 1) * GCH],
                            WO[cw % 4][:, :],
                            f"ws{cw % 4}")
                        if w == 0 and c + 1 < NGCH:
                            expand(c + 1)
                    if c + 2 < NGCH:
                        if c >= 2:
                            sch.wait("G", evmark[(c - 2) * 8 + 7])
                        mtoks[c + 2] = sch.dma(
                            "G", MSK[(c + 2) % 4][:, :],
                            wgm_gu_d[:, (c + 2) * GCH:(c + 3) * GCH],
                            f"gm{(c + 2) % 4}")
                    if c + 1 < NGCH:
                        umult(c + 1)
                while True:
                    yield
            return gen()

        def make_down_gen(wc, pA, pB, depth):
            CT = [wc.enter_context(nc.sbuf_tensor(f"dct{i}", [37, DCH], BF16))
                  for i in range(3)]
            Ud = [wc.enter_context(nc.sbuf_tensor(f"Ud{i}", [64, DCH], BF16))
                  for i in range(2)]
            IN0 = [wc.enter_context(nc.sbuf_tensor(f"di0_{i}", [64, DCH], BF16))
                   for i in range(2)]
            IN1 = [wc.enter_context(nc.sbuf_tensor(f"di1_{i}", [64, DCH], BF16))
                   for i in range(2)]
            MSK = [wc.enter_context(
                nc.sbuf_tensor(f"dm_{i}", [128, DCH], mybir.dt.uint8))
                for i in range(4)]
            WO = [wc.enter_context(nc.sbuf_tensor(f"dwo_{i}", [128, DCH], BF16))
                  for i in range(4)]
            NCG = DCH // 512
            for t in IN0 + IN1:
                sch.op("V", lambda v, t_=t: v.memset(t_[:, :], 0.0))
            msmk_dn = sch.mark("V")

            def ldtoks(c):
                sch.dma("G", CT[c % 3][0:5, :],
                        AP(tap_dn_d, c * DCH, [[NBLK, 5], [1, DCH]]),
                        f"wd{c % 3}")
                return sch.dma("G", CT[c % 3][5:37, :],
                               AP(tau_dn_d, c * DCH, [[NBLK, 32], [1, DCH]]),
                               f"wd{c % 3}")

            def gen():
                umark, evmark, stok, mtoks = {}, {}, {}, {}
                ltok, etok = {}, {}

                def expand(c):
                    sch.wait("G", ltok[c])
                    if c >= 2:
                        sch.wait("G", umark[c - 2])
                    else:
                        sch.wait("G", msmk_dn)
                    for si in range(2):
                        sch.dma("G", AP(IN0[c % 2], (si * 32) * DCH,
                                        [[DCH, 16], [1, DCH]]),
                                AP(CT[c % 3], si * DCH,
                                   [[DCH, 4], [0, 4], [1, DCH]]),
                                f"de{c % 2}")
                        etok[c] = sch.dma(
                            "G", AP(IN1[c % 2], (si * 32) * DCH,
                                    [[DCH, 16], [1, DCH]]),
                            AP(CT[c % 3], (5 + si * 16) * DCH,
                               [[DCH, 16], [1, DCH]]),
                            f"de{c % 2}")

                def umult(c):
                    sch.wait("V", etok[c])
                    if c >= 2:
                        sch.wait("V", evmark[(c - 2) * NCG + NCG - 1])
                    sch.op("V", lambda v, b=c % 2: v.tensor_mul(
                        Ud[b][:, :], IN0[b][:, :], IN1[b][:, :]))
                    umark[c] = sch.mark("V")

                for c in range(NDCH):
                    yield
                    buf = c % 2
                    if c == 0:
                        for cpre in (0, 1):
                            mtoks[cpre] = sch.dma(
                                "G", MSK[cpre % 4][:, :],
                                wgm_dn_d[:, cpre * DCH:(cpre + 1) * DCH],
                                f"dm{cpre % 4}")
                        ltok[0] = ldtoks(0)
                        ltok[1] = ldtoks(1)
                        expand(0)
                        umult(0)
                    if c + 2 < NDCH:
                        if c >= 1:
                            sch.wait("G", etok[c - 1])
                        ltok[c + 2] = ldtoks(c + 2)
                    sch.wait("T", umark[c])
                    first_ev = True
                    for cg in range(NCG):
                        yield
                        i2 = c * NCG + cg
                        bk = i2 % depth
                        if i2 >= depth:
                            sch.wait("T", evmark[i2 - depth])
                        for u, po in ((0, pA[bk]), (32, pB[bk])):
                            sch.op("T", lambda t, u_=u, b=buf, pp=po,
                                   cc=cg * 512:
                                   t.matmul(pp[:, :], at_sb[u_:u_ + 16, :],
                                            Ud[b][u_:u_ + 16, cc:cc + 512],
                                            start=True, stop=True,
                                            tile_position=(u_, 0)))
                        mm = sch.mark("T")
                        sch.wait("S", mm)
                        if first_ev:
                            if c >= 4:
                                sch.wait("S", stok[c - 4])
                            sch.wait("V", mtoks[c])
                            first_ev = False
                        wo_ap = WO[c % 4][:, cg * 512:(cg + 1) * 512]
                        mk_ap = MSK[c % 4][:, cg * 512:(cg + 1) * 512]
                        sch.op("S", lambda s, o_=wo_ap, pp=pB[bk]:
                               s.copy(o_, pp[:, :]))
                        smk = sch.mark("S")
                        sch.wait("V", smk)
                        sch.op("V", lambda v, o_=wo_ap, m_=mk_ap, pp=pA[bk]:
                               v.copy_predicated(o_, m_, pp[:, :]))
                        evmark[i2] = sch.mark("V")
                        if cg == NCG // 2 and c + 1 < NDCH:
                            expand(c + 1)
                    od = DCH // 64
                    sch.wait("Y", evmark[c * NCG + NCG - 1])
                    stok[c] = sch.dma(
                        "Y",
                        AP(wbuf_d, c * od,
                           [[64 * D, 128], [D, 64], [1, od]]),
                        AP(WO[c % 4], 0, [[DCH, 128], [od, 64], [1, od]]),
                        f"dws{c % 4}")
                    if c + 2 < NDCH:
                        if c >= 2:
                            sch.wait("G", evmark[(c - 2) * NCG + NCG - 1])
                        mtoks[c + 2] = sch.dma(
                            "G", MSK[(c + 2) % 4][:, :],
                            wgm_dn_d[:, (c + 2) * DCH:(c + 3) * DCH],
                            f"dm{(c + 2) % 4}")
                    if c + 1 < NDCH:
                        umult(c + 1)
                while True:
                    yield
            return gen()

        qscope = ctx.enter_context(contextlib.ExitStack())
        q_all = qscope.enter_context(
            nc.sbuf_tensor("qall", [128, H * TPC], BF16))
        if "W" in PHASES:
            with contextlib.ExitStack() as wsc:
                pguA = [wsc.enter_context(
                    nc.psum_tensor(f"pguA{i}", [128, 512], F32))
                    for i in range(4)]
                pguB = [wsc.enter_context(
                    nc.psum_tensor(f"pguB{i}", [128, 512], F32))
                    for i in range(4)]
                gugen = make_gu_gen(wsc, pguA, pguB, depth=4)
                for _ in range(NGCH * 9 + 20):
                    next(gugen)
            sch.barrier()

        if "B" in PHASES:
            with contextlib.ExitStack() as bc:

                x2T = bc.enter_context(nc.sbuf_tensor("x2T", [128, 16 * S], BF16))
                xt = [bc.enter_context(nc.sbuf_tensor(f"xt{i}", [128, D], F32))
                      for i in range(2)]
                wq = [bc.enter_context(nc.sbuf_tensor(f"wq{i}", [128, 4096], BF16))
                      for i in range(2)]
                kst = [bc.enter_context(nc.sbuf_tensor(f"kst{i}", [128, 512], BF16))
                       for i in range(4)]
                with contextlib.ExitStack() as tc:
                    ptr = [tc.enter_context(
                        nc.psum_tensor(f"ptr{i}", [128, 1024], BF16))
                        for i in range(2)]
                    ln_ev, nmks, trmk = {}, {}, {}

                    def b_transpose(ti):
                        for hc in range(16):
                            p = ptr[hc // 8]
                            if hc % 8 == 0:
                                sch.wait("T", nmks[ti])
                                if ti >= 1:
                                    sch.wait("T", ln_ev.get((ti - 1, hc // 8)))
                            sch.op("T", lambda t, p_=p, h=hc, b=ti % 2:
                                   t.matmul(p_[:, (h % 8) * 128:(h % 8) * 128 + 128],
                                            lnsqs[b][:, h * 128:h * 128 + 128],
                                            ident[:, :], is_transpose=True,
                                            start=(h % 8 == 0), stop=(h % 8 == 7)))
                            if hc % 8 == 7:
                                tmk = sch.mark("T")
                                trmk[ti] = tmk
                                sch.wait("V", tmk)
                                sch.op("V", lambda v, p_=p, t_=ti, g=hc // 8:
                                       v.tensor_copy(
                                           AP(x2T, (g * 8) * S + t_ * 128,
                                              [[16 * S, 128], [S, 8], [1, 128]]),
                                           p_[:, :]))
                                ln_ev[(ti, hc // 8)] = sch.mark("V")

                    for ti in range(16):
                        bufp = ti % 2
                        if ti >= 2:
                            sch.wait("Y", nmks[ti - 2])
                        xtok = sch.dma("Y", xt[bufp][:, :],
                                       x_row[ti * 128:(ti + 1) * 128, :],
                                       f"bx{ti % 2}")
                        sch.wait("V", xtok)
                        sch.wait("S", xtok)
                        if ti >= 2:
                            sch.wait("S", trmk[ti - 2])
                            sch.wait("V", trmk[ti - 2])
                        _emit_ln(sch, col, 8 * bufp, xt[bufp][:, :],
                                 lnsqs[bufp], out_bf=lnsqs[bufp])
                        nmks[ti] = sch.mark("V")
                        if ti >= 1:
                            b_transpose(ti - 1)
                    b_transpose(15)
                    sch.barrier()
                with contextlib.ExitStack() as qc:
                    pq = [qc.enter_context(
                        nc.psum_tensor(f"pq{i}", [128, 512], F32))
                        for i in range(4)]
                    evtix, wtok, kdma, qmmk = {}, {}, {}, {}
                    for oc in range(48):
                        if oc % 2 == 0:
                            if oc >= 4:
                                sch.wait("G", qmmk[oc - 3])
                            wtok[oc // 2] = sch.dma(
                                "G", wq[(oc // 2) % 2][:, :],
                                AP(wqkvT, oc * 128,
                                   [[3 * D, 128], [128 * 3 * D, 16], [1, 256]]),
                                f"wq{(oc // 2) % 2}")
                        tgs = [0, 2] if oc < 16 else [0, 1, 2, 3]
                        kind, hh = oc // 16, oc % 16
                        sch.wait("T", wtok[oc // 2])
                        for tix, tg in enumerate(tgs):
                            sch.wait("T", evtix.get((oc - 1, tix)))
                            for hc in range(16):
                                sch.op("T", lambda t, h=hc, o=oc, g=tg, bi=tix:
                                       t.matmul(
                                           pq[bi][:, :],
                                           wq[(o // 2) % 2][:, h * 256 + (o % 2) * 128:
                                                            h * 256 + (o % 2) * 128 + 128],
                                           x2T[:, h * S + g * 512:h * S + g * 512 + 512],
                                           start=(h == 0), stop=(h == 15)))
                            qmmk[oc] = qmm = sch.mark("T")
                            sch.wait("V", qmm)
                            if kind == 0:
                                dst = q_all[:, hh * TPC + (tg // 2) * 512:
                                            hh * TPC + (tg // 2) * 512 + 512]
                                sch.op("V", lambda v, d=dst, bi=tix, o=oc:
                                       v.tensor_scalar(
                                           d, pq[bi][:, :],
                                           bqkv[:, o:o + 1], None, ALU.add))
                            else:
                                if kdma.get(tix) is not None:
                                    sch.wait("V", kdma[tix])
                                sch.op("V", lambda v, d=kst[tix],
                                       bi=tix, o=oc:
                                       v.tensor_scalar(
                                           d[:, :], pq[bi][:, :],
                                           bqkv[:, o:o + 1], None, ALU.add))
                                vmk = sch.mark("V")
                                sch.wait("Y", vmk)
                                kdma[tix] = sch.dma(
                                    "Y",
                                    AP(kvbuf, ((kind - 1) * 16 + hh) * 128 * S
                                       + tg * 512, [[S, 128], [1, 512]]),
                                    kst[tix][:, :], f"kv{tix}")
                            evtix[(oc, tix)] = sch.mark("V")
                    if gugen is not None:
                        for _ in range(24):
                            next(gugen)
            sch.barrier()

        attnout = qscope.enter_context(
            nc.sbuf_tensor("attnout", [128, H * TPC], BF16))
        if "C" in PHASES:
            with contextlib.ExitStack() as cc:
                cmask = cc.enter_context(
                    nc.sbuf_tensor("cmasksb", [128, 16 * 512], BF16))
                cmtok = sch.dma(
                    "Y", AP(cmask, 0, [[16 * 512, 128], [512, 16], [1, 512]]),
                    AP(cm_d, 0, [[512, 128], [128 * 512, 16], [1, 512]]), "c2")
                sch.wait("V", cmtok)
                kb = [cc.enter_context(nc.sbuf_tensor(f"kb{i}", [128, S], BF16))
                      for i in range(2)]
                vb = [cc.enter_context(nc.sbuf_tensor(f"vb{i}", [128, S], BF16))
                      for i in range(2)]
                vT = [cc.enter_context(nc.sbuf_tensor(f"vT{i}", [128, S], BF16))
                      for i in range(2)]
                pT = [cc.enter_context(nc.sbuf_tensor(f"pT{i}", [128, 16 * 512], BF16))
                      for i in range(2)]
                rs = cc.enter_context(nc.sbuf_tensor("rs", [1, 512], BF16))
                bc_sb = cc.enter_context(nc.sbuf_tensor("bcsb", [128, 512], BF16))
                psc = [cc.enter_context(nc.psum_tensor(f"psc{i}", [128, 512], F32))
                       for i in range(4)]
                pvt = cc.enter_context(nc.psum_tensor("pvt", [128, 1024], BF16))
                pav = [cc.enter_context(nc.psum_tensor(f"pav{i}", [128, 512], F32))
                       for i in range(2)]
                prow = cc.enter_context(nc.psum_tensor("prow", [1, 512], F32))

                vtev = {}
                expmark = {}
                scmark = {}
                attn_emk = {}
                rmk = {}
                pending_tail = [None]

                def emit_tail(it_p):
                    h_p, ts_p = it_p // 2, it_p % 2
                    sch.wait("T", rmk[it_p])
                    if it_p >= 1:
                        sch.wait("T", attn_emk.get(it_p - 1))
                    sch.op("T", lambda t: t.matmul(
                        psc[2][:, :], ones[0:1, 0:128], rs[:, :],
                        start=True, stop=True))
                    bmk = sch.mark("T")
                    sch.wait("S", bmk)
                    sch.op("S", lambda s: s.copy(bc_sb[:, :], psc[2][:, :]))
                    csm = sch.mark("S")
                    sch.wait("V", csm)
                    sch.op("V", lambda v, ts=ts_p, hh=h_p, pa=pav[it_p % 2]:
                           v.tensor_mul(
                               attnout[:, hh * TPC + ts * 512:
                                       hh * TPC + ts * 512 + 512],
                               pa[:, :], bc_sb[:, :]))
                    attn_emk[it_p] = sch.mark("V")

                for h in range(H):
                    hb = h % 2
                    if h >= 2:
                        sch.wait("Y", attn_emk[2 * (h - 2) + 1])
                    ktok = sch.dma("Y", kb[hb][:, :],
                                   AP(kvbuf, h * 128 * S, [[S, 128], [1, S]]),
                                   f"ck{h % 2}")
                    vtok = sch.dma("Y", vb[hb][:, :],
                                   AP(kvbuf, (16 + h) * 128 * S, [[S, 128], [1, S]]),
                                   f"cv{h % 2}")
                    sch.wait("T", vtok)
                    for g2 in range(2):
                        if (h, g2) != (0, 0):
                            prevk = (h, g2 - 1) if g2 else (h - 1, 1)
                            sch.wait("T", vtev[prevk])
                        for tkc in range(8 * g2, 8 * g2 + 8):
                            sch.op("T", lambda t, c=tkc, b=hb:
                                   t.matmul(pvt[:, (c % 8) * 128:(c % 8) * 128 + 128],
                                            vb[b][:, c * 128:c * 128 + 128],
                                            ident[:, :], is_transpose=True,
                                            start=(c % 8 == 0), stop=(c % 8 == 7)))
                        tmk = sch.mark("T")
                        sch.wait("V", tmk)
                        sch.op("V", lambda v, b=hb, g=g2:
                               v.tensor_copy(vT[b][:, g * 1024:(g + 1) * 1024],
                                             pvt[:, :]))
                        vtev[(h, g2)] = sch.mark("V")
                    sch.wait("T", vtev[(h, 0)])
                    sch.wait("T", vtev[(h, 1)])
                    sch.wait("T", ktok)
                    for tslot in range(2):
                        C = CTX[tslot]
                        it = 2 * h + tslot
                        sch.wait("S", attn_emk.get(it - 2))
                        for tkc in range(C):
                            if tkc >= 4:
                                sch.wait("T", expmark[(it, tkc - 4)])
                            elif it >= 1:
                                last = [c for c in range(CTX[(it - 1) % 2])
                                        if c % 4 == tkc % 4]
                                if last:
                                    sch.wait("T", expmark[(it - 1, last[-1])])
                            if tkc % 4 == 2:
                                sch.wait("T", attn_emk.get(it - 1))
                            sch.op("T", lambda t, b=hb, c=tkc, ts=tslot, hh=h:
                                   t.matmul(psc[c % 4][:, :],
                                            kb[b][:, c * 128:c * 128 + 128],
                                            q_all[:, hh * TPC + ts * 512:
                                                  hh * TPC + ts * 512 + 512],
                                            start=True, stop=True))
                            scmark[(it, tkc)] = sch.mark("T")
                            if tkc == 1 and pending_tail[0] is not None:
                                emit_tail(pending_tail[0])
                                pending_tail[0] = None
                            sch.wait("S", scmark[(it, tkc)])
                            sch.op("S", lambda s, b=tslot, c=tkc:
                                   s.activation(pT[b][:, c * 512:c * 512 + 512],
                                                psc[c % 4][:, :], AF.Exp,
                                                bias=0.0, scale=SCALE))
                            expmark[(it, tkc)] = sch.mark("S")
                            if (tslot == 0) or (tkc >= 8):
                                sch.wait("V", expmark[(it, tkc)])
                                sch.op("V", lambda v, b=tslot, c=tkc:
                                       v.tensor_mul(
                                           pT[b][:, c * 512:c * 512 + 512],
                                           pT[b][:, c * 512:c * 512 + 512],
                                           cmask[:, c * 512:c * 512 + 512]))
                                expmark[(it, tkc)] = sch.mark("V")
                            if tkc >= 2:
                                _emit_sumav(sch, it, tkc - 2, C, hb, tslot,
                                            pT, vT, prow, pav, ones, expmark,
                                            rmk, attn_emk)
                        for c2 in (C - 2, C - 1):
                            _emit_sumav(sch, it, c2, C, hb, tslot, pT, vT,
                                        prow, pav, ones, expmark, rmk,
                                        attn_emk)
                        avm = sch.mark("T")
                        sch.wait("V", avm)
                        def _recip(v):
                            with nc.allow_low_precision(reason="softmax 1/s"):
                                return v.reciprocal(rs[:, :], prow[:, :])
                        sch.op("V", _recip)
                        rmk[it] = sch.mark("V")
                        pending_tail[0] = it
                emit_tail(pending_tail[0])
                pending_tail[0] = None
            sch.barrier()
        if "D" in PHASES:
            with contextlib.ExitStack() as dc:
                def pumpd(n=1):
                    pass
                yfeat = dc.enter_context(
                    nc.sbuf_tensor("yfeat", [128, 16 * TPC], BF16))
                wo = [dc.enter_context(nc.sbuf_tensor(f"wosb{i}", [128, 4096], BF16))
                      for i in range(2)]
                ytok = dc.enter_context(nc.sbuf_tensor("ytok", [128, D], F32))
                lno = ytok
                bout = dc.enter_context(nc.sbuf_tensor("boutsb", [128, D], F32))
                xti = [dc.enter_context(nc.sbuf_tensor(f"xtiD{i}", [128, D], F32))
                       for i in range(2)]
                botok = sch.dma("Y", bout[:, :], bout_d[:, :], "c3")
                sch.wait("V", botok)
                with contextlib.ExitStack() as pc:
                    po = [pc.enter_context(
                        nc.psum_tensor(f"po{i}", [128, 512], F32))
                        for i in range(4)]
                    ev, wtok, ommk = {}, {}, {}
                    for oc in range(16):
                        if oc % 2 == 0:
                            if oc >= 4:
                                sch.wait("G", ommk[oc - 3])
                            wtok[oc // 2] = sch.dma(
                                "G", wo[(oc // 2) % 2][:, :],
                                AP(woutT, oc * 128,
                                   [[D, 128], [128 * D, 16], [1, 256]]),
                                f"wo{(oc // 2) % 2}")
                        bks = (oc % 2) * 2
                        sch.wait("T", wtok[oc // 2])
                        if oc >= 2:
                            sch.wait("T", ev[oc - 2])
                        for dcx in range(16):
                            pumpd(1)
                            for tg in range(2):
                                sch.op("T", lambda t, d_=dcx, o=oc, g=tg, bi=bks + tg:
                                       t.matmul(
                                           po[bi][:, :],
                                           wo[(o // 2) % 2][:, d_ * 256 + (o % 2) * 128:
                                                            d_ * 256 + (o % 2) * 128 + 128],
                                           attnout[:, d_ * TPC + g * 512:
                                                   d_ * TPC + g * 512 + 512],
                                           start=(d_ == 0), stop=(d_ == 15)))
                        ommk[oc] = omm = sch.mark("T")
                        sch.wait("S", omm)
                        for tg in range(2):
                            sch.op("S", lambda s, o=oc, g=tg, bi=bks + tg:
                                   s.copy(yfeat[:, o * TPC + g * 512:
                                                o * TPC + g * 512 + 512],
                                          po[bi][:, :]))
                        ev[oc] = sch.mark("S")
                    sch.barrier()
                with contextlib.ExitStack() as pc:
                    ptb = [pc.enter_context(
                        nc.psum_tensor(f"ptb{i}", [128, 1024], BF16))
                        for i in range(2)]
                    prev = {}
                    xst = {}
                    for to in range(8):
                        pumpd(4)
                        if to >= 2:
                            sch.wait("Y", xst[to - 2])
                        xtok = sch.dma("Y", xti[to % 2][:, :],
                                       x_row[_own_tile(to) * 128:
                                             _own_tile(to) * 128 + 128, :],
                                       f"dx{to % 2}")
                        tg, tl = to // 4, to % 4
                        for ocx in range(16):
                            p = ptb[ocx // 8]
                            if ocx % 8 == 0 and to >= 1:
                                sch.wait("T", prev.get((to - 1, ocx // 8)))
                            sch.op("T", lambda t, p_=p, o=ocx, g=tg, l=tl:
                                   t.matmul(p_[:, (o % 8) * 128:(o % 8) * 128 + 128],
                                            yfeat[:, o * TPC + g * 512 + l * 128:
                                                  o * TPC + g * 512 + l * 128 + 128],
                                            ident[:, :], is_transpose=True,
                                            start=(o % 8 == 0), stop=(o % 8 == 7)))
                            if ocx % 8 == 7:
                                tmk = sch.mark("T")
                                sch.wait("V", tmk)
                                if to >= 1:
                                    sch.wait("V", prev[to - 1])
                                sch.op("V", lambda v, p_=p, g=ocx // 8:
                                       v.scalar_tensor_tensor(
                                           ytok[:, g * 1024:(g + 1) * 1024],
                                           p_[:, :], 0.0,
                                           bout[:, g * 1024:(g + 1) * 1024],
                                           ALU.add, ALU.add))
                                prev[(to, ocx // 8)] = sch.mark("V")
                        sch.wait("S", prev[(to, 0)])
                        sch.wait("S", prev[(to, 1)])
                        _emit_ln(sch, col, 8 * (to % 2), ytok[:, :], lnsq,
                                 out_f32=lno)
                        sch.wait("V", xtok)
                        sch.op("V", lambda v, o=to:
                               v.tensor_add(xti[o % 2][:, :], xti[o % 2][:, :],
                                            lno[:, :]))
                        prev[to] = sch.mark("V")
                        sch.wait("Y", prev[to])
                        xst[to] = sch.dma("Y",
                                          xmid[to * 128:(to + 1) * 128, :],
                                          xti[to % 2][:, :], f"dxs{to % 2}")
                    pumpd(400)
            sch.barrier()
        qscope.close()

        if "W" in PHASES:
            with contextlib.ExitStack() as wdc:
                pdnA = [wdc.enter_context(
                    nc.psum_tensor(f"pdnA{i}", [128, 512], F32))
                    for i in range(4)]
                pdnB = [wdc.enter_context(
                    nc.psum_tensor(f"pdnB{i}", [128, 512], F32))
                    for i in range(4)]
                dgen = make_down_gen(wdc, pdnA, pdnB, depth=4)
                for _ in range(NDCH * 9 + 20):
                    next(dgen)
            sch.barrier()

        if "E" in PHASES:
            with contextlib.ExitStack() as ec:
                x2f = ec.enter_context(nc.sbuf_tensor("x2f", [128, 16 * 512], BF16))
                act = ec.enter_context(nc.sbuf_tensor("actsb", [128, 64 * 512], BF16))
                wg = [ec.enter_context(nc.sbuf_tensor(f"wgsb{i}", [128, 2048], BF16))
                      for i in range(4)]
                wu = [ec.enter_context(nc.sbuf_tensor(f"wusb{i}", [128, 2048], BF16))
                      for i in range(4)]
                wd = [ec.enter_context(nc.sbuf_tensor(f"wdsb{i}", [128, 2048], BF16))
                      for i in range(2)]
                xtie = [ec.enter_context(nc.sbuf_tensor(f"xtiE{i}", [128, D], F32))
                       for i in range(2)]
                sil = [ec.enter_context(nc.sbuf_tensor(f"silsb{i}", [128, 512], BF16))
                       for i in range(4)]
                y2f = ec.enter_context(nc.sbuf_tensor("y2f", [128, 16 * 512], BF16))
                ytoke = ec.enter_context(nc.sbuf_tensor("ytokE", [128, D], F32))
                lnoe = ytoke

                for g in range(2):
                    sch.barrier()
                    with contextlib.ExitStack() as pc:
                        ptr = [pc.enter_context(
                            nc.psum_tensor(f"ptrE{g}{i}", [128, 1024], BF16))
                            for i in range(2)]
                        pev, nmks, trmk = {}, {}, {}

                        def e_transpose(tl):
                            for hc in range(16):
                                p = ptr[hc // 8]
                                if hc % 8 == 0:
                                    sch.wait("T", nmks[tl])
                                    if tl >= 1:
                                        sch.wait("T", pev.get((tl - 1, hc // 8)))
                                sch.op("T", lambda t, p_=p, h=hc, b=tl % 2:
                                       t.matmul(p_[:, (h % 8) * 128:(h % 8) * 128 + 128],
                                                lnsqs[b][:, h * 128:h * 128 + 128],
                                                ident[:, :], is_transpose=True,
                                                start=(h % 8 == 0), stop=(h % 8 == 7)))
                                if hc % 8 == 7:
                                    tmk = sch.mark("T")
                                    trmk[tl] = tmk
                                    sch.wait("V", tmk)
                                    sch.op("V", lambda v, p_=p, t_=tl, gg=hc // 8:
                                           v.tensor_copy(
                                               AP(x2f, (gg * 8) * 512 + t_ * 128,
                                                  [[16 * 512, 128], [512, 8], [1, 128]]),
                                               p_[:, :]))
                                    pev[(tl, hc // 8)] = sch.mark("V")

                        for tl in range(4):
                            to = g * 4 + tl
                            if tl >= 2:
                                sch.wait("Y", nmks[tl - 2])
                            xtok = sch.dma("Y", xtie[tl % 2][:, :],
                                           xmid[to * 128:(to + 1) * 128, :],
                                           f"ex{tl % 2}")
                            sch.wait("V", xtok)
                            sch.wait("S", xtok)
                            if tl >= 2:
                                sch.wait("S", trmk[tl - 2])
                                sch.wait("V", trmk[tl - 2])
                            _emit_ln(sch, col, 8 * (tl % 2),
                                     xtie[tl % 2][:, :], lnsqs[tl % 2],
                                     out_bf=lnsqs[tl % 2])
                            nmks[tl] = sch.mark("V")
                            if tl >= 1:
                                e_transpose(tl - 1)
                        e_transpose(3)
                        sch.barrier()
                    with contextlib.ExitStack() as pc:
                        pg = [pc.enter_context(
                            nc.psum_tensor(f"pgE{g}{i}", [128, 512], F32))
                            for i in range(4)]
                        pu = [pc.enter_context(
                            nc.psum_tensor(f"puE{g}{i}", [128, 512], F32))
                            for i in range(4)]
                        ev, wtokg, wtoku, gmmk = {}, {}, {}, {}
                        for oc in range(64):
                            if oc >= 4:
                                sch.wait("Y", gmmk[oc - 4])
                            wtokg[oc] = sch.dma(
                                "Y", wg[oc % 4][:, :],
                                AP(wbuf_g, oc * GCH, [[NBLK, 128], [1, GCH]]),
                                f"eg{oc % 4}")
                            wtoku[oc] = sch.dma(
                                "Y", wu[oc % 4][:, :],
                                AP(wbuf_u, oc * GCH, [[NBLK, 128], [1, GCH]]),
                                f"eu{oc % 4}")
                            bkg = oc % 4
                            sch.wait("T", wtokg[oc])
                            sch.wait("T", wtoku[oc])
                            if oc >= 4:
                                sch.wait("T", ev[oc - 4])
                            for hc in range(16):
                                sch.op("T", lambda t, h=hc, o=oc, bi=bkg:
                                       t.matmul(
                                           pg[bi][:, :],
                                           wg[o % 4][:, h * 128:h * 128 + 128],
                                           x2f[:, h * 512:h * 512 + 512],
                                           start=(h == 0), stop=(h == 15)))
                                sch.op("T", lambda t, h=hc, o=oc, bi=bkg:
                                       t.matmul(
                                           pu[bi][:, :],
                                           wu[o % 4][:, h * 128:h * 128 + 128],
                                           x2f[:, h * 512:h * 512 + 512],
                                           start=(h == 0), stop=(h == 15)))
                            gmmk[oc] = gmm = sch.mark("T")
                            sch.wait("S", gmm)
                            if oc >= 4:
                                sch.wait("S", ev[oc - 4])
                            sch.op("S", lambda s, bi=bkg, o=oc:
                                   s.activation(sil[o % 4][:, :], pg[bi][:, :],
                                                AF.Silu, bias=0.0, scale=1.0))
                            smk = sch.mark("S")
                            sch.wait("V", smk)
                            sch.op("V", lambda v, o=oc, bi=bkg:
                                   v.tensor_mul(act[:, o * 512:o * 512 + 512],
                                                sil[o % 4][:, :], pu[bi][:, :]))
                            ev[oc] = sch.mark("V")
                        sch.barrier()
                    with contextlib.ExitStack() as pc:
                        pd = [pc.enter_context(
                            nc.psum_tensor(f"pdE{g}{i}", [128, 512], F32))
                            for i in range(4)]
                        ev, wtokd, dmmk = {}, {}, {}
                        for hcp in range(8):
                            bks = (hcp % 2) * 2
                            if hcp >= 2:
                                sch.wait("T", ev[hcp - 2])
                            for ic in range(64):
                                if ic % 8 == 0:
                                    icq = ic // 8
                                    prevq = (hcp, icq - 2) if icq >= 2 else \
                                            ((hcp - 1, 6 + icq) if hcp >= 1 else None)
                                    if prevq is not None:
                                        sch.wait("Y", dmmk.get(prevq))
                                    wtokd[(hcp, icq)] = sch.dma(
                                        "Y", wd[icq % 2][:, :],
                                        AP(wbuf_d, icq * 8 * D + hcp * 256,
                                           [[64 * D, 128], [D, 8], [1, 256]]),
                                        f"ed{icq % 2}")
                                    sch.wait("T", wtokd[(hcp, icq)])
                                for o2 in range(2):
                                    sch.op("T", lambda t, i_=ic, o=o2, bi=bks:
                                           t.matmul(
                                               pd[bi + o][:, :],
                                               wd[(i_ // 8) % 2][:, (i_ % 8) * 256 + o * 128:
                                                            (i_ % 8) * 256 + o * 128 + 128],
                                               act[:, i_ * 512:i_ * 512 + 512],
                                               start=(i_ == 0), stop=(i_ == 63)))
                                if ic % 8 == 7:
                                    dmmk[(hcp, ic // 8)] = sch.mark("T")
                            dmm = sch.mark("T")
                            sch.wait("S", dmm)
                            if hcp >= 2:
                                sch.wait("S", ev[hcp - 2])
                            for o2 in range(2):
                                sch.op("S", lambda s, h=hcp, o=o2, bi=bks:
                                       s.copy(y2f[:, (2 * h + o) * 512:
                                                  (2 * h + o) * 512 + 512],
                                              pd[bi + o][:, :]))
                            ev[hcp] = sch.mark("S")
                        sch.barrier()
                    with contextlib.ExitStack() as pc:
                        ptb = [pc.enter_context(
                            nc.psum_tensor(f"ptbE{g}{i}", [128, 1024], BF16))
                            for i in range(2)]
                        prev = {}
                        ost = {}
                        for tl in range(4):
                            to = g * 4 + tl
                            if tl >= 2:
                                sch.wait("Y", ost[tl - 2])
                            xtok = sch.dma("Y", xtie[tl % 2][:, :],
                                           xmid[to * 128:(to + 1) * 128, :],
                                           f"e4x{tl % 2}")
                            for ocx in range(16):
                                p = ptb[ocx // 8]
                                if ocx % 8 == 0 and tl >= 1:
                                    sch.wait("T", prev.get((tl - 1, ocx // 8)))
                                sch.op("T", lambda t, p_=p, o=ocx, l=tl:
                                       t.matmul(p_[:, (o % 8) * 128:(o % 8) * 128 + 128],
                                                y2f[:, o * 512 + l * 128:
                                                    o * 512 + l * 128 + 128],
                                                ident[:, :], is_transpose=True,
                                                start=(o % 8 == 0), stop=(o % 8 == 7)))
                                if ocx % 8 == 7:
                                    tmk = sch.mark("T")
                                    sch.wait("S", tmk)
                                    if tl >= 1:
                                        sch.wait("S", prev[tl - 1])
                                    sch.op("S", lambda s, p_=p, gg=ocx // 8:
                                           s.copy(ytoke[:, gg * 1024:(gg + 1) * 1024],
                                                  p_[:, :]))
                                    prev[(tl, ocx // 8)] = sch.mark("S")
                            sch.wait("V", prev[(tl, 0)])
                            sch.wait("V", prev[(tl, 1)])
                            _emit_ln(sch, col, 8 * (tl % 2), ytoke[:, :], lnsq,
                                     out_f32=lnoe)
                            sch.wait("V", xtok)
                            sch.op("V", lambda v, l=tl:
                                   v.tensor_add(xtie[l % 2][:, :],
                                                xtie[l % 2][:, :], lnoe[:, :]))
                            fmk = sch.mark("V")
                            prev[tl] = fmk
                            sch.wait("Y", fmk)
                            ost[tl] = sch.dma(
                                "Y", out_d[to * 128:(to + 1) * 128, :],
                                xtie[tl % 2][:, :], f"out{tl % 2}")

        sch.finish("Y")
        sch.emit(nc, ctx)

    return nc


_TPR = {}


def _perm_gu():
    if "gu" not in _TPR:
        p = np.arange(GCH).reshape(GCH // 16, 16).T.ravel()
        _TPR["gu"] = (np.arange(NBLK).reshape(-1, GCH)[:, p]).ravel()
    return _TPR["gu"]


def _perm_dn():
    if "dn" not in _TPR:
        p = np.arange(DCH).reshape(DCH // 64, 64).T.ravel()
        _TPR["dn"] = (np.arange(NBLK).reshape(-1, DCH)[:, p]).ravel()
    return _TPR["dn"]


def _tau_tiled():
    cst = _consts()
    tp = cst["taupow"]
    return np.concatenate([np.tile(tp[0:4], (4, 1)), np.tile(tp[4:8], (4, 1))])


def _tpr0():
    if 0 not in _TPR:
        _TPR[0] = np.ascontiguousarray(_tau_tiled()[:, _perm_gu()])
    return _TPR[0]


def _tpr1():
    if 1 not in _TPR:
        _TPR[1] = np.ascontiguousarray(_tau_tiled()[:, _perm_dn()])
    return _TPR[1]


def _get_program():
    global _PROG
    if _PROG is None:
        _PROG = build_program()
    return _PROG


def kernel(L0_attn_x, L0_attn_w_qkv, L0_attn_b_qkv, L0_attn_w_out,
           L0_attn_b_out, L0_ffn_gate_cp, L0_ffn_up_cp, L0_ffn_down_cp):
    cst = _consts()
    nc = _get_program()

    x = np.ascontiguousarray(np.asarray(L0_attn_x, dtype=np.float32))
    wqkvT = np.ascontiguousarray(np.asarray(L0_attn_w_qkv, np.float32).T)
    woutT = np.ascontiguousarray(np.asarray(L0_attn_w_out, np.float32).T)
    bqkv = np.ascontiguousarray(
        np.asarray(L0_attn_b_qkv, np.float32).reshape(48, 128).T)
    bout_rep = np.ascontiguousarray(
        np.broadcast_to(np.asarray(L0_attn_b_out, np.float32)[None, :],
                        (128, D)))
    cps = [np.asarray(c, np.float32) for c in
           (L0_ffn_gate_cp, L0_ffn_up_cp, L0_ffn_down_cp)]
    taps = [c[cst["tapidx"]] for c in cps]
    tap_gu = np.ascontiguousarray(
        np.concatenate([taps[0], taps[1]])[:, _perm_gu()])
    tap_dn = np.ascontiguousarray(taps[2][:, _perm_dn()])
    wgm_gu = np.ascontiguousarray(cst["wgmask"][:, _perm_gu()])
    wgm_dn = np.ascontiguousarray(cst["wgmask"][:, _perm_dn()])

    in_maps = []
    for c in range(NCORES):
        b, j = c // 2, c % 2
        gk = [j, 1 - j, 2 + j, 3 - j]
        xr = np.ascontiguousarray(x[b].reshape(4, 512, D)[gk].reshape(S, D))
        in_maps.append(dict(
            x_row=xr, wqkvT=wqkvT, bqkv=bqkv, woutT=woutT,
            bout_rep=bout_rep, tap_gu=tap_gu, tap_dn=tap_dn,
            tau_gu=_tpr0(), tau_dn=_tpr1(),
            atilde=cst["atilde"], wgmask_gu=wgm_gu, wgmask_dn=wgm_dn,
            cmask=np.ascontiguousarray(cst["cmask"][j]),
            ident=cst["ident"], ones=cst["ones"],
        ))

    res = run_bass_kernel_spmd(nc, in_maps, core_ids=list(range(NCORES)))

    out = np.empty((B, S, D), dtype=np.float32)
    for c in range(NCORES):
        b, j = c // 2, c % 2
        o = res.results[c]["out"]
        out[b, j * 512:(j + 1) * 512] = o[0:512]
        out[b, (2 + j) * 512:(3 + j) * 512] = o[512:1024]
    return out


# version 22
# speedup vs baseline: 1.3739x; 1.0168x over previous
import contextlib
import math

import numpy as np
import ml_dtypes

import concourse.bass as bass
import concourse.mybir as mybir
from concourse.bass_utils import run_bass_kernel_spmd

F32 = mybir.dt.float32
BF16 = mybir.dt.bfloat16
AF = mybir.ActivationFunctionType
ALU = mybir.AluOpType
AX = mybir.AxisListType
AP = bass.AP

B, S, D = 4, 2048, 2048
H, HD = 16, 128
INTER = 8192
NCTRL = 130156
NSAMP = INTER * D
NBLK = NSAMP // 128
GCH = 2048
NGCH = NBLK // GCH
DCH = 4096
NDCH = NBLK // DCH
TPC = 1024
SCALE = 1.0 / math.sqrt(HD)
INV_D = 1.0 / D
EPS = 1e-5
NCORES = 8
CTX = (8, 16)


_CONSTS = None


def _consts():
    global _CONSTS
    if _CONSTS is not None:
        return _CONSTS
    t = np.linspace(0.0, NCTRL - 1.0, NSAMP, dtype=np.float64)
    iarr = np.clip(np.floor(t).astype(np.int64), 0, NCTRL - 2)
    i_b = iarr[::128]
    tau = t[::128] - i_b
    s_b = (iarr.reshape(NBLK, 128) == i_b[:, None]).sum(1)
    R = (NCTRL - 1.0) / (NSAMP - 1)

    Cc = np.array([
        [0.0, -0.5, 1.0, -0.5],
        [1.0, 0.0, -2.5, 1.5],
        [0.0, 0.5, 2.0, -1.5],
        [0.0, 0.0, -0.5, 0.5]], dtype=np.float64)
    from math import comb
    qR = np.arange(128, dtype=np.float64) * R
    at = np.zeros((16, 128), dtype=np.float64)
    for k in range(4):
        for m in range(4):
            v = np.zeros_like(qR)
            for jj in range(m, 4):
                v += comb(jj, m) * Cc[k, jj] * qR ** (jj - m)
            at[4 * k + m] = v
    tp = np.zeros((8, NBLK), dtype=np.float64)
    for m in range(4):
        tp[m] = tau ** m
        tp[4 + m] = (tau - 1.0) ** m
    mask = (np.arange(128)[:, None] < s_b[None, :])
    tapidx = np.stack([np.clip(i_b + r - 1, 0, NCTRL - 1) for r in range(5)])

    cmask = {}
    for j in range(2):
        gk = [j, 1 - j, 2 + j, 3 - j]
        cm = np.zeros((16, 128, 512), dtype=np.float64)
        for cr in range(16):
            tslot = 0 if cr < 8 else 1
            gt = 2 * tslot + j
            tk = gk[cr // 4] * 512 + (cr % 4) * 128 + np.arange(128)[:, None]
            tq = gt * 512 + np.arange(512)[None, :]
            cm[cr] = (tk <= tq)
        cmask[j] = cm.astype(ml_dtypes.bfloat16)

    r0_tap = np.zeros(128, dtype=np.int64)
    r0_wt = np.zeros(128, dtype=np.int64)
    r0_tau = np.zeros(128, dtype=np.int64)
    for p in range(128):
        si, r32 = p // 32, p % 32
        k, m = (r32 % 16) // 4, r32 % 4
        r0_wt[p] = si // 2
        r0_tap[p] = k + (si % 2)
        r0_tau[p] = (si % 2) * 4 + m
    r1_tap = np.zeros(64, dtype=np.int64)
    r1_tau = np.zeros(64, dtype=np.int64)
    for p in range(64):
        si, r32 = p // 32, p % 32
        k, m = (r32 % 16) // 4, r32 % 4
        r1_tap[p] = k + si
        r1_tau[p] = si * 4 + m
    _CONSTS = dict(
        tapidx=tapidx, s_b=s_b,
        r0_tap=r0_tap, r0_wt=r0_wt, r0_tau=r0_tau,
        r1_tap=r1_tap, r1_tau=r1_tau,
        atilde=at.astype(np.float32),
        taupow=tp.astype(np.float32),
        wgmask=mask.astype(np.uint8),
        ident=np.eye(128, dtype=ml_dtypes.bfloat16),
        ones=np.ones((128, 128), dtype=ml_dtypes.bfloat16),
        cmask=cmask,
    )
    return _CONSTS


class Sched:
    ENGINES = ("T", "V", "S", "G", "Y")

    def __init__(self):
        self.recs = {e: [] for e in self.ENGINES}
        self.cnt = {e.lower(): 0 for e in self.ENGINES}
        self.pend = {e: [] for e in self.ENGINES}
        self.streams = {}
        self.issuer = {}
        self.open_streams = {e: set() for e in self.ENGINES}

    CHAINED = ("V", "S", "G")

    def op(self, e, fn):
        w = list(self.pend[e])
        self.pend[e] = []
        incs = []
        if e in self.CHAINED:
            sem = e.lower()
            if self.cnt[sem]:
                w.append((sem, self.cnt[sem]))
            self.cnt[sem] += 1
            incs.append((sem, 1, self.cnt[sem]))
        self.recs[e].append([fn, tuple(w), incs, False])

    def mark(self, e):
        rec = self.recs[e][-1]
        if rec[3]:
            sem, _n, tgt = rec[2][0]
            return (sem, tgt)
        if rec[2]:
            sem, _n, tgt = rec[2][0]
            return (sem, tgt)
        sem = e.lower()
        self.cnt[sem] += 1
        rec[2].append((sem, 1, self.cnt[sem]))
        return (sem, self.cnt[sem])

    def dma(self, e, out, in_, stream):
        sname = "q_" + stream
        assert self.issuer.setdefault(sname, e) == e, (sname, e)
        self.streams[sname] = self.streams.get(sname, 0) + 16
        tgt = self.streams[sname]
        self.open_streams[e].add(sname)
        w = tuple(self.pend[e])
        self.pend[e] = []
        self.recs[e].append(
            [lambda eng, o=out, i=in_: eng.dma_start(out=o, in_=i), w,
             [(sname, 16, tgt)], True])
        return (sname, tgt)

    def wait(self, e, tok):
        if tok is not None:
            self.pend[e].append(tok)

    def barrier(self):
        toks = []
        for e in self.ENGINES:
            if not self.recs[e] and not self.open_streams[e]:
                continue
            if self.open_streams[e]:
                for sname in sorted(self.open_streams[e]):
                    self.pend[e].append((sname, self.streams[sname]))
                self.open_streams[e] = set()
                self.op(e, lambda eng: eng.nop())
            toks.append(self.mark(e))
        for e in self.ENGINES:
            for t in toks:
                self.pend[e].append(t)

    def finish(self, e):
        for sname in sorted(self.open_streams[e]):
            self.pend[e].append((sname, self.streams[sname]))
        self.op(e, lambda eng: eng.nop())

    def emit(self, nc, ctx):
        sems = {}
        for sn in self.cnt:
            sems[sn] = ctx.enter_context(nc.semaphore("sem_" + sn))
        for sn in self.streams:
            sems[sn] = ctx.enter_context(nc.semaphore(sn))

        def replay(e):
            def f(eng):
                for fn, waits, incs, _isdma in self.recs[e]:
                    agg = {}
                    for (sn, v) in waits:
                        agg[sn] = max(agg.get(sn, 0), v)
                    for sn, v in agg.items():
                        eng.wait_ge(sems[sn], v)
                    inst = fn(eng)
                    for (sn, n, _tgt) in incs:
                        inst.then_inc(sems[sn], n)
            return f
        with nc.Block() as blk:
            blk.tensor(replay("T"))
            blk.vector(replay("V"))
            blk.scalar(replay("S"))
            blk.gpsimd(replay("G"))
            blk.sync(replay("Y"))


def _emit_sumav(sch, it, c, C, hb, tslot, pT, vT, prow, pav, ones, expmark,
                rmk, attn_emk):
    sch.wait("T", expmark[(it, c)])
    if c == 0:
        sch.wait("T", rmk.get(it - 1))
        sch.wait("T", attn_emk.get(it - 2))
    sch.op("T", lambda t, b=tslot, c_=c, C_=C:
           t.matmul(prow[:, :], ones[:, 0:1],
                    pT[b][:, c_ * 512:c_ * 512 + 512],
                    start=(c_ == 0), stop=(c_ == C_ - 1)))
    sch.op("T", lambda t, b=hb, bb=tslot, c_=c, C_=C, pa=pav[it % 2]:
           t.matmul(pa[:, :], vT[b][:, c_ * 128:c_ * 128 + 128],
                    pT[bb][:, c_ * 512:c_ * 512 + 512],
                    start=(c_ == 0), stop=(c_ == C_ - 1)))


def _own_tile(to):
    return to if to < 4 else to + 4


def _emit_ln(sch, col, c0, xap, sqscratch, out_bf=None, out_f32=None):
    sch.op("V", lambda v: v.reduce_sum(col[:, c0:c0 + 1], xap, axis=AX.X))
    sch.op("S", lambda s: s.activation(sqscratch[:, :], xap, AF.Square,
                                       bias=0.0, scale=1.0,
                                       accum_out=col[:, c0 + 1:c0 + 2]))
    sqm = sch.mark("S")
    sch.op("V", lambda v: v.tensor_scalar(col[:, c0 + 2:c0 + 3],
                                          col[:, c0:c0 + 1], INV_D, None,
                                          ALU.mult))
    sch.op("V", lambda v: v.tensor_mul(col[:, c0 + 3:c0 + 4],
                                       col[:, c0 + 2:c0 + 3],
                                       col[:, c0 + 2:c0 + 3]))
    sch.wait("V", sqm)
    sch.op("V", lambda v: v.tensor_scalar(col[:, c0 + 4:c0 + 5],
                                          col[:, c0 + 1:c0 + 2], INV_D,
                                          col[:, c0 + 3:c0 + 4],
                                          ALU.mult, ALU.subtract))
    vmk = sch.mark("V")
    sch.wait("S", vmk)
    sch.op("S", lambda s: s.activation(col[:, c0 + 5:c0 + 6],
                                       col[:, c0 + 4:c0 + 5], AF.Sqrt,
                                       bias=EPS, scale=1.0))
    smk = sch.mark("S")
    sch.wait("V", smk)
    sch.op("V", lambda v: v.reciprocal(col[:, c0 + 6:c0 + 7],
                                       col[:, c0 + 5:c0 + 6]))
    sch.op("V", lambda v: v.tensor_mul(col[:, c0 + 7:c0 + 8],
                                       col[:, c0 + 2:c0 + 3],
                                       col[:, c0 + 6:c0 + 7]))
    if out_bf is not None:
        sch.op("V", lambda v: v.tensor_scalar(out_bf[:, :], xap,
                                              col[:, c0 + 6:c0 + 7],
                                              col[:, c0 + 7:c0 + 8],
                                              ALU.mult, ALU.subtract))
    if out_f32 is not None:
        sch.op("V", lambda v: v.tensor_scalar(out_f32[:, :], xap,
                                              col[:, c0 + 6:c0 + 7],
                                              col[:, c0 + 7:c0 + 8],
                                              ALU.mult, ALU.subtract))


_PROG = None
PHASES = ("W", "B", "C", "D", "E")


def build_program():
    nc = bass.Bass("TRN2", target_bir_lowering=False)
    sch = Sched()

    x_row = nc.dram_tensor("x_row", [S, D], F32, kind="ExternalInput")
    wqkvT = nc.dram_tensor("wqkvT", [D, 3 * D], F32, kind="ExternalInput")
    bqkv_d = nc.dram_tensor("bqkv", [128, 48], F32, kind="ExternalInput")
    woutT = nc.dram_tensor("woutT", [D, D], F32, kind="ExternalInput")
    bout_d = nc.dram_tensor("bout_rep", [128, D], F32, kind="ExternalInput")
    tap_gu_d = nc.dram_tensor("tap_gu", [10, NBLK], F32, kind="ExternalInput")
    tau_gu_d = nc.dram_tensor("tau_gu", [32, NBLK], F32, kind="ExternalInput")
    tap_dn_d = nc.dram_tensor("tap_dn", [5, NBLK], F32, kind="ExternalInput")
    tau_dn_d = nc.dram_tensor("tau_dn", [32, NBLK], F32, kind="ExternalInput")
    at_d = nc.dram_tensor("atilde", [16, 128], F32, kind="ExternalInput")
    wgm_gu_d = nc.dram_tensor("wgmask_gu", [128, NBLK], mybir.dt.uint8,
                              kind="ExternalInput")
    wgm_dn_d = nc.dram_tensor("wgmask_dn", [128, NBLK], mybir.dt.uint8,
                              kind="ExternalInput")
    cm_d = nc.dram_tensor("cmask", [16, 128, 512], BF16, kind="ExternalInput")
    id_d = nc.dram_tensor("ident", [128, 128], BF16, kind="ExternalInput")
    on_d = nc.dram_tensor("ones", [128, 128], BF16, kind="ExternalInput")
    out_d = nc.dram_tensor("out", [TPC, D], F32, kind="ExternalOutput")

    wbuf_g = nc.dram_tensor("wbuf_g", [128, NBLK], BF16)
    wbuf_u = nc.dram_tensor("wbuf_u", [128, NBLK], BF16)
    wbuf_d = nc.dram_tensor("wbuf_d", [128, 64, D], BF16)
    kvbuf = nc.dram_tensor("kvbuf", [2 * H, 128, S], BF16)
    xmid = nc.dram_tensor("xmid", [TPC, D], F32)

    ctx = contextlib.ExitStack()
    with ctx:
        csb = ctx.enter_context(nc.sbuf_tensor("constsb", [128, 4], F32))
        for ci, cv in enumerate((0.0, EPS, 1.0, SCALE)):
            nc.const_aps.register(F32, float(cv), csb[:, ci:ci + 1])
            sch.op("V", lambda v, i=ci, val=float(cv):
                   v.memset(csb[:, i:i + 1], val))
        ident = ctx.enter_context(nc.sbuf_tensor("identsb", [128, 128], BF16))
        ones = ctx.enter_context(nc.sbuf_tensor("onessb", [128, 128], BF16))
        at_sb = ctx.enter_context(nc.sbuf_tensor("atsb", [128, 128], BF16))
        bqkv = ctx.enter_context(nc.sbuf_tensor("bqkvsb", [128, 48], F32))
        col = ctx.enter_context(nc.sbuf_tensor("colsb", [128, 16], F32))
        lnsq = ctx.enter_context(nc.sbuf_tensor("lnsqsb", [128, D], BF16))
        lnsq2 = ctx.enter_context(nc.sbuf_tensor("lnsqsb2", [128, D], BF16))
        lnsqs = [lnsq, lnsq2]

        sch.dma("Y", ident[:, :], id_d[:, :], "c0")
        sch.dma("Y", ones[:, :], on_d[:, :], "c0")
        sch.dma("Y", bqkv[:, :], bqkv_d[:, :], "c0")
        for u in (0, 32, 64, 96):
            sch.dma("G", at_sb[u:u + 16, :], at_d[:, :], "c1")
        sch.barrier()

        def make_gu_gen(wc, pA, pB, depth):
            CT = [wc.enter_context(nc.sbuf_tensor(f"gct{i}", [42, GCH], BF16))
                  for i in range(3)]
            U = [wc.enter_context(nc.sbuf_tensor(f"Ug{i}", [128, GCH], BF16))
                 for i in range(2)]
            IN0 = [wc.enter_context(nc.sbuf_tensor(f"gi0_{i}", [128, GCH], BF16))
                   for i in range(2)]
            IN1 = [wc.enter_context(nc.sbuf_tensor(f"gi1_{i}", [128, GCH], BF16))
                   for i in range(2)]
            MSK = [wc.enter_context(
                nc.sbuf_tensor(f"gm_{i}", [128, GCH], mybir.dt.uint8))
                for i in range(4)]
            WO = [wc.enter_context(nc.sbuf_tensor(f"gwo_{i}", [128, GCH], BF16))
                  for i in range(4)]
            streams = [(wbuf_g, 0, 32), (wbuf_u, 64, 96)]
            for t in IN0 + IN1:
                sch.op("V", lambda v, t_=t: v.memset(t_[:, :], 0.0))
            msmk_gu = sch.mark("V")

            def ldtoks(c):
                sch.dma("G", CT[c % 3][0:10, :],
                        AP(tap_gu_d, c * GCH, [[NBLK, 10], [1, GCH]]),
                        f"wg{c % 3}")
                return sch.dma("G", CT[c % 3][10:42, :],
                               AP(tau_gu_d, c * GCH, [[NBLK, 32], [1, GCH]]),
                               f"wg{c % 3}")

            def gen():
                umark, evmark, stok, mtoks = {}, {}, {}, {}
                ltok, etok = {}, {}

                def expand(c):
                    sch.wait("G", ltok[c])
                    if c >= 2:
                        sch.wait("G", umark[c - 2])
                    else:
                        sch.wait("G", msmk_gu)
                    for si in range(4):
                        wt, dlt = si // 2, si % 2
                        sch.dma("G", AP(IN0[c % 2], (si * 32) * GCH,
                                        [[GCH, 16], [1, GCH]]),
                                AP(CT[c % 3], (wt * 5 + dlt) * GCH,
                                   [[GCH, 4], [0, 4], [1, GCH]]),
                                f"we{c % 2}")
                        etok[c] = sch.dma(
                            "G", AP(IN1[c % 2], (si * 32) * GCH,
                                    [[GCH, 16], [1, GCH]]),
                            AP(CT[c % 3], (10 + dlt * 16) * GCH,
                               [[GCH, 16], [1, GCH]]),
                            f"we{c % 2}")

                def umult(c):
                    sch.wait("V", etok[c])
                    if c >= 2:
                        sch.wait("V", evmark[(c - 2) * 8 + 7])
                    sch.op("V", lambda v, b=c % 2: v.tensor_mul(
                        U[b][:, :], IN0[b][:, :], IN1[b][:, :]))
                    umark[c] = sch.mark("V")

                for c in range(NGCH):
                    yield
                    buf = c % 2
                    if c == 0:
                        for cpre in (0, 1):
                            mtoks[cpre] = sch.dma(
                                "G", MSK[cpre % 4][:, :],
                                wgm_gu_d[:, cpre * GCH:(cpre + 1) * GCH],
                                f"gm{cpre % 4}")
                        ltok[0] = ldtoks(0)
                        ltok[1] = ldtoks(1)
                        expand(0)
                        umult(0)
                    if c + 2 < NGCH:
                        if c >= 1:
                            sch.wait("G", etok[c - 1])
                        ltok[c + 2] = ldtoks(c + 2)
                    sch.wait("T", umark[c])
                    for w in range(2):
                        wb, uA, uB = streams[w]
                        cw = c * 2 + w
                        first_ev = True
                        for q4 in range(4):
                            yield
                            i2 = c * 8 + w * 4 + q4
                            bk = i2 % depth
                            if i2 >= depth:
                                sch.wait("T", evmark[i2 - depth])
                            for u, po in ((uA, pA[bk]), (uB, pB[bk])):
                                sch.op("T", lambda t, u_=u, b=buf, pp=po,
                                       cc=q4 * 512:
                                       t.matmul(pp[:, :],
                                                at_sb[u_:u_ + 16, :],
                                                U[b][u_:u_ + 16, cc:cc + 512],
                                                start=True, stop=True,
                                                tile_position=(u_, 0)))
                            mm = sch.mark("T")
                            sch.wait("S", mm)
                            if first_ev:
                                if cw >= 4:
                                    sch.wait("S", stok[cw - 4])
                                if w == 0:
                                    sch.wait("V", mtoks[c])
                                first_ev = False
                            wo_ap = WO[cw % 4][:, q4 * 512:(q4 + 1) * 512]
                            mk_ap = MSK[c % 4][:, q4 * 512:(q4 + 1) * 512]
                            sch.op("S", lambda s, o_=wo_ap, pp=pB[bk]:
                                   s.copy(o_, pp[:, :]))
                            smk = sch.mark("S")
                            sch.wait("V", smk)
                            sch.op("V", lambda v, o_=wo_ap, m_=mk_ap,
                                   pp=pA[bk]:
                                   v.copy_predicated(o_, m_, pp[:, :]))
                            evmark[i2] = sch.mark("V")
                        sch.wait("Y", evmark[c * 8 + w * 4 + 3])
                        stok[cw] = sch.dma(
                            "Y", wb[:, c * GCH:(c + 1) * GCH],
                            WO[cw % 4][:, :],
                            f"ws{cw % 4}")
                        if w == 0 and c + 1 < NGCH:
                            expand(c + 1)
                    if c + 2 < NGCH:
                        if c >= 2:
                            sch.wait("G", evmark[(c - 2) * 8 + 7])
                        mtoks[c + 2] = sch.dma(
                            "G", MSK[(c + 2) % 4][:, :],
                            wgm_gu_d[:, (c + 2) * GCH:(c + 3) * GCH],
                            f"gm{(c + 2) % 4}")
                    if c + 1 < NGCH:
                        umult(c + 1)
                while True:
                    yield
            return gen()

        def make_down_gen(wc, pA, pB, depth):
            CT = [wc.enter_context(nc.sbuf_tensor(f"dct{i}", [37, DCH], BF16))
                  for i in range(3)]
            Ud = [wc.enter_context(nc.sbuf_tensor(f"Ud{i}", [64, DCH], BF16))
                  for i in range(2)]
            IN0 = [wc.enter_context(nc.sbuf_tensor(f"di0_{i}", [64, DCH], BF16))
                   for i in range(2)]
            IN1 = [wc.enter_context(nc.sbuf_tensor(f"di1_{i}", [64, DCH], BF16))
                   for i in range(2)]
            MSK = [wc.enter_context(
                nc.sbuf_tensor(f"dm_{i}", [128, DCH], mybir.dt.uint8))
                for i in range(4)]
            WO = [wc.enter_context(nc.sbuf_tensor(f"dwo_{i}", [128, DCH], BF16))
                  for i in range(4)]
            NCG = DCH // 512
            for t in IN0 + IN1:
                sch.op("V", lambda v, t_=t: v.memset(t_[:, :], 0.0))
            msmk_dn = sch.mark("V")

            def ldtoks(c):
                sch.dma("G", CT[c % 3][0:5, :],
                        AP(tap_dn_d, c * DCH, [[NBLK, 5], [1, DCH]]),
                        f"wd{c % 3}")
                return sch.dma("G", CT[c % 3][5:37, :],
                               AP(tau_dn_d, c * DCH, [[NBLK, 32], [1, DCH]]),
                               f"wd{c % 3}")

            def gen():
                umark, evmark, stok, mtoks = {}, {}, {}, {}
                ltok, etok = {}, {}

                def expand(c):
                    sch.wait("G", ltok[c])
                    if c >= 2:
                        sch.wait("G", umark[c - 2])
                    else:
                        sch.wait("G", msmk_dn)
                    for si in range(2):
                        sch.dma("G", AP(IN0[c % 2], (si * 32) * DCH,
                                        [[DCH, 16], [1, DCH]]),
                                AP(CT[c % 3], si * DCH,
                                   [[DCH, 4], [0, 4], [1, DCH]]),
                                f"de{c % 2}")
                        etok[c] = sch.dma(
                            "G", AP(IN1[c % 2], (si * 32) * DCH,
                                    [[DCH, 16], [1, DCH]]),
                            AP(CT[c % 3], (5 + si * 16) * DCH,
                               [[DCH, 16], [1, DCH]]),
                            f"de{c % 2}")

                def umult(c):
                    sch.wait("V", etok[c])
                    if c >= 2:
                        sch.wait("V", evmark[(c - 2) * NCG + NCG - 1])
                    sch.op("V", lambda v, b=c % 2: v.tensor_mul(
                        Ud[b][:, :], IN0[b][:, :], IN1[b][:, :]))
                    umark[c] = sch.mark("V")

                for c in range(NDCH):
                    yield
                    buf = c % 2
                    if c == 0:
                        for cpre in (0, 1):
                            mtoks[cpre] = sch.dma(
                                "G", MSK[cpre % 4][:, :],
                                wgm_dn_d[:, cpre * DCH:(cpre + 1) * DCH],
                                f"dm{cpre % 4}")
                        ltok[0] = ldtoks(0)
                        ltok[1] = ldtoks(1)
                        expand(0)
                        umult(0)
                    if c + 2 < NDCH:
                        if c >= 1:
                            sch.wait("G", etok[c - 1])
                        ltok[c + 2] = ldtoks(c + 2)
                    sch.wait("T", umark[c])
                    first_ev = True
                    for cg in range(NCG):
                        yield
                        i2 = c * NCG + cg
                        bk = i2 % depth
                        if i2 >= depth:
                            sch.wait("T", evmark[i2 - depth])
                        for u, po in ((0, pA[bk]), (32, pB[bk])):
                            sch.op("T", lambda t, u_=u, b=buf, pp=po,
                                   cc=cg * 512:
                                   t.matmul(pp[:, :], at_sb[u_:u_ + 16, :],
                                            Ud[b][u_:u_ + 16, cc:cc + 512],
                                            start=True, stop=True,
                                            tile_position=(u_, 0)))
                        mm = sch.mark("T")
                        sch.wait("S", mm)
                        if first_ev:
                            if c >= 4:
                                sch.wait("S", stok[c - 4])
                            sch.wait("V", mtoks[c])
                            first_ev = False
                        wo_ap = WO[c % 4][:, cg * 512:(cg + 1) * 512]
                        mk_ap = MSK[c % 4][:, cg * 512:(cg + 1) * 512]
                        sch.op("S", lambda s, o_=wo_ap, pp=pB[bk]:
                               s.copy(o_, pp[:, :]))
                        smk = sch.mark("S")
                        sch.wait("V", smk)
                        sch.op("V", lambda v, o_=wo_ap, m_=mk_ap, pp=pA[bk]:
                               v.copy_predicated(o_, m_, pp[:, :]))
                        evmark[i2] = sch.mark("V")
                        if cg == NCG // 2 and c + 1 < NDCH:
                            expand(c + 1)
                    sch.wait("Y", evmark[c * NCG + NCG - 1])
                    stok[c] = sch.dma(
                        "Y",
                        AP(wbuf_d, c * DCH, [[64 * D, 128], [1, DCH]]),
                        WO[c % 4][:, :],
                        f"dws{c % 4}")
                    if c + 2 < NDCH:
                        if c >= 2:
                            sch.wait("G", evmark[(c - 2) * NCG + NCG - 1])
                        mtoks[c + 2] = sch.dma(
                            "G", MSK[(c + 2) % 4][:, :],
                            wgm_dn_d[:, (c + 2) * DCH:(c + 3) * DCH],
                            f"dm{(c + 2) % 4}")
                    if c + 1 < NDCH:
                        umult(c + 1)
                while True:
                    yield
            return gen()

        qscope = ctx.enter_context(contextlib.ExitStack())
        q_all = qscope.enter_context(
            nc.sbuf_tensor("qall", [128, H * TPC], BF16))
        if "W" in PHASES:
            with contextlib.ExitStack() as wsc:
                pguA = [wsc.enter_context(
                    nc.psum_tensor(f"pguA{i}", [128, 512], F32))
                    for i in range(4)]
                pguB = [wsc.enter_context(
                    nc.psum_tensor(f"pguB{i}", [128, 512], F32))
                    for i in range(4)]
                gugen = make_gu_gen(wsc, pguA, pguB, depth=4)
                for _ in range(NGCH * 9 + 20):
                    next(gugen)
            sch.barrier()

        if "B" in PHASES:
            with contextlib.ExitStack() as bc:

                x2T = bc.enter_context(nc.sbuf_tensor("x2T", [128, 16 * S], BF16))
                xt = [bc.enter_context(nc.sbuf_tensor(f"xt{i}", [128, D], F32))
                      for i in range(2)]
                wq = [bc.enter_context(nc.sbuf_tensor(f"wq{i}", [128, 4096], BF16))
                      for i in range(2)]
                kst = [bc.enter_context(nc.sbuf_tensor(f"kst{i}", [128, 512], BF16))
                       for i in range(4)]
                with contextlib.ExitStack() as tc:
                    ptr = [tc.enter_context(
                        nc.psum_tensor(f"ptr{i}", [128, 1024], BF16))
                        for i in range(2)]
                    ln_ev, nmks, trmk = {}, {}, {}

                    def b_transpose(ti):
                        for hc in range(16):
                            p = ptr[hc // 8]
                            if hc % 8 == 0:
                                sch.wait("T", nmks[ti])
                                if ti >= 1:
                                    sch.wait("T", ln_ev.get((ti - 1, hc // 8)))
                            sch.op("T", lambda t, p_=p, h=hc, b=ti % 2:
                                   t.matmul(p_[:, (h % 8) * 128:(h % 8) * 128 + 128],
                                            lnsqs[b][:, h * 128:h * 128 + 128],
                                            ident[:, :], is_transpose=True,
                                            start=(h % 8 == 0), stop=(h % 8 == 7)))
                            if hc % 8 == 7:
                                tmk = sch.mark("T")
                                trmk[ti] = tmk
                                sch.wait("V", tmk)
                                sch.op("V", lambda v, p_=p, t_=ti, g=hc // 8:
                                       v.tensor_copy(
                                           AP(x2T, (g * 8) * S + t_ * 128,
                                              [[16 * S, 128], [S, 8], [1, 128]]),
                                           p_[:, :]))
                                ln_ev[(ti, hc // 8)] = sch.mark("V")

                    for ti in range(16):
                        bufp = ti % 2
                        if ti >= 2:
                            sch.wait("Y", nmks[ti - 2])
                        xtok = sch.dma("Y", xt[bufp][:, :],
                                       x_row[ti * 128:(ti + 1) * 128, :],
                                       f"bx{ti % 2}")
                        sch.wait("V", xtok)
                        sch.wait("S", xtok)
                        if ti >= 2:
                            sch.wait("S", trmk[ti - 2])
                            sch.wait("V", trmk[ti - 2])
                        _emit_ln(sch, col, 8 * bufp, xt[bufp][:, :],
                                 lnsqs[bufp], out_bf=lnsqs[bufp])
                        nmks[ti] = sch.mark("V")
                        if ti >= 1:
                            b_transpose(ti - 1)
                    b_transpose(15)
                    sch.barrier()
                with contextlib.ExitStack() as qc:
                    pq = [qc.enter_context(
                        nc.psum_tensor(f"pq{i}", [128, 512], F32))
                        for i in range(4)]
                    evtix, wtok, kdma, qmmk = {}, {}, {}, {}
                    for oc in range(48):
                        if oc % 2 == 0:
                            if oc >= 4:
                                sch.wait("G", qmmk[oc - 3])
                            wtok[oc // 2] = sch.dma(
                                "G", wq[(oc // 2) % 2][:, :],
                                AP(wqkvT, oc * 128,
                                   [[3 * D, 128], [128 * 3 * D, 16], [1, 256]]),
                                f"wq{(oc // 2) % 2}")
                        tgs = [0, 2] if oc < 16 else [0, 1, 2, 3]
                        kind, hh = oc // 16, oc % 16
                        sch.wait("T", wtok[oc // 2])
                        for tix, tg in enumerate(tgs):
                            sch.wait("T", evtix.get((oc - 1, tix)))
                            for hc in range(16):
                                sch.op("T", lambda t, h=hc, o=oc, g=tg, bi=tix:
                                       t.matmul(
                                           pq[bi][:, :],
                                           wq[(o // 2) % 2][:, h * 256 + (o % 2) * 128:
                                                            h * 256 + (o % 2) * 128 + 128],
                                           x2T[:, h * S + g * 512:h * S + g * 512 + 512],
                                           start=(h == 0), stop=(h == 15)))
                            qmmk[oc] = qmm = sch.mark("T")
                            sch.wait("V", qmm)
                            if kind == 0:
                                dst = q_all[:, hh * TPC + (tg // 2) * 512:
                                            hh * TPC + (tg // 2) * 512 + 512]
                                sch.op("V", lambda v, d=dst, bi=tix, o=oc:
                                       v.tensor_scalar(
                                           d, pq[bi][:, :],
                                           bqkv[:, o:o + 1], None, ALU.add))
                            else:
                                if kdma.get(tix) is not None:
                                    sch.wait("V", kdma[tix])
                                sch.op("V", lambda v, d=kst[tix],
                                       bi=tix, o=oc:
                                       v.tensor_scalar(
                                           d[:, :], pq[bi][:, :],
                                           bqkv[:, o:o + 1], None, ALU.add))
                                vmk = sch.mark("V")
                                sch.wait("Y", vmk)
                                kdma[tix] = sch.dma(
                                    "Y",
                                    AP(kvbuf, ((kind - 1) * 16 + hh) * 128 * S
                                       + tg * 512, [[S, 128], [1, 512]]),
                                    kst[tix][:, :], f"kv{tix}")
                            evtix[(oc, tix)] = sch.mark("V")
                    if gugen is not None:
                        for _ in range(24):
                            next(gugen)
            sch.barrier()

        attnout = qscope.enter_context(
            nc.sbuf_tensor("attnout", [128, H * TPC], BF16))
        if "C" in PHASES:
            with contextlib.ExitStack() as cc:
                cmask = cc.enter_context(
                    nc.sbuf_tensor("cmasksb", [128, 16 * 512], BF16))
                cmtok = sch.dma(
                    "Y", AP(cmask, 0, [[16 * 512, 128], [512, 16], [1, 512]]),
                    AP(cm_d, 0, [[512, 128], [128 * 512, 16], [1, 512]]), "c2")
                sch.wait("V", cmtok)
                kb = [cc.enter_context(nc.sbuf_tensor(f"kb{i}", [128, S], BF16))
                      for i in range(2)]
                vb = [cc.enter_context(nc.sbuf_tensor(f"vb{i}", [128, S], BF16))
                      for i in range(2)]
                vT = [cc.enter_context(nc.sbuf_tensor(f"vT{i}", [128, S], BF16))
                      for i in range(2)]
                pT = [cc.enter_context(nc.sbuf_tensor(f"pT{i}", [128, 16 * 512], BF16))
                      for i in range(2)]
                rs = cc.enter_context(nc.sbuf_tensor("rs", [1, 512], BF16))
                bc_sb = cc.enter_context(nc.sbuf_tensor("bcsb", [128, 512], BF16))
                psc = [cc.enter_context(nc.psum_tensor(f"psc{i}", [128, 512], F32))
                       for i in range(4)]
                pvt = cc.enter_context(nc.psum_tensor("pvt", [128, 1024], BF16))
                pav = [cc.enter_context(nc.psum_tensor(f"pav{i}", [128, 512], F32))
                       for i in range(2)]
                prow = cc.enter_context(nc.psum_tensor("prow", [1, 512], F32))

                vtev = {}
                expmark = {}
                scmark = {}
                attn_emk = {}
                rmk = {}
                pending_tail = [None]

                def emit_tail(it_p):
                    h_p, ts_p = it_p // 2, it_p % 2
                    sch.wait("T", rmk[it_p])
                    if it_p >= 1:
                        sch.wait("T", attn_emk.get(it_p - 1))
                    sch.op("T", lambda t: t.matmul(
                        psc[2][:, :], ones[0:1, 0:128], rs[:, :],
                        start=True, stop=True))
                    bmk = sch.mark("T")
                    sch.wait("S", bmk)
                    sch.op("S", lambda s: s.copy(bc_sb[:, :], psc[2][:, :]))
                    csm = sch.mark("S")
                    sch.wait("V", csm)
                    sch.op("V", lambda v, ts=ts_p, hh=h_p, pa=pav[it_p % 2]:
                           v.tensor_mul(
                               attnout[:, hh * TPC + ts * 512:
                                       hh * TPC + ts * 512 + 512],
                               pa[:, :], bc_sb[:, :]))
                    attn_emk[it_p] = sch.mark("V")

                for h in range(H):
                    hb = h % 2
                    if h >= 2:
                        sch.wait("Y", attn_emk[2 * (h - 2) + 1])
                    ktok = sch.dma("Y", kb[hb][:, :],
                                   AP(kvbuf, h * 128 * S, [[S, 128], [1, S]]),
                                   f"ck{h % 2}")
                    vtok = sch.dma("Y", vb[hb][:, :],
                                   AP(kvbuf, (16 + h) * 128 * S, [[S, 128], [1, S]]),
                                   f"cv{h % 2}")
                    sch.wait("T", vtok)
                    for g2 in range(2):
                        if (h, g2) != (0, 0):
                            prevk = (h, g2 - 1) if g2 else (h - 1, 1)
                            sch.wait("T", vtev[prevk])
                        for tkc in range(8 * g2, 8 * g2 + 8):
                            sch.op("T", lambda t, c=tkc, b=hb:
                                   t.matmul(pvt[:, (c % 8) * 128:(c % 8) * 128 + 128],
                                            vb[b][:, c * 128:c * 128 + 128],
                                            ident[:, :], is_transpose=True,
                                            start=(c % 8 == 0), stop=(c % 8 == 7)))
                        tmk = sch.mark("T")
                        sch.wait("V", tmk)
                        sch.op("V", lambda v, b=hb, g=g2:
                               v.tensor_copy(vT[b][:, g * 1024:(g + 1) * 1024],
                                             pvt[:, :]))
                        vtev[(h, g2)] = sch.mark("V")
                    sch.wait("T", vtev[(h, 0)])
                    sch.wait("T", vtev[(h, 1)])
                    sch.wait("T", ktok)
                    for tslot in range(2):
                        C = CTX[tslot]
                        it = 2 * h + tslot
                        sch.wait("S", attn_emk.get(it - 2))
                        for tkc in range(C):
                            if tkc >= 4:
                                sch.wait("T", expmark[(it, tkc - 4)])
                            elif it >= 1:
                                last = [c for c in range(CTX[(it - 1) % 2])
                                        if c % 4 == tkc % 4]
                                if last:
                                    sch.wait("T", expmark[(it - 1, last[-1])])
                            if tkc % 4 == 2:
                                sch.wait("T", attn_emk.get(it - 1))
                            sch.op("T", lambda t, b=hb, c=tkc, ts=tslot, hh=h:
                                   t.matmul(psc[c % 4][:, :],
                                            kb[b][:, c * 128:c * 128 + 128],
                                            q_all[:, hh * TPC + ts * 512:
                                                  hh * TPC + ts * 512 + 512],
                                            start=True, stop=True))
                            scmark[(it, tkc)] = sch.mark("T")
                            if tkc == 1 and pending_tail[0] is not None:
                                emit_tail(pending_tail[0])
                                pending_tail[0] = None
                            sch.wait("S", scmark[(it, tkc)])
                            sch.op("S", lambda s, b=tslot, c=tkc:
                                   s.activation(pT[b][:, c * 512:c * 512 + 512],
                                                psc[c % 4][:, :], AF.Exp,
                                                bias=0.0, scale=SCALE))
                            expmark[(it, tkc)] = sch.mark("S")
                            if (tslot == 0) or (tkc >= 8):
                                sch.wait("V", expmark[(it, tkc)])
                                sch.op("V", lambda v, b=tslot, c=tkc:
                                       v.tensor_mul(
                                           pT[b][:, c * 512:c * 512 + 512],
                                           pT[b][:, c * 512:c * 512 + 512],
                                           cmask[:, c * 512:c * 512 + 512]))
                                expmark[(it, tkc)] = sch.mark("V")
                            if tkc >= 2:
                                _emit_sumav(sch, it, tkc - 2, C, hb, tslot,
                                            pT, vT, prow, pav, ones, expmark,
                                            rmk, attn_emk)
                        for c2 in (C - 2, C - 1):
                            _emit_sumav(sch, it, c2, C, hb, tslot, pT, vT,
                                        prow, pav, ones, expmark, rmk,
                                        attn_emk)
                        avm = sch.mark("T")
                        sch.wait("V", avm)
                        def _recip(v):
                            with nc.allow_low_precision(reason="softmax 1/s"):
                                return v.reciprocal(rs[:, :], prow[:, :])
                        sch.op("V", _recip)
                        rmk[it] = sch.mark("V")
                        pending_tail[0] = it
                emit_tail(pending_tail[0])
                pending_tail[0] = None
            sch.barrier()
        if "D" in PHASES:
            with contextlib.ExitStack() as dc:
                def pumpd(n=1):
                    pass
                yfeat = dc.enter_context(
                    nc.sbuf_tensor("yfeat", [128, 16 * TPC], BF16))
                wo = [dc.enter_context(nc.sbuf_tensor(f"wosb{i}", [128, 4096], BF16))
                      for i in range(2)]
                ytok = dc.enter_context(nc.sbuf_tensor("ytok", [128, D], F32))
                lno = ytok
                bout = dc.enter_context(nc.sbuf_tensor("boutsb", [128, D], F32))
                xti = [dc.enter_context(nc.sbuf_tensor(f"xtiD{i}", [128, D], F32))
                       for i in range(2)]
                botok = sch.dma("Y", bout[:, :], bout_d[:, :], "c3")
                sch.wait("V", botok)
                with contextlib.ExitStack() as pc:
                    po = [pc.enter_context(
                        nc.psum_tensor(f"po{i}", [128, 512], F32))
                        for i in range(4)]
                    ev, wtok, ommk = {}, {}, {}
                    for oc in range(16):
                        if oc % 2 == 0:
                            if oc >= 4:
                                sch.wait("G", ommk[oc - 3])
                            wtok[oc // 2] = sch.dma(
                                "G", wo[(oc // 2) % 2][:, :],
                                AP(woutT, oc * 128,
                                   [[D, 128], [128 * D, 16], [1, 256]]),
                                f"wo{(oc // 2) % 2}")
                        bks = (oc % 2) * 2
                        sch.wait("T", wtok[oc // 2])
                        if oc >= 2:
                            sch.wait("T", ev[oc - 2])
                        for dcx in range(16):
                            pumpd(1)
                            for tg in range(2):
                                sch.op("T", lambda t, d_=dcx, o=oc, g=tg, bi=bks + tg:
                                       t.matmul(
                                           po[bi][:, :],
                                           wo[(o // 2) % 2][:, d_ * 256 + (o % 2) * 128:
                                                            d_ * 256 + (o % 2) * 128 + 128],
                                           attnout[:, d_ * TPC + g * 512:
                                                   d_ * TPC + g * 512 + 512],
                                           start=(d_ == 0), stop=(d_ == 15)))
                        ommk[oc] = omm = sch.mark("T")
                        sch.wait("S", omm)
                        for tg in range(2):
                            sch.op("S", lambda s, o=oc, g=tg, bi=bks + tg:
                                   s.copy(yfeat[:, o * TPC + g * 512:
                                                o * TPC + g * 512 + 512],
                                          po[bi][:, :]))
                        ev[oc] = sch.mark("S")
                    sch.barrier()
                with contextlib.ExitStack() as pc:
                    ptb = [pc.enter_context(
                        nc.psum_tensor(f"ptb{i}", [128, 1024], BF16))
                        for i in range(2)]
                    prev = {}
                    xst = {}
                    for to in range(8):
                        pumpd(4)
                        if to >= 2:
                            sch.wait("Y", xst[to - 2])
                        xtok = sch.dma("Y", xti[to % 2][:, :],
                                       x_row[_own_tile(to) * 128:
                                             _own_tile(to) * 128 + 128, :],
                                       f"dx{to % 2}")
                        tg, tl = to // 4, to % 4
                        for ocx in range(16):
                            p = ptb[ocx // 8]
                            if ocx % 8 == 0 and to >= 1:
                                sch.wait("T", prev.get((to - 1, ocx // 8)))
                            sch.op("T", lambda t, p_=p, o=ocx, g=tg, l=tl:
                                   t.matmul(p_[:, (o % 8) * 128:(o % 8) * 128 + 128],
                                            yfeat[:, o * TPC + g * 512 + l * 128:
                                                  o * TPC + g * 512 + l * 128 + 128],
                                            ident[:, :], is_transpose=True,
                                            start=(o % 8 == 0), stop=(o % 8 == 7)))
                            if ocx % 8 == 7:
                                tmk = sch.mark("T")
                                sch.wait("V", tmk)
                                if to >= 1:
                                    sch.wait("V", prev[to - 1])
                                sch.op("V", lambda v, p_=p, g=ocx // 8:
                                       v.scalar_tensor_tensor(
                                           ytok[:, g * 1024:(g + 1) * 1024],
                                           p_[:, :], 0.0,
                                           bout[:, g * 1024:(g + 1) * 1024],
                                           ALU.add, ALU.add))
                                prev[(to, ocx // 8)] = sch.mark("V")
                        sch.wait("S", prev[(to, 0)])
                        sch.wait("S", prev[(to, 1)])
                        _emit_ln(sch, col, 8 * (to % 2), ytok[:, :], lnsq,
                                 out_f32=lno)
                        sch.wait("V", xtok)
                        sch.op("V", lambda v, o=to:
                               v.tensor_add(xti[o % 2][:, :], xti[o % 2][:, :],
                                            lno[:, :]))
                        prev[to] = sch.mark("V")
                        sch.wait("Y", prev[to])
                        xst[to] = sch.dma("Y",
                                          xmid[to * 128:(to + 1) * 128, :],
                                          xti[to % 2][:, :], f"dxs{to % 2}")
                    pumpd(400)
            sch.barrier()
        qscope.close()

        if "W" in PHASES:
            with contextlib.ExitStack() as wdc:
                pdnA = [wdc.enter_context(
                    nc.psum_tensor(f"pdnA{i}", [128, 512], F32))
                    for i in range(4)]
                pdnB = [wdc.enter_context(
                    nc.psum_tensor(f"pdnB{i}", [128, 512], F32))
                    for i in range(4)]
                dgen = make_down_gen(wdc, pdnA, pdnB, depth=4)
                for _ in range(NDCH * 9 + 20):
                    next(dgen)
            sch.barrier()

        if "E" in PHASES:
            with contextlib.ExitStack() as ec:
                x2f = ec.enter_context(nc.sbuf_tensor("x2f", [128, 16 * 512], BF16))
                act = ec.enter_context(nc.sbuf_tensor("actsb", [128, 64 * 512], BF16))
                wg = [ec.enter_context(nc.sbuf_tensor(f"wgsb{i}", [128, 2048], BF16))
                      for i in range(4)]
                wu = [ec.enter_context(nc.sbuf_tensor(f"wusb{i}", [128, 2048], BF16))
                      for i in range(4)]
                wd = [ec.enter_context(nc.sbuf_tensor(f"wdsb{i}", [128, 2048], BF16))
                      for i in range(2)]
                xtie = [ec.enter_context(nc.sbuf_tensor(f"xtiE{i}", [128, D], F32))
                       for i in range(2)]
                sil = [ec.enter_context(nc.sbuf_tensor(f"silsb{i}", [128, 512], BF16))
                       for i in range(4)]
                y2f = ec.enter_context(nc.sbuf_tensor("y2f", [128, 16 * 512], BF16))
                ytoke = ec.enter_context(nc.sbuf_tensor("ytokE", [128, D], F32))
                lnoe = ytoke

                for g in range(2):
                    sch.barrier()
                    with contextlib.ExitStack() as pc:
                        ptr = [pc.enter_context(
                            nc.psum_tensor(f"ptrE{g}{i}", [128, 1024], BF16))
                            for i in range(2)]
                        pev, nmks, trmk = {}, {}, {}

                        def e_transpose(tl):
                            for hc in range(16):
                                p = ptr[hc // 8]
                                if hc % 8 == 0:
                                    sch.wait("T", nmks[tl])
                                    if tl >= 1:
                                        sch.wait("T", pev.get((tl - 1, hc // 8)))
                                sch.op("T", lambda t, p_=p, h=hc, b=tl % 2:
                                       t.matmul(p_[:, (h % 8) * 128:(h % 8) * 128 + 128],
                                                lnsqs[b][:, h * 128:h * 128 + 128],
                                                ident[:, :], is_transpose=True,
                                                start=(h % 8 == 0), stop=(h % 8 == 7)))
                                if hc % 8 == 7:
                                    tmk = sch.mark("T")
                                    trmk[tl] = tmk
                                    sch.wait("V", tmk)
                                    sch.op("V", lambda v, p_=p, t_=tl, gg=hc // 8:
                                           v.tensor_copy(
                                               AP(x2f, (gg * 8) * 512 + t_ * 128,
                                                  [[16 * 512, 128], [512, 8], [1, 128]]),
                                               p_[:, :]))
                                    pev[(tl, hc // 8)] = sch.mark("V")

                        for tl in range(4):
                            to = g * 4 + tl
                            if tl >= 2:
                                sch.wait("Y", nmks[tl - 2])
                            xtok = sch.dma("Y", xtie[tl % 2][:, :],
                                           xmid[to * 128:(to + 1) * 128, :],
                                           f"ex{tl % 2}")
                            sch.wait("V", xtok)
                            sch.wait("S", xtok)
                            if tl >= 2:
                                sch.wait("S", trmk[tl - 2])
                                sch.wait("V", trmk[tl - 2])
                            _emit_ln(sch, col, 8 * (tl % 2),
                                     xtie[tl % 2][:, :], lnsqs[tl % 2],
                                     out_bf=lnsqs[tl % 2])
                            nmks[tl] = sch.mark("V")
                            if tl >= 1:
                                e_transpose(tl - 1)
                        e_transpose(3)
                        sch.barrier()
                    with contextlib.ExitStack() as pc:
                        pg = [pc.enter_context(
                            nc.psum_tensor(f"pgE{g}{i}", [128, 512], F32))
                            for i in range(4)]
                        pu = [pc.enter_context(
                            nc.psum_tensor(f"puE{g}{i}", [128, 512], F32))
                            for i in range(4)]
                        ev, wtokg, wtoku, gmmk = {}, {}, {}, {}
                        for oc in range(64):
                            if oc >= 4:
                                sch.wait("Y", gmmk[oc - 4])
                            wtokg[oc] = sch.dma(
                                "Y", wg[oc % 4][:, :],
                                AP(wbuf_g, oc * GCH, [[NBLK, 128], [1, GCH]]),
                                f"eg{oc % 4}")
                            wtoku[oc] = sch.dma(
                                "Y", wu[oc % 4][:, :],
                                AP(wbuf_u, oc * GCH, [[NBLK, 128], [1, GCH]]),
                                f"eu{oc % 4}")
                            bkg = oc % 4
                            sch.wait("T", wtokg[oc])
                            sch.wait("T", wtoku[oc])
                            if oc >= 4:
                                sch.wait("T", ev[oc - 4])
                            for hc in range(16):
                                sch.op("T", lambda t, h=hc, o=oc, bi=bkg:
                                       t.matmul(
                                           pg[bi][:, :],
                                           wg[o % 4][:, h * 128:h * 128 + 128],
                                           x2f[:, h * 512:h * 512 + 512],
                                           start=(h == 0), stop=(h == 15)))
                                sch.op("T", lambda t, h=hc, o=oc, bi=bkg:
                                       t.matmul(
                                           pu[bi][:, :],
                                           wu[o % 4][:, h * 128:h * 128 + 128],
                                           x2f[:, h * 512:h * 512 + 512],
                                           start=(h == 0), stop=(h == 15)))
                            gmmk[oc] = gmm = sch.mark("T")
                            sch.wait("S", gmm)
                            if oc >= 4:
                                sch.wait("S", ev[oc - 4])
                            sch.op("S", lambda s, bi=bkg, o=oc:
                                   s.activation(sil[o % 4][:, :], pg[bi][:, :],
                                                AF.Silu, bias=0.0, scale=1.0))
                            smk = sch.mark("S")
                            sch.wait("V", smk)
                            sch.op("V", lambda v, o=oc, bi=bkg:
                                   v.tensor_mul(act[:, o * 512:o * 512 + 512],
                                                sil[o % 4][:, :], pu[bi][:, :]))
                            ev[oc] = sch.mark("V")
                        sch.barrier()
                    with contextlib.ExitStack() as pc:
                        pd = [pc.enter_context(
                            nc.psum_tensor(f"pdE{g}{i}", [128, 512], F32))
                            for i in range(4)]
                        ev, wtokd, dmmk = {}, {}, {}
                        for hcp in range(8):
                            bks = (hcp % 2) * 2
                            if hcp >= 2:
                                sch.wait("T", ev[hcp - 2])
                            for ic in range(64):
                                if ic % 8 == 0:
                                    icq = ic // 8
                                    prevq = (hcp, icq - 2) if icq >= 2 else \
                                            ((hcp - 1, 6 + icq) if hcp >= 1 else None)
                                    if prevq is not None:
                                        sch.wait("Y", dmmk.get(prevq))
                                    wtokd[(hcp, icq)] = sch.dma(
                                        "Y", wd[icq % 2][:, :],
                                        AP(wbuf_d, icq * 8 * D + hcp * 256,
                                           [[64 * D, 128], [D, 8], [1, 256]]),
                                        f"ed{icq % 2}")
                                    sch.wait("T", wtokd[(hcp, icq)])
                                for o2 in range(2):
                                    sch.op("T", lambda t, i_=ic, o=o2, bi=bks:
                                           t.matmul(
                                               pd[bi + o][:, :],
                                               wd[(i_ // 8) % 2][:, (i_ % 8) * 256 + o * 128:
                                                            (i_ % 8) * 256 + o * 128 + 128],
                                               act[:, i_ * 512:i_ * 512 + 512],
                                               start=(i_ == 0), stop=(i_ == 63)))
                                if ic % 8 == 7:
                                    dmmk[(hcp, ic // 8)] = sch.mark("T")
                            dmm = sch.mark("T")
                            sch.wait("S", dmm)
                            if hcp >= 2:
                                sch.wait("S", ev[hcp - 2])
                            for o2 in range(2):
                                sch.op("S", lambda s, h=hcp, o=o2, bi=bks:
                                       s.copy(y2f[:, (2 * h + o) * 512:
                                                  (2 * h + o) * 512 + 512],
                                              pd[bi + o][:, :]))
                            ev[hcp] = sch.mark("S")
                        sch.barrier()
                    with contextlib.ExitStack() as pc:
                        ptb = [pc.enter_context(
                            nc.psum_tensor(f"ptbE{g}{i}", [128, 1024], BF16))
                            for i in range(2)]
                        prev = {}
                        ost = {}
                        for tl in range(4):
                            to = g * 4 + tl
                            if tl >= 2:
                                sch.wait("Y", ost[tl - 2])
                            xtok = sch.dma("Y", xtie[tl % 2][:, :],
                                           xmid[to * 128:(to + 1) * 128, :],
                                           f"e4x{tl % 2}")
                            for ocx in range(16):
                                p = ptb[ocx // 8]
                                if ocx % 8 == 0 and tl >= 1:
                                    sch.wait("T", prev.get((tl - 1, ocx // 8)))
                                sch.op("T", lambda t, p_=p, o=ocx, l=tl:
                                       t.matmul(p_[:, (o % 8) * 128:(o % 8) * 128 + 128],
                                                y2f[:, o * 512 + l * 128:
                                                    o * 512 + l * 128 + 128],
                                                ident[:, :], is_transpose=True,
                                                start=(o % 8 == 0), stop=(o % 8 == 7)))
                                if ocx % 8 == 7:
                                    tmk = sch.mark("T")
                                    sch.wait("S", tmk)
                                    if tl >= 1:
                                        sch.wait("S", prev[tl - 1])
                                    sch.op("S", lambda s, p_=p, gg=ocx // 8:
                                           s.copy(ytoke[:, gg * 1024:(gg + 1) * 1024],
                                                  p_[:, :]))
                                    prev[(tl, ocx // 8)] = sch.mark("S")
                            sch.wait("V", prev[(tl, 0)])
                            sch.wait("V", prev[(tl, 1)])
                            _emit_ln(sch, col, 8 * (tl % 2), ytoke[:, :], lnsq,
                                     out_f32=lnoe)
                            sch.wait("V", xtok)
                            sch.op("V", lambda v, l=tl:
                                   v.tensor_add(xtie[l % 2][:, :],
                                                xtie[l % 2][:, :], lnoe[:, :]))
                            fmk = sch.mark("V")
                            prev[tl] = fmk
                            sch.wait("Y", fmk)
                            ost[tl] = sch.dma(
                                "Y", out_d[to * 128:(to + 1) * 128, :],
                                xtie[tl % 2][:, :], f"out{tl % 2}")

        sch.finish("Y")
        sch.emit(nc, ctx)

    return nc


_TPR = {}


def _perm_gu():
    if "gu" not in _TPR:
        p = np.arange(GCH).reshape(GCH // 16, 16).T.ravel()
        _TPR["gu"] = (np.arange(NBLK).reshape(-1, GCH)[:, p]).ravel()
    return _TPR["gu"]


def _perm_dn():
    if "dn" not in _TPR:
        _TPR["dn"] = (np.arange(D)[None, :] * 64
                      + np.arange(64)[:, None]).ravel()
    return _TPR["dn"]


def _tau_tiled():
    cst = _consts()
    tp = cst["taupow"]
    return np.concatenate([np.tile(tp[0:4], (4, 1)), np.tile(tp[4:8], (4, 1))])


def _tpr0():
    if 0 not in _TPR:
        _TPR[0] = np.ascontiguousarray(_tau_tiled()[:, _perm_gu()])
    return _TPR[0]


def _tpr1():
    if 1 not in _TPR:
        _TPR[1] = np.ascontiguousarray(_tau_tiled()[:, _perm_dn()])
    return _TPR[1]


def _get_program():
    global _PROG
    if _PROG is None:
        _PROG = build_program()
    return _PROG


def kernel(L0_attn_x, L0_attn_w_qkv, L0_attn_b_qkv, L0_attn_w_out,
           L0_attn_b_out, L0_ffn_gate_cp, L0_ffn_up_cp, L0_ffn_down_cp):
    cst = _consts()
    nc = _get_program()

    x = np.ascontiguousarray(np.asarray(L0_attn_x, dtype=np.float32))
    wqkvT = np.ascontiguousarray(np.asarray(L0_attn_w_qkv, np.float32).T)
    woutT = np.ascontiguousarray(np.asarray(L0_attn_w_out, np.float32).T)
    bqkv = np.ascontiguousarray(
        np.asarray(L0_attn_b_qkv, np.float32).reshape(48, 128).T)
    bout_rep = np.ascontiguousarray(
        np.broadcast_to(np.asarray(L0_attn_b_out, np.float32)[None, :],
                        (128, D)))
    cps = [np.asarray(c, np.float32) for c in
           (L0_ffn_gate_cp, L0_ffn_up_cp, L0_ffn_down_cp)]
    taps = [c[cst["tapidx"]] for c in cps]
    tap_gu = np.ascontiguousarray(
        np.concatenate([taps[0], taps[1]])[:, _perm_gu()])
    tap_dn = np.ascontiguousarray(taps[2][:, _perm_dn()])
    wgm_gu = np.ascontiguousarray(cst["wgmask"][:, _perm_gu()])
    wgm_dn = np.ascontiguousarray(cst["wgmask"][:, _perm_dn()])

    in_maps = []
    for c in range(NCORES):
        b, j = c // 2, c % 2
        gk = [j, 1 - j, 2 + j, 3 - j]
        xr = np.ascontiguousarray(x[b].reshape(4, 512, D)[gk].reshape(S, D))
        in_maps.append(dict(
            x_row=xr, wqkvT=wqkvT, bqkv=bqkv, woutT=woutT,
            bout_rep=bout_rep, tap_gu=tap_gu, tap_dn=tap_dn,
            tau_gu=_tpr0(), tau_dn=_tpr1(),
            atilde=cst["atilde"], wgmask_gu=wgm_gu, wgmask_dn=wgm_dn,
            cmask=np.ascontiguousarray(cst["cmask"][j]),
            ident=cst["ident"], ones=cst["ones"],
        ))

    res = run_bass_kernel_spmd(nc, in_maps, core_ids=list(range(NCORES)))

    out = np.empty((B, S, D), dtype=np.float32)
    for c in range(NCORES):
        b, j = c // 2, c % 2
        o = res.results[c]["out"]
        out[b, j * 512:(j + 1) * 512] = o[0:512]
        out[b, (2 + j) * 512:(3 + j) * 512] = o[512:1024]
    return out


# version 23
# speedup vs baseline: 1.3850x; 1.0081x over previous
import contextlib
import math

import numpy as np
import ml_dtypes

import concourse.bass as bass
import concourse.mybir as mybir
from concourse.bass_utils import run_bass_kernel_spmd

F32 = mybir.dt.float32
BF16 = mybir.dt.bfloat16
AF = mybir.ActivationFunctionType
ALU = mybir.AluOpType
AX = mybir.AxisListType
AP = bass.AP

B, S, D = 4, 2048, 2048
H, HD = 16, 128
INTER = 8192
NCTRL = 130156
NSAMP = INTER * D
NBLK = NSAMP // 128
GCH = 2048
NGCH = NBLK // GCH
DCH = 4096
NDCH = NBLK // DCH
TPC = 1024
SCALE = 1.0 / math.sqrt(HD)
INV_D = 1.0 / D
EPS = 1e-5
NCORES = 8
CTX = (8, 16)


_CONSTS = None


def _consts():
    global _CONSTS
    if _CONSTS is not None:
        return _CONSTS
    t = np.linspace(0.0, NCTRL - 1.0, NSAMP, dtype=np.float64)
    iarr = np.clip(np.floor(t).astype(np.int64), 0, NCTRL - 2)
    i_b = iarr[::128]
    tau = t[::128] - i_b
    s_b = (iarr.reshape(NBLK, 128) == i_b[:, None]).sum(1)
    R = (NCTRL - 1.0) / (NSAMP - 1)

    Cc = np.array([
        [0.0, -0.5, 1.0, -0.5],
        [1.0, 0.0, -2.5, 1.5],
        [0.0, 0.5, 2.0, -1.5],
        [0.0, 0.0, -0.5, 0.5]], dtype=np.float64)
    from math import comb
    qR = np.arange(128, dtype=np.float64) * R
    at = np.zeros((16, 128), dtype=np.float64)
    for k in range(4):
        for m in range(4):
            v = np.zeros_like(qR)
            for jj in range(m, 4):
                v += comb(jj, m) * Cc[k, jj] * qR ** (jj - m)
            at[4 * k + m] = v
    tp = np.zeros((8, NBLK), dtype=np.float64)
    for m in range(4):
        tp[m] = tau ** m
        tp[4 + m] = (tau - 1.0) ** m
    mask = (np.arange(128)[:, None] < s_b[None, :])
    tapidx = np.stack([np.clip(i_b + r - 1, 0, NCTRL - 1) for r in range(5)])

    cmask = {}
    for j in range(2):
        gk = [j, 1 - j, 2 + j, 3 - j]
        cm = np.zeros((16, 128, 512), dtype=np.float64)
        for cr in range(16):
            tslot = 0 if cr < 8 else 1
            gt = 2 * tslot + j
            tk = gk[cr // 4] * 512 + (cr % 4) * 128 + np.arange(128)[:, None]
            tq = gt * 512 + np.arange(512)[None, :]
            cm[cr] = (tk <= tq)
        cmask[j] = cm.astype(ml_dtypes.bfloat16)

    r0_tap = np.zeros(128, dtype=np.int64)
    r0_wt = np.zeros(128, dtype=np.int64)
    r0_tau = np.zeros(128, dtype=np.int64)
    for p in range(128):
        si, r32 = p // 32, p % 32
        k, m = (r32 % 16) // 4, r32 % 4
        r0_wt[p] = si // 2
        r0_tap[p] = k + (si % 2)
        r0_tau[p] = (si % 2) * 4 + m
    r1_tap = np.zeros(64, dtype=np.int64)
    r1_tau = np.zeros(64, dtype=np.int64)
    for p in range(64):
        si, r32 = p // 32, p % 32
        k, m = (r32 % 16) // 4, r32 % 4
        r1_tap[p] = k + si
        r1_tau[p] = si * 4 + m
    _CONSTS = dict(
        tapidx=tapidx, s_b=s_b,
        r0_tap=r0_tap, r0_wt=r0_wt, r0_tau=r0_tau,
        r1_tap=r1_tap, r1_tau=r1_tau,
        atilde=at.astype(np.float32),
        taupow=tp.astype(np.float32),
        wgmask=mask.astype(np.uint8),
        ident=np.eye(128, dtype=ml_dtypes.bfloat16),
        ones=np.ones((128, 128), dtype=ml_dtypes.bfloat16),
        cmask=cmask,
    )
    return _CONSTS


class Sched:
    ENGINES = ("T", "V", "S", "G", "Y")

    def __init__(self):
        self.recs = {e: [] for e in self.ENGINES}
        self.cnt = {e.lower(): 0 for e in self.ENGINES}
        self.pend = {e: [] for e in self.ENGINES}
        self.streams = {}
        self.issuer = {}
        self.open_streams = {e: set() for e in self.ENGINES}

    CHAINED = ("V", "S", "G")

    def op(self, e, fn):
        w = list(self.pend[e])
        self.pend[e] = []
        incs = []
        if e in self.CHAINED:
            sem = e.lower()
            if self.cnt[sem]:
                w.append((sem, self.cnt[sem]))
            self.cnt[sem] += 1
            incs.append((sem, 1, self.cnt[sem]))
        self.recs[e].append([fn, tuple(w), incs, False])

    def mark(self, e):
        rec = self.recs[e][-1]
        if rec[3]:
            sem, _n, tgt = rec[2][0]
            return (sem, tgt)
        if rec[2]:
            sem, _n, tgt = rec[2][0]
            return (sem, tgt)
        sem = e.lower()
        self.cnt[sem] += 1
        rec[2].append((sem, 1, self.cnt[sem]))
        return (sem, self.cnt[sem])

    def dma(self, e, out, in_, stream):
        sname = "q_" + stream
        assert self.issuer.setdefault(sname, e) == e, (sname, e)
        self.streams[sname] = self.streams.get(sname, 0) + 16
        tgt = self.streams[sname]
        self.open_streams[e].add(sname)
        w = tuple(self.pend[e])
        self.pend[e] = []
        self.recs[e].append(
            [lambda eng, o=out, i=in_: eng.dma_start(out=o, in_=i), w,
             [(sname, 16, tgt)], True])
        return (sname, tgt)

    def wait(self, e, tok):
        if tok is not None:
            self.pend[e].append(tok)

    def barrier(self):
        toks = []
        for e in self.ENGINES:
            if not self.recs[e] and not self.open_streams[e]:
                continue
            if self.open_streams[e]:
                for sname in sorted(self.open_streams[e]):
                    self.pend[e].append((sname, self.streams[sname]))
                self.open_streams[e] = set()
                self.op(e, lambda eng: eng.nop())
            toks.append(self.mark(e))
        for e in self.ENGINES:
            for t in toks:
                self.pend[e].append(t)

    def finish(self, e):
        for sname in sorted(self.open_streams[e]):
            self.pend[e].append((sname, self.streams[sname]))
        self.op(e, lambda eng: eng.nop())

    def emit(self, nc, ctx):
        sems = {}
        for sn in self.cnt:
            sems[sn] = ctx.enter_context(nc.semaphore("sem_" + sn))
        for sn in self.streams:
            sems[sn] = ctx.enter_context(nc.semaphore(sn))

        def replay(e):
            def f(eng):
                for fn, waits, incs, _isdma in self.recs[e]:
                    agg = {}
                    for (sn, v) in waits:
                        agg[sn] = max(agg.get(sn, 0), v)
                    for sn, v in agg.items():
                        eng.wait_ge(sems[sn], v)
                    inst = fn(eng)
                    for (sn, n, _tgt) in incs:
                        inst.then_inc(sems[sn], n)
            return f
        with nc.Block() as blk:
            blk.tensor(replay("T"))
            blk.vector(replay("V"))
            blk.scalar(replay("S"))
            blk.gpsimd(replay("G"))
            blk.sync(replay("Y"))


def _emit_sumav(sch, it, c, C, hb, tslot, pT, vT, prow, pav, ones, expmark,
                rmk, attn_emk):
    sch.wait("T", expmark[(it, c)])
    if c == 0:
        sch.wait("T", rmk.get(it - 1))
        sch.wait("T", attn_emk.get(it - 2))
    sch.op("T", lambda t, b=tslot, c_=c, C_=C:
           t.matmul(prow[:, :], ones[:, 0:1],
                    pT[b][:, c_ * 512:c_ * 512 + 512],
                    start=(c_ == 0), stop=(c_ == C_ - 1)))
    sch.op("T", lambda t, b=hb, bb=tslot, c_=c, C_=C, pa=pav[it % 2]:
           t.matmul(pa[:, :], vT[b][:, c_ * 128:c_ * 128 + 128],
                    pT[bb][:, c_ * 512:c_ * 512 + 512],
                    start=(c_ == 0), stop=(c_ == C_ - 1)))


def _own_tile(to):
    return to if to < 4 else to + 4


def _emit_ln(sch, col, c0, xap, sqscratch, out_bf=None, out_f32=None):
    sch.op("V", lambda v: v.reduce_sum(col[:, c0:c0 + 1], xap, axis=AX.X))
    sch.op("S", lambda s: s.activation(sqscratch[:, :], xap, AF.Square,
                                       bias=0.0, scale=1.0,
                                       accum_out=col[:, c0 + 1:c0 + 2]))
    sqm = sch.mark("S")
    sch.op("V", lambda v: v.tensor_scalar(col[:, c0 + 2:c0 + 3],
                                          col[:, c0:c0 + 1], INV_D, None,
                                          ALU.mult))
    sch.op("V", lambda v: v.tensor_mul(col[:, c0 + 3:c0 + 4],
                                       col[:, c0 + 2:c0 + 3],
                                       col[:, c0 + 2:c0 + 3]))
    sch.wait("V", sqm)
    sch.op("V", lambda v: v.tensor_scalar(col[:, c0 + 4:c0 + 5],
                                          col[:, c0 + 1:c0 + 2], INV_D,
                                          col[:, c0 + 3:c0 + 4],
                                          ALU.mult, ALU.subtract))
    vmk = sch.mark("V")
    sch.wait("S", vmk)
    sch.op("S", lambda s: s.activation(col[:, c0 + 5:c0 + 6],
                                       col[:, c0 + 4:c0 + 5], AF.Sqrt,
                                       bias=EPS, scale=1.0))
    smk = sch.mark("S")
    sch.wait("V", smk)
    sch.op("V", lambda v: v.reciprocal(col[:, c0 + 6:c0 + 7],
                                       col[:, c0 + 5:c0 + 6]))
    sch.op("V", lambda v: v.tensor_mul(col[:, c0 + 7:c0 + 8],
                                       col[:, c0 + 2:c0 + 3],
                                       col[:, c0 + 6:c0 + 7]))
    if out_bf is not None:
        sch.op("V", lambda v: v.tensor_scalar(out_bf[:, :], xap,
                                              col[:, c0 + 6:c0 + 7],
                                              col[:, c0 + 7:c0 + 8],
                                              ALU.mult, ALU.subtract))
    if out_f32 is not None:
        sch.op("V", lambda v: v.tensor_scalar(out_f32[:, :], xap,
                                              col[:, c0 + 6:c0 + 7],
                                              col[:, c0 + 7:c0 + 8],
                                              ALU.mult, ALU.subtract))


_PROG = None
PHASES = ("W", "B", "C", "D", "E")


def build_program():
    nc = bass.Bass("TRN2", target_bir_lowering=False)
    sch = Sched()

    x_row = nc.dram_tensor("x_row", [S, D], F32, kind="ExternalInput")
    wqkvT = nc.dram_tensor("wqkvT", [D, 3 * D], F32, kind="ExternalInput")
    bqkv_d = nc.dram_tensor("bqkv", [128, 48], F32, kind="ExternalInput")
    woutT = nc.dram_tensor("woutT", [D, D], F32, kind="ExternalInput")
    bout_d = nc.dram_tensor("bout_rep", [128, D], F32, kind="ExternalInput")
    tap_gu_d = nc.dram_tensor("tap_gu", [10, NBLK], F32, kind="ExternalInput")
    tau_gu_d = nc.dram_tensor("tau_gu", [32, NBLK], F32, kind="ExternalInput")
    tap_dn_d = nc.dram_tensor("tap_dn", [5, NBLK], F32, kind="ExternalInput")
    tau_dn_d = nc.dram_tensor("tau_dn", [32, NBLK], F32, kind="ExternalInput")
    at_d = nc.dram_tensor("atilde", [16, 128], F32, kind="ExternalInput")
    wgm_gu_d = nc.dram_tensor("wgmask_gu", [128, NBLK], mybir.dt.uint8,
                              kind="ExternalInput")
    wgm_dn_d = nc.dram_tensor("wgmask_dn", [128, NBLK], mybir.dt.uint8,
                              kind="ExternalInput")
    cm_d = nc.dram_tensor("cmask", [16, 128, 512], BF16, kind="ExternalInput")
    id_d = nc.dram_tensor("ident", [128, 128], BF16, kind="ExternalInput")
    on_d = nc.dram_tensor("ones", [128, 128], BF16, kind="ExternalInput")
    out_d = nc.dram_tensor("out", [TPC, D], F32, kind="ExternalOutput")

    wbuf_g = nc.dram_tensor("wbuf_g", [128, NBLK], BF16)
    wbuf_u = nc.dram_tensor("wbuf_u", [128, NBLK], BF16)
    wbuf_d = nc.dram_tensor("wbuf_d", [128, 64, D], BF16)
    kvbuf = nc.dram_tensor("kvbuf", [2 * H, 128, S], BF16)
    xmid = nc.dram_tensor("xmid", [TPC, D], F32)

    ctx = contextlib.ExitStack()
    with ctx:
        csb = ctx.enter_context(nc.sbuf_tensor("constsb", [128, 4], F32))
        for ci, cv in enumerate((0.0, EPS, 1.0, SCALE)):
            nc.const_aps.register(F32, float(cv), csb[:, ci:ci + 1])
            sch.op("V", lambda v, i=ci, val=float(cv):
                   v.memset(csb[:, i:i + 1], val))
        ident = ctx.enter_context(nc.sbuf_tensor("identsb", [128, 128], BF16))
        ones = ctx.enter_context(nc.sbuf_tensor("onessb", [128, 128], BF16))
        at_sb = ctx.enter_context(nc.sbuf_tensor("atsb", [128, 128], BF16))
        bqkv = ctx.enter_context(nc.sbuf_tensor("bqkvsb", [128, 48], F32))
        col = ctx.enter_context(nc.sbuf_tensor("colsb", [128, 16], F32))
        lnsq = ctx.enter_context(nc.sbuf_tensor("lnsqsb", [128, D], BF16))
        lnsq2 = ctx.enter_context(nc.sbuf_tensor("lnsqsb2", [128, D], BF16))
        lnsqs = [lnsq, lnsq2]

        sch.dma("Y", ident[:, :], id_d[:, :], "c0")
        sch.dma("Y", ones[:, :], on_d[:, :], "c0")
        sch.dma("Y", bqkv[:, :], bqkv_d[:, :], "c0")
        for u in (0, 32, 64, 96):
            sch.dma("G", at_sb[u:u + 16, :], at_d[:, :], "c1")
        sch.barrier()

        def make_gu_gen(wc, pA, pB, depth):
            CT = [wc.enter_context(nc.sbuf_tensor(f"gct{i}", [42, GCH], BF16))
                  for i in range(3)]
            U = [wc.enter_context(nc.sbuf_tensor(f"Ug{i}", [128, GCH], BF16))
                 for i in range(2)]
            IN0 = [wc.enter_context(nc.sbuf_tensor(f"gi0_{i}", [128, GCH], BF16))
                   for i in range(2)]
            IN1 = [wc.enter_context(nc.sbuf_tensor(f"gi1_{i}", [128, GCH], BF16))
                   for i in range(2)]
            MSK = [wc.enter_context(
                nc.sbuf_tensor(f"gm_{i}", [128, GCH], mybir.dt.uint8))
                for i in range(4)]
            WO = [wc.enter_context(nc.sbuf_tensor(f"gwo_{i}", [128, GCH], BF16))
                  for i in range(4)]
            streams = [(wbuf_g, 0, 32), (wbuf_u, 64, 96)]
            for t in IN0 + IN1:
                sch.op("V", lambda v, t_=t: v.memset(t_[:, :], 0.0))
            msmk_gu = sch.mark("V")

            def ldtoks(c):
                sch.dma("G", CT[c % 3][0:10, :],
                        AP(tap_gu_d, c * GCH, [[NBLK, 10], [1, GCH]]),
                        f"wg{c % 3}")
                return sch.dma("G", CT[c % 3][10:42, :],
                               AP(tau_gu_d, c * GCH, [[NBLK, 32], [1, GCH]]),
                               f"wg{c % 3}")

            def gen():
                umark, evmark, stok, mtoks = {}, {}, {}, {}
                ltok, etok = {}, {}

                def expand(c):
                    sch.wait("G", ltok[c])
                    if c >= 2:
                        sch.wait("G", umark[c - 2])
                    else:
                        sch.wait("G", msmk_gu)
                    for si in range(4):
                        wt, dlt = si // 2, si % 2
                        sch.dma("G", AP(IN0[c % 2], (si * 32) * GCH,
                                        [[GCH, 16], [1, GCH]]),
                                AP(CT[c % 3], (wt * 5 + dlt) * GCH,
                                   [[GCH, 4], [0, 4], [1, GCH]]),
                                f"we{c % 2}")
                        etok[c] = sch.dma(
                            "G", AP(IN1[c % 2], (si * 32) * GCH,
                                    [[GCH, 16], [1, GCH]]),
                            AP(CT[c % 3], (10 + dlt * 16) * GCH,
                               [[GCH, 16], [1, GCH]]),
                            f"we{c % 2}")

                def umult(c):
                    sch.wait("V", etok[c])
                    if c >= 2:
                        sch.wait("V", evmark[(c - 2) * 8 + 7])
                    sch.op("V", lambda v, b=c % 2: v.tensor_mul(
                        U[b][:, :], IN0[b][:, :], IN1[b][:, :]))
                    umark[c] = sch.mark("V")

                for c in range(NGCH):
                    yield
                    buf = c % 2
                    if c == 0:
                        for cpre in (0, 1):
                            mtoks[cpre] = sch.dma(
                                "G", MSK[cpre % 4][:, :],
                                wgm_gu_d[:, cpre * GCH:(cpre + 1) * GCH],
                                f"gm{cpre % 4}")
                        ltok[0] = ldtoks(0)
                        ltok[1] = ldtoks(1)
                        expand(0)
                        umult(0)
                    if c + 2 < NGCH:
                        if c >= 1:
                            sch.wait("G", etok[c - 1])
                        ltok[c + 2] = ldtoks(c + 2)
                    sch.wait("T", umark[c])
                    for w in range(2):
                        wb, uA, uB = streams[w]
                        cw = c * 2 + w
                        first_ev = True
                        for q4 in range(4):
                            yield
                            i2 = c * 8 + w * 4 + q4
                            bk = i2 % depth
                            if i2 >= depth:
                                sch.wait("T", evmark[i2 - depth])
                            for u, po in ((uA, pA[bk]), (uB, pB[bk])):
                                sch.op("T", lambda t, u_=u, b=buf, pp=po,
                                       cc=q4 * 512:
                                       t.matmul(pp[:, :],
                                                at_sb[u_:u_ + 16, :],
                                                U[b][u_:u_ + 16, cc:cc + 512],
                                                start=True, stop=True,
                                                tile_position=(u_, 0)))
                            mm = sch.mark("T")
                            sch.wait("S", mm)
                            if first_ev:
                                if cw >= 4:
                                    sch.wait("S", stok[cw - 4])
                                if w == 0:
                                    sch.wait("V", mtoks[c])
                                first_ev = False
                            wo_ap = WO[cw % 4][:, q4 * 512:(q4 + 1) * 512]
                            mk_ap = MSK[c % 4][:, q4 * 512:(q4 + 1) * 512]
                            sch.op("S", lambda s, o_=wo_ap, pp=pB[bk]:
                                   s.copy(o_, pp[:, :]))
                            smk = sch.mark("S")
                            sch.wait("V", smk)
                            sch.op("V", lambda v, o_=wo_ap, m_=mk_ap,
                                   pp=pA[bk]:
                                   v.copy_predicated(o_, m_, pp[:, :]))
                            evmark[i2] = sch.mark("V")
                        sch.wait("Y", evmark[c * 8 + w * 4 + 3])
                        stok[cw] = sch.dma(
                            "Y", wb[:, c * GCH:(c + 1) * GCH],
                            WO[cw % 4][:, :],
                            f"ws{cw % 4}")
                        if w == 0 and c + 1 < NGCH:
                            expand(c + 1)
                            umult(c + 1)
                    if c + 2 < NGCH:
                        if c >= 2:
                            sch.wait("G", evmark[(c - 2) * 8 + 7])
                        mtoks[c + 2] = sch.dma(
                            "G", MSK[(c + 2) % 4][:, :],
                            wgm_gu_d[:, (c + 2) * GCH:(c + 3) * GCH],
                            f"gm{(c + 2) % 4}")
                while True:
                    yield
            return gen()

        def make_down_gen(wc, pA, pB, depth):
            CT = [wc.enter_context(nc.sbuf_tensor(f"dct{i}", [37, DCH], BF16))
                  for i in range(3)]
            Ud = [wc.enter_context(nc.sbuf_tensor(f"Ud{i}", [64, DCH], BF16))
                  for i in range(2)]
            IN0 = [wc.enter_context(nc.sbuf_tensor(f"di0_{i}", [64, DCH], BF16))
                   for i in range(2)]
            IN1 = [wc.enter_context(nc.sbuf_tensor(f"di1_{i}", [64, DCH], BF16))
                   for i in range(2)]
            MSK = [wc.enter_context(
                nc.sbuf_tensor(f"dm_{i}", [128, DCH], mybir.dt.uint8))
                for i in range(4)]
            WO = [wc.enter_context(nc.sbuf_tensor(f"dwo_{i}", [128, DCH], BF16))
                  for i in range(4)]
            NCG = DCH // 512
            for t in IN0 + IN1:
                sch.op("V", lambda v, t_=t: v.memset(t_[:, :], 0.0))
            msmk_dn = sch.mark("V")

            def ldtoks(c):
                sch.dma("G", CT[c % 3][0:5, :],
                        AP(tap_dn_d, c * DCH, [[NBLK, 5], [1, DCH]]),
                        f"wd{c % 3}")
                return sch.dma("G", CT[c % 3][5:37, :],
                               AP(tau_dn_d, c * DCH, [[NBLK, 32], [1, DCH]]),
                               f"wd{c % 3}")

            def gen():
                umark, evmark, stok, mtoks = {}, {}, {}, {}
                ltok, etok = {}, {}

                def expand(c):
                    sch.wait("G", ltok[c])
                    if c >= 2:
                        sch.wait("G", umark[c - 2])
                    else:
                        sch.wait("G", msmk_dn)
                    for si in range(2):
                        sch.dma("G", AP(IN0[c % 2], (si * 32) * DCH,
                                        [[DCH, 16], [1, DCH]]),
                                AP(CT[c % 3], si * DCH,
                                   [[DCH, 4], [0, 4], [1, DCH]]),
                                f"de{c % 2}")
                        etok[c] = sch.dma(
                            "G", AP(IN1[c % 2], (si * 32) * DCH,
                                    [[DCH, 16], [1, DCH]]),
                            AP(CT[c % 3], (5 + si * 16) * DCH,
                               [[DCH, 16], [1, DCH]]),
                            f"de{c % 2}")

                def umult(c):
                    sch.wait("V", etok[c])
                    if c >= 2:
                        sch.wait("V", evmark[(c - 2) * NCG + NCG - 1])
                    sch.op("V", lambda v, b=c % 2: v.tensor_mul(
                        Ud[b][:, :], IN0[b][:, :], IN1[b][:, :]))
                    umark[c] = sch.mark("V")

                for c in range(NDCH):
                    yield
                    buf = c % 2
                    if c == 0:
                        for cpre in (0, 1):
                            mtoks[cpre] = sch.dma(
                                "G", MSK[cpre % 4][:, :],
                                wgm_dn_d[:, cpre * DCH:(cpre + 1) * DCH],
                                f"dm{cpre % 4}")
                        ltok[0] = ldtoks(0)
                        ltok[1] = ldtoks(1)
                        expand(0)
                        umult(0)
                    if c + 2 < NDCH:
                        if c >= 1:
                            sch.wait("G", etok[c - 1])
                        ltok[c + 2] = ldtoks(c + 2)
                    sch.wait("T", umark[c])
                    first_ev = True
                    for cg in range(NCG):
                        yield
                        i2 = c * NCG + cg
                        bk = i2 % depth
                        if i2 >= depth:
                            sch.wait("T", evmark[i2 - depth])
                        for u, po in ((0, pA[bk]), (32, pB[bk])):
                            sch.op("T", lambda t, u_=u, b=buf, pp=po,
                                   cc=cg * 512:
                                   t.matmul(pp[:, :], at_sb[u_:u_ + 16, :],
                                            Ud[b][u_:u_ + 16, cc:cc + 512],
                                            start=True, stop=True,
                                            tile_position=(u_, 0)))
                        mm = sch.mark("T")
                        sch.wait("S", mm)
                        if first_ev:
                            if c >= 4:
                                sch.wait("S", stok[c - 4])
                            sch.wait("V", mtoks[c])
                            first_ev = False
                        wo_ap = WO[c % 4][:, cg * 512:(cg + 1) * 512]
                        mk_ap = MSK[c % 4][:, cg * 512:(cg + 1) * 512]
                        sch.op("S", lambda s, o_=wo_ap, pp=pB[bk]:
                               s.copy(o_, pp[:, :]))
                        smk = sch.mark("S")
                        sch.wait("V", smk)
                        sch.op("V", lambda v, o_=wo_ap, m_=mk_ap, pp=pA[bk]:
                               v.copy_predicated(o_, m_, pp[:, :]))
                        evmark[i2] = sch.mark("V")
                        if cg == NCG // 2 and c + 1 < NDCH:
                            expand(c + 1)
                            umult(c + 1)
                    sch.wait("Y", evmark[c * NCG + NCG - 1])
                    stok[c] = sch.dma(
                        "Y",
                        AP(wbuf_d, c * DCH, [[64 * D, 128], [1, DCH]]),
                        WO[c % 4][:, :],
                        f"dws{c % 4}")
                    if c + 2 < NDCH:
                        if c >= 2:
                            sch.wait("G", evmark[(c - 2) * NCG + NCG - 1])
                        mtoks[c + 2] = sch.dma(
                            "G", MSK[(c + 2) % 4][:, :],
                            wgm_dn_d[:, (c + 2) * DCH:(c + 3) * DCH],
                            f"dm{(c + 2) % 4}")
                while True:
                    yield
            return gen()

        qscope = ctx.enter_context(contextlib.ExitStack())
        q_all = qscope.enter_context(
            nc.sbuf_tensor("qall", [128, H * TPC], BF16))
        if "W" in PHASES:
            with contextlib.ExitStack() as wsc:
                pguA = [wsc.enter_context(
                    nc.psum_tensor(f"pguA{i}", [128, 512], F32))
                    for i in range(4)]
                pguB = [wsc.enter_context(
                    nc.psum_tensor(f"pguB{i}", [128, 512], F32))
                    for i in range(4)]
                gugen = make_gu_gen(wsc, pguA, pguB, depth=4)
                for _ in range(NGCH * 9 + 20):
                    next(gugen)
            sch.barrier()

        if "B" in PHASES:
            with contextlib.ExitStack() as bc:

                x2T = bc.enter_context(nc.sbuf_tensor("x2T", [128, 16 * S], BF16))
                xt = [bc.enter_context(nc.sbuf_tensor(f"xt{i}", [128, D], F32))
                      for i in range(2)]
                wq = [bc.enter_context(nc.sbuf_tensor(f"wq{i}", [128, 4096], BF16))
                      for i in range(2)]
                kst = [bc.enter_context(nc.sbuf_tensor(f"kst{i}", [128, 512], BF16))
                       for i in range(4)]
                with contextlib.ExitStack() as tc:
                    ptr = [tc.enter_context(
                        nc.psum_tensor(f"ptr{i}", [128, 1024], BF16))
                        for i in range(2)]
                    ln_ev, nmks, trmk = {}, {}, {}

                    def b_transpose(ti):
                        for hc in range(16):
                            p = ptr[hc // 8]
                            if hc % 8 == 0:
                                sch.wait("T", nmks[ti])
                                if ti >= 1:
                                    sch.wait("T", ln_ev.get((ti - 1, hc // 8)))
                            sch.op("T", lambda t, p_=p, h=hc, b=ti % 2:
                                   t.matmul(p_[:, (h % 8) * 128:(h % 8) * 128 + 128],
                                            lnsqs[b][:, h * 128:h * 128 + 128],
                                            ident[:, :], is_transpose=True,
                                            start=(h % 8 == 0), stop=(h % 8 == 7)))
                            if hc % 8 == 7:
                                tmk = sch.mark("T")
                                trmk[ti] = tmk
                                sch.wait("V", tmk)
                                sch.op("V", lambda v, p_=p, t_=ti, g=hc // 8:
                                       v.tensor_copy(
                                           AP(x2T, (g * 8) * S + t_ * 128,
                                              [[16 * S, 128], [S, 8], [1, 128]]),
                                           p_[:, :]))
                                ln_ev[(ti, hc // 8)] = sch.mark("V")

                    for ti in range(16):
                        bufp = ti % 2
                        if ti >= 2:
                            sch.wait("Y", nmks[ti - 2])
                        xtok = sch.dma("Y", xt[bufp][:, :],
                                       x_row[ti * 128:(ti + 1) * 128, :],
                                       f"bx{ti % 2}")
                        sch.wait("V", xtok)
                        sch.wait("S", xtok)
                        if ti >= 2:
                            sch.wait("S", trmk[ti - 2])
                            sch.wait("V", trmk[ti - 2])
                        _emit_ln(sch, col, 8 * bufp, xt[bufp][:, :],
                                 lnsqs[bufp], out_bf=lnsqs[bufp])
                        nmks[ti] = sch.mark("V")
                        if ti >= 1:
                            b_transpose(ti - 1)
                    b_transpose(15)
                    sch.barrier()
                with contextlib.ExitStack() as qc:
                    pq = [qc.enter_context(
                        nc.psum_tensor(f"pq{i}", [128, 512], F32))
                        for i in range(4)]
                    evtix, wtok, kdma, qmmk = {}, {}, {}, {}
                    for oc in range(48):
                        if oc % 2 == 0:
                            if oc >= 4:
                                sch.wait("G", qmmk[oc - 3])
                            wtok[oc // 2] = sch.dma(
                                "G", wq[(oc // 2) % 2][:, :],
                                AP(wqkvT, oc * 128,
                                   [[3 * D, 128], [128 * 3 * D, 16], [1, 256]]),
                                f"wq{(oc // 2) % 2}")
                        tgs = [0, 2] if oc < 16 else [0, 1, 2, 3]
                        kind, hh = oc // 16, oc % 16
                        sch.wait("T", wtok[oc // 2])
                        for tix, tg in enumerate(tgs):
                            sch.wait("T", evtix.get((oc - 1, tix)))
                            for hc in range(16):
                                sch.op("T", lambda t, h=hc, o=oc, g=tg, bi=tix:
                                       t.matmul(
                                           pq[bi][:, :],
                                           wq[(o // 2) % 2][:, h * 256 + (o % 2) * 128:
                                                            h * 256 + (o % 2) * 128 + 128],
                                           x2T[:, h * S + g * 512:h * S + g * 512 + 512],
                                           start=(h == 0), stop=(h == 15)))
                            qmmk[oc] = qmm = sch.mark("T")
                            sch.wait("V", qmm)
                            if kind == 0:
                                dst = q_all[:, hh * TPC + (tg // 2) * 512:
                                            hh * TPC + (tg // 2) * 512 + 512]
                                sch.op("V", lambda v, d=dst, bi=tix, o=oc:
                                       v.tensor_scalar(
                                           d, pq[bi][:, :],
                                           bqkv[:, o:o + 1], None, ALU.add))
                            else:
                                if kdma.get(tix) is not None:
                                    sch.wait("V", kdma[tix])
                                sch.op("V", lambda v, d=kst[tix],
                                       bi=tix, o=oc:
                                       v.tensor_scalar(
                                           d[:, :], pq[bi][:, :],
                                           bqkv[:, o:o + 1], None, ALU.add))
                                vmk = sch.mark("V")
                                sch.wait("Y", vmk)
                                kdma[tix] = sch.dma(
                                    "Y",
                                    AP(kvbuf, ((kind - 1) * 16 + hh) * 128 * S
                                       + tg * 512, [[S, 128], [1, 512]]),
                                    kst[tix][:, :], f"kv{tix}")
                            evtix[(oc, tix)] = sch.mark("V")
                    if gugen is not None:
                        for _ in range(24):
                            next(gugen)
            sch.barrier()

        attnout = qscope.enter_context(
            nc.sbuf_tensor("attnout", [128, H * TPC], BF16))
        if "C" in PHASES:
            with contextlib.ExitStack() as cc:
                cmask = cc.enter_context(
                    nc.sbuf_tensor("cmasksb", [128, 16 * 512], BF16))
                cmtok = sch.dma(
                    "Y", AP(cmask, 0, [[16 * 512, 128], [512, 16], [1, 512]]),
                    AP(cm_d, 0, [[512, 128], [128 * 512, 16], [1, 512]]), "c2")
                sch.wait("V", cmtok)
                kb = [cc.enter_context(nc.sbuf_tensor(f"kb{i}", [128, S], BF16))
                      for i in range(2)]
                vb = [cc.enter_context(nc.sbuf_tensor(f"vb{i}", [128, S], BF16))
                      for i in range(2)]
                vT = [cc.enter_context(nc.sbuf_tensor(f"vT{i}", [128, S], BF16))
                      for i in range(2)]
                pT = [cc.enter_context(nc.sbuf_tensor(f"pT{i}", [128, 16 * 512], BF16))
                      for i in range(2)]
                rs = cc.enter_context(nc.sbuf_tensor("rs", [1, 512], BF16))
                bc_sb = cc.enter_context(nc.sbuf_tensor("bcsb", [128, 512], BF16))
                psc = [cc.enter_context(nc.psum_tensor(f"psc{i}", [128, 512], F32))
                       for i in range(4)]
                pvt = cc.enter_context(nc.psum_tensor("pvt", [128, 1024], BF16))
                pav = [cc.enter_context(nc.psum_tensor(f"pav{i}", [128, 512], F32))
                       for i in range(2)]
                prow = cc.enter_context(nc.psum_tensor("prow", [1, 512], F32))

                vtev = {}
                expmark = {}
                scmark = {}
                attn_emk = {}
                rmk = {}
                pending_tail = [None]

                def emit_tail(it_p):
                    h_p, ts_p = it_p // 2, it_p % 2
                    sch.wait("T", rmk[it_p])
                    if it_p >= 1:
                        sch.wait("T", attn_emk.get(it_p - 1))
                    sch.op("T", lambda t: t.matmul(
                        psc[2][:, :], ones[0:1, 0:128], rs[:, :],
                        start=True, stop=True))
                    bmk = sch.mark("T")
                    sch.wait("S", bmk)
                    sch.op("S", lambda s: s.copy(bc_sb[:, :], psc[2][:, :]))
                    csm = sch.mark("S")
                    sch.wait("V", csm)
                    sch.op("V", lambda v, ts=ts_p, hh=h_p, pa=pav[it_p % 2]:
                           v.tensor_mul(
                               attnout[:, hh * TPC + ts * 512:
                                       hh * TPC + ts * 512 + 512],
                               pa[:, :], bc_sb[:, :]))
                    attn_emk[it_p] = sch.mark("V")

                for h in range(H):
                    hb = h % 2
                    if h >= 2:
                        sch.wait("Y", attn_emk[2 * (h - 2) + 1])
                    ktok = sch.dma("Y", kb[hb][:, :],
                                   AP(kvbuf, h * 128 * S, [[S, 128], [1, S]]),
                                   f"ck{h % 2}")
                    vtok = sch.dma("Y", vb[hb][:, :],
                                   AP(kvbuf, (16 + h) * 128 * S, [[S, 128], [1, S]]),
                                   f"cv{h % 2}")
                    sch.wait("T", vtok)
                    for g2 in range(2):
                        if (h, g2) != (0, 0):
                            prevk = (h, g2 - 1) if g2 else (h - 1, 1)
                            sch.wait("T", vtev[prevk])
                        for tkc in range(8 * g2, 8 * g2 + 8):
                            sch.op("T", lambda t, c=tkc, b=hb:
                                   t.matmul(pvt[:, (c % 8) * 128:(c % 8) * 128 + 128],
                                            vb[b][:, c * 128:c * 128 + 128],
                                            ident[:, :], is_transpose=True,
                                            start=(c % 8 == 0), stop=(c % 8 == 7)))
                        tmk = sch.mark("T")
                        sch.wait("V", tmk)
                        sch.op("V", lambda v, b=hb, g=g2:
                               v.tensor_copy(vT[b][:, g * 1024:(g + 1) * 1024],
                                             pvt[:, :]))
                        vtev[(h, g2)] = sch.mark("V")
                    sch.wait("T", vtev[(h, 0)])
                    sch.wait("T", vtev[(h, 1)])
                    sch.wait("T", ktok)
                    for tslot in range(2):
                        C = CTX[tslot]
                        it = 2 * h + tslot
                        sch.wait("S", attn_emk.get(it - 2))
                        for tkc in range(C):
                            if tkc >= 4:
                                sch.wait("T", expmark[(it, tkc - 4)])
                            elif it >= 1:
                                last = [c for c in range(CTX[(it - 1) % 2])
                                        if c % 4 == tkc % 4]
                                if last:
                                    sch.wait("T", expmark[(it - 1, last[-1])])
                            if tkc % 4 == 2:
                                sch.wait("T", attn_emk.get(it - 1))
                            sch.op("T", lambda t, b=hb, c=tkc, ts=tslot, hh=h:
                                   t.matmul(psc[c % 4][:, :],
                                            kb[b][:, c * 128:c * 128 + 128],
                                            q_all[:, hh * TPC + ts * 512:
                                                  hh * TPC + ts * 512 + 512],
                                            start=True, stop=True))
                            scmark[(it, tkc)] = sch.mark("T")
                            if tkc == 1 and pending_tail[0] is not None:
                                emit_tail(pending_tail[0])
                                pending_tail[0] = None
                            sch.wait("S", scmark[(it, tkc)])
                            sch.op("S", lambda s, b=tslot, c=tkc:
                                   s.activation(pT[b][:, c * 512:c * 512 + 512],
                                                psc[c % 4][:, :], AF.Exp,
                                                bias=0.0, scale=SCALE))
                            expmark[(it, tkc)] = sch.mark("S")
                            if (tslot == 0) or (tkc >= 8):
                                sch.wait("V", expmark[(it, tkc)])
                                sch.op("V", lambda v, b=tslot, c=tkc:
                                       v.tensor_mul(
                                           pT[b][:, c * 512:c * 512 + 512],
                                           pT[b][:, c * 512:c * 512 + 512],
                                           cmask[:, c * 512:c * 512 + 512]))
                                expmark[(it, tkc)] = sch.mark("V")
                            if tkc >= 2:
                                _emit_sumav(sch, it, tkc - 2, C, hb, tslot,
                                            pT, vT, prow, pav, ones, expmark,
                                            rmk, attn_emk)
                        for c2 in (C - 2, C - 1):
                            _emit_sumav(sch, it, c2, C, hb, tslot, pT, vT,
                                        prow, pav, ones, expmark, rmk,
                                        attn_emk)
                        avm = sch.mark("T")
                        sch.wait("V", avm)
                        def _recip(v):
                            with nc.allow_low_precision(reason="softmax 1/s"):
                                return v.reciprocal(rs[:, :], prow[:, :])
                        sch.op("V", _recip)
                        rmk[it] = sch.mark("V")
                        pending_tail[0] = it
                emit_tail(pending_tail[0])
                pending_tail[0] = None
            sch.barrier()
        if "D" in PHASES:
            with contextlib.ExitStack() as dc:
                def pumpd(n=1):
                    pass
                yfeat = dc.enter_context(
                    nc.sbuf_tensor("yfeat", [128, 16 * TPC], BF16))
                wo = [dc.enter_context(nc.sbuf_tensor(f"wosb{i}", [128, 4096], BF16))
                      for i in range(2)]
                ytok = dc.enter_context(nc.sbuf_tensor("ytok", [128, D], F32))
                lno = ytok
                bout = dc.enter_context(nc.sbuf_tensor("boutsb", [128, D], F32))
                xti = [dc.enter_context(nc.sbuf_tensor(f"xtiD{i}", [128, D], F32))
                       for i in range(2)]
                botok = sch.dma("Y", bout[:, :], bout_d[:, :], "c3")
                sch.wait("V", botok)
                with contextlib.ExitStack() as pc:
                    po = [pc.enter_context(
                        nc.psum_tensor(f"po{i}", [128, 512], F32))
                        for i in range(4)]
                    ev, wtok, ommk = {}, {}, {}
                    for oc in range(16):
                        if oc % 2 == 0:
                            if oc >= 4:
                                sch.wait("G", ommk[oc - 3])
                            wtok[oc // 2] = sch.dma(
                                "G", wo[(oc // 2) % 2][:, :],
                                AP(woutT, oc * 128,
                                   [[D, 128], [128 * D, 16], [1, 256]]),
                                f"wo{(oc // 2) % 2}")
                        bks = (oc % 2) * 2
                        sch.wait("T", wtok[oc // 2])
                        if oc >= 2:
                            sch.wait("T", ev[oc - 2])
                        for dcx in range(16):
                            pumpd(1)
                            for tg in range(2):
                                sch.op("T", lambda t, d_=dcx, o=oc, g=tg, bi=bks + tg:
                                       t.matmul(
                                           po[bi][:, :],
                                           wo[(o // 2) % 2][:, d_ * 256 + (o % 2) * 128:
                                                            d_ * 256 + (o % 2) * 128 + 128],
                                           attnout[:, d_ * TPC + g * 512:
                                                   d_ * TPC + g * 512 + 512],
                                           start=(d_ == 0), stop=(d_ == 15)))
                        ommk[oc] = omm = sch.mark("T")
                        sch.wait("S", omm)
                        for tg in range(2):
                            sch.op("S", lambda s, o=oc, g=tg, bi=bks + tg:
                                   s.copy(yfeat[:, o * TPC + g * 512:
                                                o * TPC + g * 512 + 512],
                                          po[bi][:, :]))
                        ev[oc] = sch.mark("S")
                    sch.barrier()
                with contextlib.ExitStack() as pc:
                    ptb = [pc.enter_context(
                        nc.psum_tensor(f"ptb{i}", [128, 1024], BF16))
                        for i in range(2)]
                    prev = {}
                    xst = {}
                    for to in range(8):
                        pumpd(4)
                        if to >= 2:
                            sch.wait("Y", xst[to - 2])
                        xtok = sch.dma("Y", xti[to % 2][:, :],
                                       x_row[_own_tile(to) * 128:
                                             _own_tile(to) * 128 + 128, :],
                                       f"dx{to % 2}")
                        tg, tl = to // 4, to % 4
                        for ocx in range(16):
                            p = ptb[ocx // 8]
                            if ocx % 8 == 0 and to >= 1:
                                sch.wait("T", prev.get((to - 1, ocx // 8)))
                            sch.op("T", lambda t, p_=p, o=ocx, g=tg, l=tl:
                                   t.matmul(p_[:, (o % 8) * 128:(o % 8) * 128 + 128],
                                            yfeat[:, o * TPC + g * 512 + l * 128:
                                                  o * TPC + g * 512 + l * 128 + 128],
                                            ident[:, :], is_transpose=True,
                                            start=(o % 8 == 0), stop=(o % 8 == 7)))
                            if ocx % 8 == 7:
                                tmk = sch.mark("T")
                                sch.wait("V", tmk)
                                if to >= 1:
                                    sch.wait("V", prev[to - 1])
                                sch.op("V", lambda v, p_=p, g=ocx // 8:
                                       v.scalar_tensor_tensor(
                                           ytok[:, g * 1024:(g + 1) * 1024],
                                           p_[:, :], 0.0,
                                           bout[:, g * 1024:(g + 1) * 1024],
                                           ALU.add, ALU.add))
                                prev[(to, ocx // 8)] = sch.mark("V")
                        sch.wait("S", prev[(to, 0)])
                        sch.wait("S", prev[(to, 1)])
                        _emit_ln(sch, col, 8 * (to % 2), ytok[:, :], lnsq,
                                 out_f32=lno)
                        sch.wait("V", xtok)
                        sch.op("V", lambda v, o=to:
                               v.tensor_add(xti[o % 2][:, :], xti[o % 2][:, :],
                                            lno[:, :]))
                        prev[to] = sch.mark("V")
                        sch.wait("Y", prev[to])
                        xst[to] = sch.dma("Y",
                                          xmid[to * 128:(to + 1) * 128, :],
                                          xti[to % 2][:, :], f"dxs{to % 2}")
                    pumpd(400)
            sch.barrier()
        qscope.close()

        if "W" in PHASES:
            with contextlib.ExitStack() as wdc:
                pdnA = [wdc.enter_context(
                    nc.psum_tensor(f"pdnA{i}", [128, 512], F32))
                    for i in range(4)]
                pdnB = [wdc.enter_context(
                    nc.psum_tensor(f"pdnB{i}", [128, 512], F32))
                    for i in range(4)]
                dgen = make_down_gen(wdc, pdnA, pdnB, depth=4)
                for _ in range(NDCH * 9 + 20):
                    next(dgen)
            sch.barrier()

        if "E" in PHASES:
            with contextlib.ExitStack() as ec:
                x2f = ec.enter_context(nc.sbuf_tensor("x2f", [128, 16 * 512], BF16))
                act = ec.enter_context(nc.sbuf_tensor("actsb", [128, 64 * 512], BF16))
                wg = [ec.enter_context(nc.sbuf_tensor(f"wgsb{i}", [128, 2048], BF16))
                      for i in range(4)]
                wu = [ec.enter_context(nc.sbuf_tensor(f"wusb{i}", [128, 2048], BF16))
                      for i in range(4)]
                wd = [ec.enter_context(nc.sbuf_tensor(f"wdsb{i}", [128, 2048], BF16))
                      for i in range(2)]
                xtie = [ec.enter_context(nc.sbuf_tensor(f"xtiE{i}", [128, D], F32))
                       for i in range(2)]
                sil = [ec.enter_context(nc.sbuf_tensor(f"silsb{i}", [128, 512], BF16))
                       for i in range(4)]
                y2f = ec.enter_context(nc.sbuf_tensor("y2f", [128, 16 * 512], BF16))
                ytoke = ec.enter_context(nc.sbuf_tensor("ytokE", [128, D], F32))
                lnoe = ytoke

                for g in range(2):
                    sch.barrier()
                    with contextlib.ExitStack() as pc:
                        ptr = [pc.enter_context(
                            nc.psum_tensor(f"ptrE{g}{i}", [128, 1024], BF16))
                            for i in range(2)]
                        pev, nmks, trmk = {}, {}, {}

                        def e_transpose(tl):
                            for hc in range(16):
                                p = ptr[hc // 8]
                                if hc % 8 == 0:
                                    sch.wait("T", nmks[tl])
                                    if tl >= 1:
                                        sch.wait("T", pev.get((tl - 1, hc // 8)))
                                sch.op("T", lambda t, p_=p, h=hc, b=tl % 2:
                                       t.matmul(p_[:, (h % 8) * 128:(h % 8) * 128 + 128],
                                                lnsqs[b][:, h * 128:h * 128 + 128],
                                                ident[:, :], is_transpose=True,
                                                start=(h % 8 == 0), stop=(h % 8 == 7)))
                                if hc % 8 == 7:
                                    tmk = sch.mark("T")
                                    trmk[tl] = tmk
                                    sch.wait("V", tmk)
                                    sch.op("V", lambda v, p_=p, t_=tl, gg=hc // 8:
                                           v.tensor_copy(
                                               AP(x2f, (gg * 8) * 512 + t_ * 128,
                                                  [[16 * 512, 128], [512, 8], [1, 128]]),
                                               p_[:, :]))
                                    pev[(tl, hc // 8)] = sch.mark("V")

                        for tl in range(4):
                            to = g * 4 + tl
                            if tl >= 2:
                                sch.wait("Y", nmks[tl - 2])
                            xtok = sch.dma("Y", xtie[tl % 2][:, :],
                                           xmid[to * 128:(to + 1) * 128, :],
                                           f"ex{tl % 2}")
                            sch.wait("V", xtok)
                            sch.wait("S", xtok)
                            if tl >= 2:
                                sch.wait("S", trmk[tl - 2])
                                sch.wait("V", trmk[tl - 2])
                            _emit_ln(sch, col, 8 * (tl % 2),
                                     xtie[tl % 2][:, :], lnsqs[tl % 2],
                                     out_bf=lnsqs[tl % 2])
                            nmks[tl] = sch.mark("V")
                            if tl >= 1:
                                e_transpose(tl - 1)
                        e_transpose(3)
                        sch.barrier()
                    with contextlib.ExitStack() as pc:
                        pg = [pc.enter_context(
                            nc.psum_tensor(f"pgE{g}{i}", [128, 512], F32))
                            for i in range(4)]
                        pu = [pc.enter_context(
                            nc.psum_tensor(f"puE{g}{i}", [128, 512], F32))
                            for i in range(4)]
                        ev, wtokg, wtoku, gmmk = {}, {}, {}, {}
                        for oc in range(64):
                            if oc >= 4:
                                sch.wait("Y", gmmk[oc - 4])
                            wtokg[oc] = sch.dma(
                                "Y", wg[oc % 4][:, :],
                                AP(wbuf_g, oc * GCH, [[NBLK, 128], [1, GCH]]),
                                f"eg{oc % 4}")
                            wtoku[oc] = sch.dma(
                                "Y", wu[oc % 4][:, :],
                                AP(wbuf_u, oc * GCH, [[NBLK, 128], [1, GCH]]),
                                f"eu{oc % 4}")
                            bkg = oc % 4
                            sch.wait("T", wtokg[oc])
                            sch.wait("T", wtoku[oc])
                            if oc >= 4:
                                sch.wait("T", ev[oc - 4])
                            for hc in range(16):
                                sch.op("T", lambda t, h=hc, o=oc, bi=bkg:
                                       t.matmul(
                                           pg[bi][:, :],
                                           wg[o % 4][:, h * 128:h * 128 + 128],
                                           x2f[:, h * 512:h * 512 + 512],
                                           start=(h == 0), stop=(h == 15)))
                                sch.op("T", lambda t, h=hc, o=oc, bi=bkg:
                                       t.matmul(
                                           pu[bi][:, :],
                                           wu[o % 4][:, h * 128:h * 128 + 128],
                                           x2f[:, h * 512:h * 512 + 512],
                                           start=(h == 0), stop=(h == 15)))
                            gmmk[oc] = gmm = sch.mark("T")
                            sch.wait("S", gmm)
                            if oc >= 4:
                                sch.wait("S", ev[oc - 4])
                            sch.op("S", lambda s, bi=bkg, o=oc:
                                   s.activation(sil[o % 4][:, :], pg[bi][:, :],
                                                AF.Silu, bias=0.0, scale=1.0))
                            smk = sch.mark("S")
                            sch.wait("V", smk)
                            sch.op("V", lambda v, o=oc, bi=bkg:
                                   v.tensor_mul(act[:, o * 512:o * 512 + 512],
                                                sil[o % 4][:, :], pu[bi][:, :]))
                            ev[oc] = sch.mark("V")
                        sch.barrier()
                    with contextlib.ExitStack() as pc:
                        pd = [pc.enter_context(
                            nc.psum_tensor(f"pdE{g}{i}", [128, 512], F32))
                            for i in range(4)]
                        ev, wtokd, dmmk = {}, {}, {}
                        for hcp in range(8):
                            bks = (hcp % 2) * 2
                            if hcp >= 2:
                                sch.wait("T", ev[hcp - 2])
                            for ic in range(64):
                                if ic % 8 == 0:
                                    icq = ic // 8
                                    prevq = (hcp, icq - 2) if icq >= 2 else \
                                            ((hcp - 1, 6 + icq) if hcp >= 1 else None)
                                    if prevq is not None:
                                        sch.wait("Y", dmmk.get(prevq))
                                    wtokd[(hcp, icq)] = sch.dma(
                                        "Y", wd[icq % 2][:, :],
                                        AP(wbuf_d, icq * 8 * D + hcp * 256,
                                           [[64 * D, 128], [D, 8], [1, 256]]),
                                        f"ed{icq % 2}")
                                    sch.wait("T", wtokd[(hcp, icq)])
                                for o2 in range(2):
                                    sch.op("T", lambda t, i_=ic, o=o2, bi=bks:
                                           t.matmul(
                                               pd[bi + o][:, :],
                                               wd[(i_ // 8) % 2][:, (i_ % 8) * 256 + o * 128:
                                                            (i_ % 8) * 256 + o * 128 + 128],
                                               act[:, i_ * 512:i_ * 512 + 512],
                                               start=(i_ == 0), stop=(i_ == 63)))
                                if ic % 8 == 7:
                                    dmmk[(hcp, ic // 8)] = sch.mark("T")
                            dmm = sch.mark("T")
                            sch.wait("S", dmm)
                            if hcp >= 2:
                                sch.wait("S", ev[hcp - 2])
                            for o2 in range(2):
                                sch.op("S", lambda s, h=hcp, o=o2, bi=bks:
                                       s.copy(y2f[:, (2 * h + o) * 512:
                                                  (2 * h + o) * 512 + 512],
                                              pd[bi + o][:, :]))
                            ev[hcp] = sch.mark("S")
                        sch.barrier()
                    with contextlib.ExitStack() as pc:
                        ptb = [pc.enter_context(
                            nc.psum_tensor(f"ptbE{g}{i}", [128, 1024], BF16))
                            for i in range(2)]
                        prev = {}
                        ost = {}
                        for tl in range(4):
                            to = g * 4 + tl
                            if tl >= 2:
                                sch.wait("Y", ost[tl - 2])
                            xtok = sch.dma("Y", xtie[tl % 2][:, :],
                                           xmid[to * 128:(to + 1) * 128, :],
                                           f"e4x{tl % 2}")
                            for ocx in range(16):
                                p = ptb[ocx // 8]
                                if ocx % 8 == 0 and tl >= 1:
                                    sch.wait("T", prev.get((tl - 1, ocx // 8)))
                                sch.op("T", lambda t, p_=p, o=ocx, l=tl:
                                       t.matmul(p_[:, (o % 8) * 128:(o % 8) * 128 + 128],
                                                y2f[:, o * 512 + l * 128:
                                                    o * 512 + l * 128 + 128],
                                                ident[:, :], is_transpose=True,
                                                start=(o % 8 == 0), stop=(o % 8 == 7)))
                                if ocx % 8 == 7:
                                    tmk = sch.mark("T")
                                    sch.wait("S", tmk)
                                    if tl >= 1:
                                        sch.wait("S", prev[tl - 1])
                                    sch.op("S", lambda s, p_=p, gg=ocx // 8:
                                           s.copy(ytoke[:, gg * 1024:(gg + 1) * 1024],
                                                  p_[:, :]))
                                    prev[(tl, ocx // 8)] = sch.mark("S")
                            sch.wait("V", prev[(tl, 0)])
                            sch.wait("V", prev[(tl, 1)])
                            _emit_ln(sch, col, 8 * (tl % 2), ytoke[:, :], lnsq,
                                     out_f32=lnoe)
                            sch.wait("V", xtok)
                            sch.op("V", lambda v, l=tl:
                                   v.tensor_add(xtie[l % 2][:, :],
                                                xtie[l % 2][:, :], lnoe[:, :]))
                            fmk = sch.mark("V")
                            prev[tl] = fmk
                            sch.wait("Y", fmk)
                            ost[tl] = sch.dma(
                                "Y", out_d[to * 128:(to + 1) * 128, :],
                                xtie[tl % 2][:, :], f"out{tl % 2}")

        sch.finish("Y")
        sch.emit(nc, ctx)

    return nc


_TPR = {}


def _perm_gu():
    if "gu" not in _TPR:
        p = np.arange(GCH).reshape(GCH // 16, 16).T.ravel()
        _TPR["gu"] = (np.arange(NBLK).reshape(-1, GCH)[:, p]).ravel()
    return _TPR["gu"]


def _perm_dn():
    if "dn" not in _TPR:
        _TPR["dn"] = (np.arange(D)[None, :] * 64
                      + np.arange(64)[:, None]).ravel()
    return _TPR["dn"]


def _tau_tiled():
    cst = _consts()
    tp = cst["taupow"]
    return np.concatenate([np.tile(tp[0:4], (4, 1)), np.tile(tp[4:8], (4, 1))])


def _tpr0():
    if 0 not in _TPR:
        _TPR[0] = np.ascontiguousarray(_tau_tiled()[:, _perm_gu()])
    return _TPR[0]


def _tpr1():
    if 1 not in _TPR:
        _TPR[1] = np.ascontiguousarray(_tau_tiled()[:, _perm_dn()])
    return _TPR[1]


def _get_program():
    global _PROG
    if _PROG is None:
        _PROG = build_program()
    return _PROG


def kernel(L0_attn_x, L0_attn_w_qkv, L0_attn_b_qkv, L0_attn_w_out,
           L0_attn_b_out, L0_ffn_gate_cp, L0_ffn_up_cp, L0_ffn_down_cp):
    cst = _consts()
    nc = _get_program()

    x = np.ascontiguousarray(np.asarray(L0_attn_x, dtype=np.float32))
    wqkvT = np.ascontiguousarray(np.asarray(L0_attn_w_qkv, np.float32).T)
    woutT = np.ascontiguousarray(np.asarray(L0_attn_w_out, np.float32).T)
    bqkv = np.ascontiguousarray(
        np.asarray(L0_attn_b_qkv, np.float32).reshape(48, 128).T)
    bout_rep = np.ascontiguousarray(
        np.broadcast_to(np.asarray(L0_attn_b_out, np.float32)[None, :],
                        (128, D)))
    cps = [np.asarray(c, np.float32) for c in
           (L0_ffn_gate_cp, L0_ffn_up_cp, L0_ffn_down_cp)]
    taps = [c[cst["tapidx"]] for c in cps]
    tap_gu = np.ascontiguousarray(
        np.concatenate([taps[0], taps[1]])[:, _perm_gu()])
    tap_dn = np.ascontiguousarray(taps[2][:, _perm_dn()])
    wgm_gu = np.ascontiguousarray(cst["wgmask"][:, _perm_gu()])
    wgm_dn = np.ascontiguousarray(cst["wgmask"][:, _perm_dn()])

    in_maps = []
    for c in range(NCORES):
        b, j = c // 2, c % 2
        gk = [j, 1 - j, 2 + j, 3 - j]
        xr = np.ascontiguousarray(x[b].reshape(4, 512, D)[gk].reshape(S, D))
        in_maps.append(dict(
            x_row=xr, wqkvT=wqkvT, bqkv=bqkv, woutT=woutT,
            bout_rep=bout_rep, tap_gu=tap_gu, tap_dn=tap_dn,
            tau_gu=_tpr0(), tau_dn=_tpr1(),
            atilde=cst["atilde"], wgmask_gu=wgm_gu, wgmask_dn=wgm_dn,
            cmask=np.ascontiguousarray(cst["cmask"][j]),
            ident=cst["ident"], ones=cst["ones"],
        ))

    res = run_bass_kernel_spmd(nc, in_maps, core_ids=list(range(NCORES)))

    out = np.empty((B, S, D), dtype=np.float32)
    for c in range(NCORES):
        b, j = c // 2, c % 2
        o = res.results[c]["out"]
        out[b, j * 512:(j + 1) * 512] = o[0:512]
        out[b, (2 + j) * 512:(3 + j) * 512] = o[512:1024]
    return out


# version 24
# speedup vs baseline: 1.3949x; 1.0072x over previous
import contextlib
import math

import numpy as np
import ml_dtypes

import concourse.bass as bass
import concourse.mybir as mybir
from concourse.bass_utils import run_bass_kernel_spmd

F32 = mybir.dt.float32
BF16 = mybir.dt.bfloat16
AF = mybir.ActivationFunctionType
ALU = mybir.AluOpType
AX = mybir.AxisListType
AP = bass.AP

B, S, D = 4, 2048, 2048
H, HD = 16, 128
INTER = 8192
NCTRL = 130156
NSAMP = INTER * D
NBLK = NSAMP // 128
GCH = 2048
NGCH = NBLK // GCH
DCH = 4096
NDCH = NBLK // DCH
TPC = 1024
SCALE = 1.0 / math.sqrt(HD)
INV_D = 1.0 / D
EPS = 1e-5
NCORES = 8
CTX = (8, 16)


_CONSTS = None


def _consts():
    global _CONSTS
    if _CONSTS is not None:
        return _CONSTS
    t = np.linspace(0.0, NCTRL - 1.0, NSAMP, dtype=np.float64)
    iarr = np.clip(np.floor(t).astype(np.int64), 0, NCTRL - 2)
    i_b = iarr[::128]
    tau = t[::128] - i_b
    s_b = (iarr.reshape(NBLK, 128) == i_b[:, None]).sum(1)
    R = (NCTRL - 1.0) / (NSAMP - 1)

    Cc = np.array([
        [0.0, -0.5, 1.0, -0.5],
        [1.0, 0.0, -2.5, 1.5],
        [0.0, 0.5, 2.0, -1.5],
        [0.0, 0.0, -0.5, 0.5]], dtype=np.float64)
    from math import comb
    qR = np.arange(128, dtype=np.float64) * R
    at = np.zeros((16, 128), dtype=np.float64)
    for k in range(4):
        for m in range(4):
            v = np.zeros_like(qR)
            for jj in range(m, 4):
                v += comb(jj, m) * Cc[k, jj] * qR ** (jj - m)
            at[4 * k + m] = v
    tp = np.zeros((8, NBLK), dtype=np.float64)
    for m in range(4):
        tp[m] = tau ** m
        tp[4 + m] = (tau - 1.0) ** m
    mask = (np.arange(128)[:, None] < s_b[None, :])
    tapidx = np.stack([np.clip(i_b + r - 1, 0, NCTRL - 1) for r in range(5)])

    cmask = {}
    for j in range(2):
        gk = [j, 1 - j, 2 + j, 3 - j]
        cm = np.zeros((16, 128, 512), dtype=np.float64)
        for cr in range(16):
            tslot = 0 if cr < 8 else 1
            gt = 2 * tslot + j
            tk = gk[cr // 4] * 512 + (cr % 4) * 128 + np.arange(128)[:, None]
            tq = gt * 512 + np.arange(512)[None, :]
            cm[cr] = (tk <= tq)
        cmask[j] = cm.astype(ml_dtypes.bfloat16)

    r0_tap = np.zeros(128, dtype=np.int64)
    r0_wt = np.zeros(128, dtype=np.int64)
    r0_tau = np.zeros(128, dtype=np.int64)
    for p in range(128):
        si, r32 = p // 32, p % 32
        k, m = (r32 % 16) // 4, r32 % 4
        r0_wt[p] = si // 2
        r0_tap[p] = k + (si % 2)
        r0_tau[p] = (si % 2) * 4 + m
    r1_tap = np.zeros(64, dtype=np.int64)
    r1_tau = np.zeros(64, dtype=np.int64)
    for p in range(64):
        si, r32 = p // 32, p % 32
        k, m = (r32 % 16) // 4, r32 % 4
        r1_tap[p] = k + si
        r1_tau[p] = si * 4 + m
    _CONSTS = dict(
        tapidx=tapidx, s_b=s_b,
        r0_tap=r0_tap, r0_wt=r0_wt, r0_tau=r0_tau,
        r1_tap=r1_tap, r1_tau=r1_tau,
        atilde=at.astype(np.float32),
        taupow=tp.astype(np.float32),
        wgmask=mask.astype(np.uint8),
        ident=np.eye(128, dtype=ml_dtypes.bfloat16),
        ones=np.ones((128, 128), dtype=ml_dtypes.bfloat16),
        cmask=cmask,
    )
    return _CONSTS


class Sched:
    ENGINES = ("T", "V", "S", "G", "Y")

    def __init__(self):
        self.recs = {e: [] for e in self.ENGINES}
        self.cnt = {e.lower(): 0 for e in self.ENGINES}
        self.pend = {e: [] for e in self.ENGINES}
        self.streams = {}
        self.issuer = {}
        self.open_streams = {e: set() for e in self.ENGINES}

    CHAINED = ("V", "S", "G")

    def op(self, e, fn):
        w = list(self.pend[e])
        self.pend[e] = []
        incs = []
        if e in self.CHAINED:
            sem = e.lower()
            if self.cnt[sem]:
                w.append((sem, self.cnt[sem]))
            self.cnt[sem] += 1
            incs.append((sem, 1, self.cnt[sem]))
        self.recs[e].append([fn, tuple(w), incs, False])

    def mark(self, e):
        rec = self.recs[e][-1]
        if rec[3]:
            sem, _n, tgt = rec[2][0]
            return (sem, tgt)
        if rec[2]:
            sem, _n, tgt = rec[2][0]
            return (sem, tgt)
        sem = e.lower()
        self.cnt[sem] += 1
        rec[2].append((sem, 1, self.cnt[sem]))
        return (sem, self.cnt[sem])

    def dma(self, e, out, in_, stream):
        sname = "q_" + stream
        assert self.issuer.setdefault(sname, e) == e, (sname, e)
        self.streams[sname] = self.streams.get(sname, 0) + 16
        tgt = self.streams[sname]
        self.open_streams[e].add(sname)
        w = tuple(self.pend[e])
        self.pend[e] = []
        self.recs[e].append(
            [lambda eng, o=out, i=in_: eng.dma_start(out=o, in_=i), w,
             [(sname, 16, tgt)], True])
        return (sname, tgt)

    def wait(self, e, tok):
        if tok is not None:
            self.pend[e].append(tok)

    def barrier(self):
        toks = []
        for e in self.ENGINES:
            if not self.recs[e] and not self.open_streams[e]:
                continue
            if self.open_streams[e]:
                for sname in sorted(self.open_streams[e]):
                    self.pend[e].append((sname, self.streams[sname]))
                self.open_streams[e] = set()
                self.op(e, lambda eng: eng.nop())
            toks.append(self.mark(e))
        for e in self.ENGINES:
            for t in toks:
                self.pend[e].append(t)

    def finish(self, e):
        for sname in sorted(self.open_streams[e]):
            self.pend[e].append((sname, self.streams[sname]))
        self.op(e, lambda eng: eng.nop())

    def emit(self, nc, ctx):
        sems = {}
        for sn in self.cnt:
            sems[sn] = ctx.enter_context(nc.semaphore("sem_" + sn))
        for sn in self.streams:
            sems[sn] = ctx.enter_context(nc.semaphore(sn))

        def replay(e):
            def f(eng):
                for fn, waits, incs, _isdma in self.recs[e]:
                    agg = {}
                    for (sn, v) in waits:
                        agg[sn] = max(agg.get(sn, 0), v)
                    for sn, v in agg.items():
                        eng.wait_ge(sems[sn], v)
                    inst = fn(eng)
                    for (sn, n, _tgt) in incs:
                        inst.then_inc(sems[sn], n)
            return f
        with nc.Block() as blk:
            blk.tensor(replay("T"))
            blk.vector(replay("V"))
            blk.scalar(replay("S"))
            blk.gpsimd(replay("G"))
            blk.sync(replay("Y"))


def _emit_sumav(sch, it, c, C, hb, tslot, pT, vT, prow, pav, ones, expmark,
                rmk, attn_emk):
    sch.wait("T", expmark[(it, c)])
    if c == 0:
        sch.wait("T", rmk.get(it - 1))
        sch.wait("T", attn_emk.get(it - 2))
    sch.op("T", lambda t, b=tslot, c_=c, C_=C:
           t.matmul(prow[:, :], ones[:, 0:1],
                    pT[b][:, c_ * 512:c_ * 512 + 512],
                    start=(c_ == 0), stop=(c_ == C_ - 1)))
    sch.op("T", lambda t, b=hb, bb=tslot, c_=c, C_=C, pa=pav[it % 2]:
           t.matmul(pa[:, :], vT[b][:, c_ * 128:c_ * 128 + 128],
                    pT[bb][:, c_ * 512:c_ * 512 + 512],
                    start=(c_ == 0), stop=(c_ == C_ - 1)))


def _own_tile(to):
    return to if to < 4 else to + 4


def _emit_ln(sch, col, c0, xap, sqscratch, out_bf=None, out_f32=None):
    sch.op("V", lambda v: v.reduce_sum(col[:, c0:c0 + 1], xap, axis=AX.X))
    sch.op("S", lambda s: s.activation(sqscratch[:, :], xap, AF.Square,
                                       bias=0.0, scale=1.0,
                                       accum_out=col[:, c0 + 1:c0 + 2]))
    sqm = sch.mark("S")
    sch.op("V", lambda v: v.tensor_scalar(col[:, c0 + 2:c0 + 3],
                                          col[:, c0:c0 + 1], INV_D, None,
                                          ALU.mult))
    sch.op("V", lambda v: v.tensor_mul(col[:, c0 + 3:c0 + 4],
                                       col[:, c0 + 2:c0 + 3],
                                       col[:, c0 + 2:c0 + 3]))
    sch.wait("V", sqm)
    sch.op("V", lambda v: v.tensor_scalar(col[:, c0 + 4:c0 + 5],
                                          col[:, c0 + 1:c0 + 2], INV_D,
                                          col[:, c0 + 3:c0 + 4],
                                          ALU.mult, ALU.subtract))
    vmk = sch.mark("V")
    sch.wait("S", vmk)
    sch.op("S", lambda s: s.activation(col[:, c0 + 5:c0 + 6],
                                       col[:, c0 + 4:c0 + 5], AF.Sqrt,
                                       bias=EPS, scale=1.0))
    smk = sch.mark("S")
    sch.wait("V", smk)
    sch.op("V", lambda v: v.reciprocal(col[:, c0 + 6:c0 + 7],
                                       col[:, c0 + 5:c0 + 6]))
    sch.op("V", lambda v: v.tensor_mul(col[:, c0 + 7:c0 + 8],
                                       col[:, c0 + 2:c0 + 3],
                                       col[:, c0 + 6:c0 + 7]))
    if out_bf is not None:
        sch.op("V", lambda v: v.tensor_scalar(out_bf[:, :], xap,
                                              col[:, c0 + 6:c0 + 7],
                                              col[:, c0 + 7:c0 + 8],
                                              ALU.mult, ALU.subtract))
    if out_f32 is not None:
        sch.op("V", lambda v: v.tensor_scalar(out_f32[:, :], xap,
                                              col[:, c0 + 6:c0 + 7],
                                              col[:, c0 + 7:c0 + 8],
                                              ALU.mult, ALU.subtract))


_PROG = None
PHASES = ("W", "B", "C", "D", "E")


def build_program():
    nc = bass.Bass("TRN2", target_bir_lowering=False)
    sch = Sched()

    x_row = nc.dram_tensor("x_row", [S, D], F32, kind="ExternalInput")
    wqkvT = nc.dram_tensor("wqkvT", [D, 3 * D], F32, kind="ExternalInput")
    bqkv_d = nc.dram_tensor("bqkv", [128, 48], F32, kind="ExternalInput")
    woutT = nc.dram_tensor("woutT", [D, D], F32, kind="ExternalInput")
    bout_d = nc.dram_tensor("bout_rep", [128, D], F32, kind="ExternalInput")
    tap_gu_d = nc.dram_tensor("tap_gu", [10, NBLK], F32, kind="ExternalInput")
    tau_gu_d = nc.dram_tensor("tau_gu", [32, NBLK], F32, kind="ExternalInput")
    tap_dn_d = nc.dram_tensor("tap_dn", [5, NBLK], F32, kind="ExternalInput")
    tau_dn_d = nc.dram_tensor("tau_dn", [32, NBLK], F32, kind="ExternalInput")
    at_d = nc.dram_tensor("atilde", [16, 128], F32, kind="ExternalInput")
    wgm_gu_d = nc.dram_tensor("wgmask_gu", [128, NBLK], mybir.dt.uint8,
                              kind="ExternalInput")
    wgm_dn_d = nc.dram_tensor("wgmask_dn", [128, NBLK], mybir.dt.uint8,
                              kind="ExternalInput")
    cm_d = nc.dram_tensor("cmask", [16, 128, 512], BF16, kind="ExternalInput")
    id_d = nc.dram_tensor("ident", [128, 128], BF16, kind="ExternalInput")
    on_d = nc.dram_tensor("ones", [128, 128], BF16, kind="ExternalInput")
    out_d = nc.dram_tensor("out", [TPC, D], F32, kind="ExternalOutput")

    wbuf_g = nc.dram_tensor("wbuf_g", [128, NBLK], BF16)
    wbuf_u = nc.dram_tensor("wbuf_u", [128, NBLK], BF16)
    wbuf_d = nc.dram_tensor("wbuf_d", [128, 64, D], BF16)
    kvbuf = nc.dram_tensor("kvbuf", [2 * H, 128, S], BF16)
    xmid = nc.dram_tensor("xmid", [TPC, D], F32)

    ctx = contextlib.ExitStack()
    with ctx:
        csb = ctx.enter_context(nc.sbuf_tensor("constsb", [128, 4], F32))
        for ci, cv in enumerate((0.0, EPS, 1.0, SCALE)):
            nc.const_aps.register(F32, float(cv), csb[:, ci:ci + 1])
            sch.op("V", lambda v, i=ci, val=float(cv):
                   v.memset(csb[:, i:i + 1], val))
        ident = ctx.enter_context(nc.sbuf_tensor("identsb", [128, 128], BF16))
        ones = ctx.enter_context(nc.sbuf_tensor("onessb", [128, 128], BF16))
        at_sb = ctx.enter_context(nc.sbuf_tensor("atsb", [128, 128], BF16))
        bqkv = ctx.enter_context(nc.sbuf_tensor("bqkvsb", [128, 48], F32))
        col = ctx.enter_context(nc.sbuf_tensor("colsb", [128, 16], F32))
        lnsq = ctx.enter_context(nc.sbuf_tensor("lnsqsb", [128, D], BF16))
        lnsq2 = ctx.enter_context(nc.sbuf_tensor("lnsqsb2", [128, D], BF16))
        lnsqs = [lnsq, lnsq2]

        sch.dma("Y", ident[:, :], id_d[:, :], "c0")
        sch.dma("Y", ones[:, :], on_d[:, :], "c0")
        sch.dma("Y", bqkv[:, :], bqkv_d[:, :], "c0")
        for u in (0, 32, 64, 96):
            sch.dma("G", at_sb[u:u + 16, :], at_d[:, :], "c1")
        sch.barrier()

        def make_gu_gen(wc, pA, pB, depth):
            CT = [wc.enter_context(nc.sbuf_tensor(f"gct{i}", [42, GCH], BF16))
                  for i in range(3)]
            U = [wc.enter_context(nc.sbuf_tensor(f"Ug{i}", [128, GCH], BF16))
                 for i in range(2)]
            IN0 = [wc.enter_context(nc.sbuf_tensor(f"gi0_{i}", [128, GCH], BF16))
                   for i in range(2)]
            IN1 = [wc.enter_context(nc.sbuf_tensor(f"gi1_{i}", [128, GCH], BF16))
                   for i in range(2)]
            MSK = [wc.enter_context(
                nc.sbuf_tensor(f"gm_{i}", [128, GCH], mybir.dt.uint8))
                for i in range(4)]
            WO = [wc.enter_context(nc.sbuf_tensor(f"gwo_{i}", [128, GCH], BF16))
                  for i in range(4)]
            streams = [(wbuf_g, 0, 32), (wbuf_u, 64, 96)]
            for t in IN0 + IN1:
                sch.op("V", lambda v, t_=t: v.memset(t_[:, :], 0.0))
            msmk_gu = sch.mark("V")

            def ldtoks(c):
                sch.dma("G", CT[c % 3][0:10, :],
                        AP(tap_gu_d, c * GCH, [[NBLK, 10], [1, GCH]]),
                        f"wg{c % 3}")
                return sch.dma("G", CT[c % 3][10:42, :],
                               AP(tau_gu_d, c * GCH, [[NBLK, 32], [1, GCH]]),
                               f"wg{c % 3}")

            def gen():
                umark, evmark, stok, mtoks = {}, {}, {}, {}
                ltok, etok = {}, {}

                def expand(c):
                    sch.wait("G", ltok[c])
                    if c >= 2:
                        sch.wait("G", umark[c - 2])
                    else:
                        sch.wait("G", msmk_gu)
                    for si in range(4):
                        wt, dlt = si // 2, si % 2
                        sch.dma("G", AP(IN0[c % 2], (si * 32) * GCH,
                                        [[GCH, 16], [1, GCH]]),
                                AP(CT[c % 3], (wt * 5 + dlt) * GCH,
                                   [[GCH, 4], [0, 4], [1, GCH]]),
                                f"we{c % 2}")
                        etok[c] = sch.dma(
                            "G", AP(IN1[c % 2], (si * 32) * GCH,
                                    [[GCH, 16], [1, GCH]]),
                            AP(CT[c % 3], (10 + dlt * 16) * GCH,
                               [[GCH, 16], [1, GCH]]),
                            f"we{c % 2}")

                def umult(c):
                    sch.wait("V", etok[c])
                    if c >= 2:
                        sch.wait("V", evmark[(c - 2) * 8 + 7])
                    sch.op("V", lambda v, b=c % 2: v.tensor_mul(
                        U[b][:, :], IN0[b][:, :], IN1[b][:, :]))
                    umark[c] = sch.mark("V")

                for c in range(NGCH):
                    yield
                    buf = c % 2
                    if c == 0:
                        for cpre in (0, 1):
                            mtoks[cpre] = sch.dma(
                                "G", MSK[cpre % 4][:, :],
                                wgm_gu_d[:, cpre * GCH:(cpre + 1) * GCH],
                                f"gm{cpre % 4}")
                        ltok[0] = ldtoks(0)
                        ltok[1] = ldtoks(1)
                        expand(0)
                        umult(0)
                    if c + 2 < NGCH:
                        if c >= 1:
                            sch.wait("G", etok[c - 1])
                        ltok[c + 2] = ldtoks(c + 2)
                    sch.wait("T", umark[c])
                    for w in range(2):
                        wb, uA, uB = streams[w]
                        cw = c * 2 + w
                        first_ev = True
                        for q4 in range(4):
                            yield
                            i2 = c * 8 + w * 4 + q4
                            bk = i2 % depth
                            if i2 >= depth:
                                sch.wait("T", evmark[i2 - depth])
                            for u, po in ((uA, pA[bk]), (uB, pB[bk])):
                                sch.op("T", lambda t, u_=u, b=buf, pp=po,
                                       cc=q4 * 512:
                                       t.matmul(pp[:, :],
                                                at_sb[u_:u_ + 16, :],
                                                U[b][u_:u_ + 16, cc:cc + 512],
                                                start=True, stop=True,
                                                tile_position=(u_, 0)))
                            mm = sch.mark("T")
                            sch.wait("S", mm)
                            if first_ev:
                                if cw >= 4:
                                    sch.wait("S", stok[cw - 4])
                                if w == 0:
                                    sch.wait("V", mtoks[c])
                                first_ev = False
                            wo_ap = WO[cw % 4][:, q4 * 512:(q4 + 1) * 512]
                            mk_ap = MSK[c % 4][:, q4 * 512:(q4 + 1) * 512]
                            sch.op("S", lambda s, o_=wo_ap, pp=pB[bk]:
                                   s.copy(o_, pp[:, :]))
                            smk = sch.mark("S")
                            sch.wait("V", smk)
                            sch.op("V", lambda v, o_=wo_ap, m_=mk_ap,
                                   pp=pA[bk]:
                                   v.copy_predicated(o_, m_, pp[:, :]))
                            evmark[i2] = sch.mark("V")
                            if w == 0 and q4 == 1 and c + 1 < NGCH:
                                expand(c + 1)
                        sch.wait("Y", evmark[c * 8 + w * 4 + 3])
                        stok[cw] = sch.dma(
                            "Y", wb[:, c * GCH:(c + 1) * GCH],
                            WO[cw % 4][:, :],
                            f"ws{cw % 4}")
                        if w == 0 and c + 1 < NGCH:
                            umult(c + 1)
                    if c + 2 < NGCH:
                        if c >= 2:
                            sch.wait("G", evmark[(c - 2) * 8 + 7])
                        mtoks[c + 2] = sch.dma(
                            "G", MSK[(c + 2) % 4][:, :],
                            wgm_gu_d[:, (c + 2) * GCH:(c + 3) * GCH],
                            f"gm{(c + 2) % 4}")
                while True:
                    yield
            return gen()

        def make_down_gen(wc, pA, pB, depth):
            CT = [wc.enter_context(nc.sbuf_tensor(f"dct{i}", [37, DCH], BF16))
                  for i in range(3)]
            Ud = [wc.enter_context(nc.sbuf_tensor(f"Ud{i}", [64, DCH], BF16))
                  for i in range(2)]
            IN0 = [wc.enter_context(nc.sbuf_tensor(f"di0_{i}", [64, DCH], BF16))
                   for i in range(2)]
            IN1 = [wc.enter_context(nc.sbuf_tensor(f"di1_{i}", [64, DCH], BF16))
                   for i in range(2)]
            MSK = [wc.enter_context(
                nc.sbuf_tensor(f"dm_{i}", [128, DCH], mybir.dt.uint8))
                for i in range(4)]
            WO = [wc.enter_context(nc.sbuf_tensor(f"dwo_{i}", [128, DCH], BF16))
                  for i in range(4)]
            NCG = DCH // 512
            for t in IN0 + IN1:
                sch.op("V", lambda v, t_=t: v.memset(t_[:, :], 0.0))
            msmk_dn = sch.mark("V")

            def ldtoks(c):
                sch.dma("G", CT[c % 3][0:5, :],
                        AP(tap_dn_d, c * DCH, [[NBLK, 5], [1, DCH]]),
                        f"wd{c % 3}")
                return sch.dma("G", CT[c % 3][5:37, :],
                               AP(tau_dn_d, c * DCH, [[NBLK, 32], [1, DCH]]),
                               f"wd{c % 3}")

            def gen():
                umark, evmark, stok, mtoks = {}, {}, {}, {}
                ltok, etok = {}, {}

                def expand(c):
                    sch.wait("G", ltok[c])
                    if c >= 2:
                        sch.wait("G", umark[c - 2])
                    else:
                        sch.wait("G", msmk_dn)
                    for si in range(2):
                        sch.dma("G", AP(IN0[c % 2], (si * 32) * DCH,
                                        [[DCH, 16], [1, DCH]]),
                                AP(CT[c % 3], si * DCH,
                                   [[DCH, 4], [0, 4], [1, DCH]]),
                                f"de{c % 2}")
                        etok[c] = sch.dma(
                            "G", AP(IN1[c % 2], (si * 32) * DCH,
                                    [[DCH, 16], [1, DCH]]),
                            AP(CT[c % 3], (5 + si * 16) * DCH,
                               [[DCH, 16], [1, DCH]]),
                            f"de{c % 2}")

                def umult(c):
                    sch.wait("V", etok[c])
                    if c >= 2:
                        sch.wait("V", evmark[(c - 2) * NCG + NCG - 1])
                    sch.op("V", lambda v, b=c % 2: v.tensor_mul(
                        Ud[b][:, :], IN0[b][:, :], IN1[b][:, :]))
                    umark[c] = sch.mark("V")

                for c in range(NDCH):
                    yield
                    buf = c % 2
                    if c == 0:
                        for cpre in (0, 1):
                            mtoks[cpre] = sch.dma(
                                "G", MSK[cpre % 4][:, :],
                                wgm_dn_d[:, cpre * DCH:(cpre + 1) * DCH],
                                f"dm{cpre % 4}")
                        ltok[0] = ldtoks(0)
                        ltok[1] = ldtoks(1)
                        expand(0)
                        umult(0)
                    if c + 2 < NDCH:
                        if c >= 1:
                            sch.wait("G", etok[c - 1])
                        ltok[c + 2] = ldtoks(c + 2)
                    sch.wait("T", umark[c])
                    first_ev = True
                    for cg in range(NCG):
                        yield
                        i2 = c * NCG + cg
                        bk = i2 % depth
                        if i2 >= depth:
                            sch.wait("T", evmark[i2 - depth])
                        for u, po in ((0, pA[bk]), (32, pB[bk])):
                            sch.op("T", lambda t, u_=u, b=buf, pp=po,
                                   cc=cg * 512:
                                   t.matmul(pp[:, :], at_sb[u_:u_ + 16, :],
                                            Ud[b][u_:u_ + 16, cc:cc + 512],
                                            start=True, stop=True,
                                            tile_position=(u_, 0)))
                        mm = sch.mark("T")
                        sch.wait("S", mm)
                        if first_ev:
                            if c >= 4:
                                sch.wait("S", stok[c - 4])
                            sch.wait("V", mtoks[c])
                            first_ev = False
                        wo_ap = WO[c % 4][:, cg * 512:(cg + 1) * 512]
                        mk_ap = MSK[c % 4][:, cg * 512:(cg + 1) * 512]
                        sch.op("S", lambda s, o_=wo_ap, pp=pB[bk]:
                               s.copy(o_, pp[:, :]))
                        smk = sch.mark("S")
                        sch.wait("V", smk)
                        sch.op("V", lambda v, o_=wo_ap, m_=mk_ap, pp=pA[bk]:
                               v.copy_predicated(o_, m_, pp[:, :]))
                        evmark[i2] = sch.mark("V")
                        if cg == 1 and c + 1 < NDCH:
                            expand(c + 1)
                        if cg == NCG // 2 and c + 1 < NDCH:
                            umult(c + 1)
                    sch.wait("Y", evmark[c * NCG + NCG - 1])
                    stok[c] = sch.dma(
                        "Y",
                        AP(wbuf_d, c * DCH, [[64 * D, 128], [1, DCH]]),
                        WO[c % 4][:, :],
                        f"dws{c % 4}")
                    if c + 2 < NDCH:
                        if c >= 2:
                            sch.wait("G", evmark[(c - 2) * NCG + NCG - 1])
                        mtoks[c + 2] = sch.dma(
                            "G", MSK[(c + 2) % 4][:, :],
                            wgm_dn_d[:, (c + 2) * DCH:(c + 3) * DCH],
                            f"dm{(c + 2) % 4}")
                while True:
                    yield
            return gen()

        qscope = ctx.enter_context(contextlib.ExitStack())
        q_all = qscope.enter_context(
            nc.sbuf_tensor("qall", [128, H * TPC], BF16))
        if "W" in PHASES:
            with contextlib.ExitStack() as wsc:
                pguA = [wsc.enter_context(
                    nc.psum_tensor(f"pguA{i}", [128, 512], F32))
                    for i in range(4)]
                pguB = [wsc.enter_context(
                    nc.psum_tensor(f"pguB{i}", [128, 512], F32))
                    for i in range(4)]
                gugen = make_gu_gen(wsc, pguA, pguB, depth=4)
                for _ in range(NGCH * 9 + 20):
                    next(gugen)
            sch.barrier()

        if "B" in PHASES:
            with contextlib.ExitStack() as bc:

                x2T = bc.enter_context(nc.sbuf_tensor("x2T", [128, 16 * S], BF16))
                xt = [bc.enter_context(nc.sbuf_tensor(f"xt{i}", [128, D], F32))
                      for i in range(2)]
                wq = [bc.enter_context(nc.sbuf_tensor(f"wq{i}", [128, 4096], BF16))
                      for i in range(2)]
                kst = [bc.enter_context(nc.sbuf_tensor(f"kst{i}", [128, 512], BF16))
                       for i in range(4)]
                with contextlib.ExitStack() as tc:
                    ptr = [tc.enter_context(
                        nc.psum_tensor(f"ptr{i}", [128, 1024], BF16))
                        for i in range(2)]
                    ln_ev, nmks, trmk = {}, {}, {}

                    def b_transpose(ti):
                        for hc in range(16):
                            p = ptr[hc // 8]
                            if hc % 8 == 0:
                                sch.wait("T", nmks[ti])
                                if ti >= 1:
                                    sch.wait("T", ln_ev.get((ti - 1, hc // 8)))
                            sch.op("T", lambda t, p_=p, h=hc, b=ti % 2:
                                   t.matmul(p_[:, (h % 8) * 128:(h % 8) * 128 + 128],
                                            lnsqs[b][:, h * 128:h * 128 + 128],
                                            ident[:, :], is_transpose=True,
                                            start=(h % 8 == 0), stop=(h % 8 == 7)))
                            if hc % 8 == 7:
                                tmk = sch.mark("T")
                                trmk[ti] = tmk
                                sch.wait("V", tmk)
                                sch.op("V", lambda v, p_=p, t_=ti, g=hc // 8:
                                       v.tensor_copy(
                                           AP(x2T, (g * 8) * S + t_ * 128,
                                              [[16 * S, 128], [S, 8], [1, 128]]),
                                           p_[:, :]))
                                ln_ev[(ti, hc // 8)] = sch.mark("V")

                    for ti in range(16):
                        bufp = ti % 2
                        if ti >= 2:
                            sch.wait("Y", nmks[ti - 2])
                        xtok = sch.dma("Y", xt[bufp][:, :],
                                       x_row[ti * 128:(ti + 1) * 128, :],
                                       f"bx{ti % 2}")
                        sch.wait("V", xtok)
                        sch.wait("S", xtok)
                        if ti >= 2:
                            sch.wait("S", trmk[ti - 2])
                            sch.wait("V", trmk[ti - 2])
                        _emit_ln(sch, col, 8 * bufp, xt[bufp][:, :],
                                 lnsqs[bufp], out_bf=lnsqs[bufp])
                        nmks[ti] = sch.mark("V")
                        if ti >= 1:
                            b_transpose(ti - 1)
                    b_transpose(15)
                    sch.barrier()
                with contextlib.ExitStack() as qc:
                    pq = [qc.enter_context(
                        nc.psum_tensor(f"pq{i}", [128, 512], F32))
                        for i in range(4)]
                    evtix, wtok, kdma, qmmk = {}, {}, {}, {}
                    for oc in range(48):
                        if oc % 2 == 0:
                            if oc >= 4:
                                sch.wait("G", qmmk[oc - 3])
                            wtok[oc // 2] = sch.dma(
                                "G", wq[(oc // 2) % 2][:, :],
                                AP(wqkvT, oc * 128,
                                   [[3 * D, 128], [128 * 3 * D, 16], [1, 256]]),
                                f"wq{(oc // 2) % 2}")
                        tgs = [0, 2] if oc < 16 else [0, 1, 2, 3]
                        kind, hh = oc // 16, oc % 16
                        sch.wait("T", wtok[oc // 2])
                        for tix, tg in enumerate(tgs):
                            sch.wait("T", evtix.get((oc - 1, tix)))
                            for hc in range(16):
                                sch.op("T", lambda t, h=hc, o=oc, g=tg, bi=tix:
                                       t.matmul(
                                           pq[bi][:, :],
                                           wq[(o // 2) % 2][:, h * 256 + (o % 2) * 128:
                                                            h * 256 + (o % 2) * 128 + 128],
                                           x2T[:, h * S + g * 512:h * S + g * 512 + 512],
                                           start=(h == 0), stop=(h == 15)))
                            qmmk[oc] = qmm = sch.mark("T")
                            sch.wait("V", qmm)
                            if kind == 0:
                                dst = q_all[:, hh * TPC + (tg // 2) * 512:
                                            hh * TPC + (tg // 2) * 512 + 512]
                                sch.op("V", lambda v, d=dst, bi=tix, o=oc:
                                       v.tensor_scalar(
                                           d, pq[bi][:, :],
                                           bqkv[:, o:o + 1], None, ALU.add))
                            else:
                                if kdma.get(tix) is not None:
                                    sch.wait("V", kdma[tix])
                                sch.op("V", lambda v, d=kst[tix],
                                       bi=tix, o=oc:
                                       v.tensor_scalar(
                                           d[:, :], pq[bi][:, :],
                                           bqkv[:, o:o + 1], None, ALU.add))
                                vmk = sch.mark("V")
                                sch.wait("Y", vmk)
                                kdma[tix] = sch.dma(
                                    "Y",
                                    AP(kvbuf, ((kind - 1) * 16 + hh) * 128 * S
                                       + tg * 512, [[S, 128], [1, 512]]),
                                    kst[tix][:, :], f"kv{tix}")
                            evtix[(oc, tix)] = sch.mark("V")
                    if gugen is not None:
                        for _ in range(24):
                            next(gugen)
            sch.barrier()

        attnout = qscope.enter_context(
            nc.sbuf_tensor("attnout", [128, H * TPC], BF16))
        if "C" in PHASES:
            with contextlib.ExitStack() as cc:
                cmask = cc.enter_context(
                    nc.sbuf_tensor("cmasksb", [128, 16 * 512], BF16))
                cmtok = sch.dma(
                    "Y", AP(cmask, 0, [[16 * 512, 128], [512, 16], [1, 512]]),
                    AP(cm_d, 0, [[512, 128], [128 * 512, 16], [1, 512]]), "c2")
                sch.wait("V", cmtok)
                kb = [cc.enter_context(nc.sbuf_tensor(f"kb{i}", [128, S], BF16))
                      for i in range(2)]
                vb = [cc.enter_context(nc.sbuf_tensor(f"vb{i}", [128, S], BF16))
                      for i in range(2)]
                vT = [cc.enter_context(nc.sbuf_tensor(f"vT{i}", [128, S], BF16))
                      for i in range(2)]
                pT = [cc.enter_context(nc.sbuf_tensor(f"pT{i}", [128, 16 * 512], BF16))
                      for i in range(2)]
                rs = cc.enter_context(nc.sbuf_tensor("rs", [1, 512], BF16))
                bc_sb = cc.enter_context(nc.sbuf_tensor("bcsb", [128, 512], BF16))
                psc = [cc.enter_context(nc.psum_tensor(f"psc{i}", [128, 512], F32))
                       for i in range(4)]
                pvt = cc.enter_context(nc.psum_tensor("pvt", [128, 1024], BF16))
                pav = [cc.enter_context(nc.psum_tensor(f"pav{i}", [128, 512], F32))
                       for i in range(2)]
                prow = cc.enter_context(nc.psum_tensor("prow", [1, 512], F32))

                vtev = {}
                expmark = {}
                scmark = {}
                attn_emk = {}
                rmk = {}
                pending_tail = [None]

                def emit_tail(it_p):
                    h_p, ts_p = it_p // 2, it_p % 2
                    sch.wait("T", rmk[it_p])
                    if it_p >= 1:
                        sch.wait("T", attn_emk.get(it_p - 1))
                    sch.op("T", lambda t: t.matmul(
                        psc[2][:, :], ones[0:1, 0:128], rs[:, :],
                        start=True, stop=True))
                    bmk = sch.mark("T")
                    sch.wait("S", bmk)
                    sch.op("S", lambda s: s.copy(bc_sb[:, :], psc[2][:, :]))
                    csm = sch.mark("S")
                    sch.wait("V", csm)
                    sch.op("V", lambda v, ts=ts_p, hh=h_p, pa=pav[it_p % 2]:
                           v.tensor_mul(
                               attnout[:, hh * TPC + ts * 512:
                                       hh * TPC + ts * 512 + 512],
                               pa[:, :], bc_sb[:, :]))
                    attn_emk[it_p] = sch.mark("V")

                for h in range(H):
                    hb = h % 2
                    if h >= 2:
                        sch.wait("Y", attn_emk[2 * (h - 2) + 1])
                    ktok = sch.dma("Y", kb[hb][:, :],
                                   AP(kvbuf, h * 128 * S, [[S, 128], [1, S]]),
                                   f"ck{h % 2}")
                    vtok = sch.dma("Y", vb[hb][:, :],
                                   AP(kvbuf, (16 + h) * 128 * S, [[S, 128], [1, S]]),
                                   f"cv{h % 2}")
                    sch.wait("T", vtok)
                    for g2 in range(2):
                        if (h, g2) != (0, 0):
                            prevk = (h, g2 - 1) if g2 else (h - 1, 1)
                            sch.wait("T", vtev[prevk])
                        for tkc in range(8 * g2, 8 * g2 + 8):
                            sch.op("T", lambda t, c=tkc, b=hb:
                                   t.matmul(pvt[:, (c % 8) * 128:(c % 8) * 128 + 128],
                                            vb[b][:, c * 128:c * 128 + 128],
                                            ident[:, :], is_transpose=True,
                                            start=(c % 8 == 0), stop=(c % 8 == 7)))
                        tmk = sch.mark("T")
                        sch.wait("V", tmk)
                        sch.op("V", lambda v, b=hb, g=g2:
                               v.tensor_copy(vT[b][:, g * 1024:(g + 1) * 1024],
                                             pvt[:, :]))
                        vtev[(h, g2)] = sch.mark("V")
                    sch.wait("T", vtev[(h, 0)])
                    sch.wait("T", vtev[(h, 1)])
                    sch.wait("T", ktok)
                    for tslot in range(2):
                        C = CTX[tslot]
                        it = 2 * h + tslot
                        sch.wait("S", attn_emk.get(it - 2))
                        for tkc in range(C):
                            if tkc >= 4:
                                sch.wait("T", expmark[(it, tkc - 4)])
                            elif it >= 1:
                                last = [c for c in range(CTX[(it - 1) % 2])
                                        if c % 4 == tkc % 4]
                                if last:
                                    sch.wait("T", expmark[(it - 1, last[-1])])
                            if tkc % 4 == 2:
                                sch.wait("T", attn_emk.get(it - 1))
                            sch.op("T", lambda t, b=hb, c=tkc, ts=tslot, hh=h:
                                   t.matmul(psc[c % 4][:, :],
                                            kb[b][:, c * 128:c * 128 + 128],
                                            q_all[:, hh * TPC + ts * 512:
                                                  hh * TPC + ts * 512 + 512],
                                            start=True, stop=True))
                            scmark[(it, tkc)] = sch.mark("T")
                            if tkc == 1 and pending_tail[0] is not None:
                                emit_tail(pending_tail[0])
                                pending_tail[0] = None
                            sch.wait("S", scmark[(it, tkc)])
                            sch.op("S", lambda s, b=tslot, c=tkc:
                                   s.activation(pT[b][:, c * 512:c * 512 + 512],
                                                psc[c % 4][:, :], AF.Exp,
                                                bias=0.0, scale=SCALE))
                            expmark[(it, tkc)] = sch.mark("S")
                            if (tslot == 0) or (tkc >= 8):
                                sch.wait("V", expmark[(it, tkc)])
                                sch.op("V", lambda v, b=tslot, c=tkc:
                                       v.tensor_mul(
                                           pT[b][:, c * 512:c * 512 + 512],
                                           pT[b][:, c * 512:c * 512 + 512],
                                           cmask[:, c * 512:c * 512 + 512]))
                                expmark[(it, tkc)] = sch.mark("V")
                            if tkc >= 2:
                                _emit_sumav(sch, it, tkc - 2, C, hb, tslot,
                                            pT, vT, prow, pav, ones, expmark,
                                            rmk, attn_emk)
                        for c2 in (C - 2, C - 1):
                            _emit_sumav(sch, it, c2, C, hb, tslot, pT, vT,
                                        prow, pav, ones, expmark, rmk,
                                        attn_emk)
                        avm = sch.mark("T")
                        sch.wait("V", avm)
                        def _recip(v):
                            with nc.allow_low_precision(reason="softmax 1/s"):
                                return v.reciprocal(rs[:, :], prow[:, :])
                        sch.op("V", _recip)
                        rmk[it] = sch.mark("V")
                        pending_tail[0] = it
                emit_tail(pending_tail[0])
                pending_tail[0] = None
            sch.barrier()
        if "D" in PHASES:
            with contextlib.ExitStack() as dc:
                def pumpd(n=1):
                    pass
                yfeat = dc.enter_context(
                    nc.sbuf_tensor("yfeat", [128, 16 * TPC], BF16))
                wo = [dc.enter_context(nc.sbuf_tensor(f"wosb{i}", [128, 4096], BF16))
                      for i in range(2)]
                ytok = dc.enter_context(nc.sbuf_tensor("ytok", [128, D], F32))
                lno = ytok
                bout = dc.enter_context(nc.sbuf_tensor("boutsb", [128, D], F32))
                xti = [dc.enter_context(nc.sbuf_tensor(f"xtiD{i}", [128, D], F32))
                       for i in range(2)]
                botok = sch.dma("Y", bout[:, :], bout_d[:, :], "c3")
                sch.wait("V", botok)
                with contextlib.ExitStack() as pc:
                    po = [pc.enter_context(
                        nc.psum_tensor(f"po{i}", [128, 512], F32))
                        for i in range(4)]
                    ev, wtok, ommk = {}, {}, {}
                    for oc in range(16):
                        if oc % 2 == 0:
                            if oc >= 4:
                                sch.wait("G", ommk[oc - 3])
                            wtok[oc // 2] = sch.dma(
                                "G", wo[(oc // 2) % 2][:, :],
                                AP(woutT, oc * 128,
                                   [[D, 128], [128 * D, 16], [1, 256]]),
                                f"wo{(oc // 2) % 2}")
                        bks = (oc % 2) * 2
                        sch.wait("T", wtok[oc // 2])
                        if oc >= 2:
                            sch.wait("T", ev[oc - 2])
                        for dcx in range(16):
                            pumpd(1)
                            for tg in range(2):
                                sch.op("T", lambda t, d_=dcx, o=oc, g=tg, bi=bks + tg:
                                       t.matmul(
                                           po[bi][:, :],
                                           wo[(o // 2) % 2][:, d_ * 256 + (o % 2) * 128:
                                                            d_ * 256 + (o % 2) * 128 + 128],
                                           attnout[:, d_ * TPC + g * 512:
                                                   d_ * TPC + g * 512 + 512],
                                           start=(d_ == 0), stop=(d_ == 15)))
                        ommk[oc] = omm = sch.mark("T")
                        sch.wait("S", omm)
                        for tg in range(2):
                            sch.op("S", lambda s, o=oc, g=tg, bi=bks + tg:
                                   s.copy(yfeat[:, o * TPC + g * 512:
                                                o * TPC + g * 512 + 512],
                                          po[bi][:, :]))
                        ev[oc] = sch.mark("S")
                    sch.barrier()
                with contextlib.ExitStack() as pc:
                    ptb = [pc.enter_context(
                        nc.psum_tensor(f"ptb{i}", [128, 1024], BF16))
                        for i in range(2)]
                    prev = {}
                    xst = {}
                    for to in range(8):
                        pumpd(4)
                        if to >= 2:
                            sch.wait("Y", xst[to - 2])
                        xtok = sch.dma("Y", xti[to % 2][:, :],
                                       x_row[_own_tile(to) * 128:
                                             _own_tile(to) * 128 + 128, :],
                                       f"dx{to % 2}")
                        tg, tl = to // 4, to % 4
                        for ocx in range(16):
                            p = ptb[ocx // 8]
                            if ocx % 8 == 0 and to >= 1:
                                sch.wait("T", prev.get((to - 1, ocx // 8)))
                            sch.op("T", lambda t, p_=p, o=ocx, g=tg, l=tl:
                                   t.matmul(p_[:, (o % 8) * 128:(o % 8) * 128 + 128],
                                            yfeat[:, o * TPC + g * 512 + l * 128:
                                                  o * TPC + g * 512 + l * 128 + 128],
                                            ident[:, :], is_transpose=True,
                                            start=(o % 8 == 0), stop=(o % 8 == 7)))
                            if ocx % 8 == 7:
                                tmk = sch.mark("T")
                                sch.wait("V", tmk)
                                if to >= 1:
                                    sch.wait("V", prev[to - 1])
                                sch.op("V", lambda v, p_=p, g=ocx // 8:
                                       v.scalar_tensor_tensor(
                                           ytok[:, g * 1024:(g + 1) * 1024],
                                           p_[:, :], 0.0,
                                           bout[:, g * 1024:(g + 1) * 1024],
                                           ALU.add, ALU.add))
                                prev[(to, ocx // 8)] = sch.mark("V")
                        sch.wait("S", prev[(to, 0)])
                        sch.wait("S", prev[(to, 1)])
                        _emit_ln(sch, col, 8 * (to % 2), ytok[:, :], lnsq,
                                 out_f32=lno)
                        sch.wait("V", xtok)
                        sch.op("V", lambda v, o=to:
                               v.tensor_add(xti[o % 2][:, :], xti[o % 2][:, :],
                                            lno[:, :]))
                        prev[to] = sch.mark("V")
                        sch.wait("Y", prev[to])
                        xst[to] = sch.dma("Y",
                                          xmid[to * 128:(to + 1) * 128, :],
                                          xti[to % 2][:, :], f"dxs{to % 2}")
                    pumpd(400)
            sch.barrier()
        qscope.close()

        if "W" in PHASES:
            with contextlib.ExitStack() as wdc:
                pdnA = [wdc.enter_context(
                    nc.psum_tensor(f"pdnA{i}", [128, 512], F32))
                    for i in range(4)]
                pdnB = [wdc.enter_context(
                    nc.psum_tensor(f"pdnB{i}", [128, 512], F32))
                    for i in range(4)]
                dgen = make_down_gen(wdc, pdnA, pdnB, depth=4)
                for _ in range(NDCH * 9 + 20):
                    next(dgen)
            sch.barrier()

        if "E" in PHASES:
            with contextlib.ExitStack() as ec:
                x2f = ec.enter_context(nc.sbuf_tensor("x2f", [128, 16 * 512], BF16))
                act = ec.enter_context(nc.sbuf_tensor("actsb", [128, 64 * 512], BF16))
                wg = [ec.enter_context(nc.sbuf_tensor(f"wgsb{i}", [128, 2048], BF16))
                      for i in range(4)]
                wu = [ec.enter_context(nc.sbuf_tensor(f"wusb{i}", [128, 2048], BF16))
                      for i in range(4)]
                wd = [ec.enter_context(nc.sbuf_tensor(f"wdsb{i}", [128, 2048], BF16))
                      for i in range(2)]
                xtie = [ec.enter_context(nc.sbuf_tensor(f"xtiE{i}", [128, D], F32))
                       for i in range(2)]
                sil = [ec.enter_context(nc.sbuf_tensor(f"silsb{i}", [128, 512], BF16))
                       for i in range(4)]
                y2f = ec.enter_context(nc.sbuf_tensor("y2f", [128, 16 * 512], BF16))
                ytoke = ec.enter_context(nc.sbuf_tensor("ytokE", [128, D], F32))
                lnoe = ytoke

                for g in range(2):
                    sch.barrier()
                    with contextlib.ExitStack() as pc:
                        ptr = [pc.enter_context(
                            nc.psum_tensor(f"ptrE{g}{i}", [128, 1024], BF16))
                            for i in range(2)]
                        pev, nmks, trmk = {}, {}, {}

                        def e_transpose(tl):
                            for hc in range(16):
                                p = ptr[hc // 8]
                                if hc % 8 == 0:
                                    sch.wait("T", nmks[tl])
                                    if tl >= 1:
                                        sch.wait("T", pev.get((tl - 1, hc // 8)))
                                sch.op("T", lambda t, p_=p, h=hc, b=tl % 2:
                                       t.matmul(p_[:, (h % 8) * 128:(h % 8) * 128 + 128],
                                                lnsqs[b][:, h * 128:h * 128 + 128],
                                                ident[:, :], is_transpose=True,
                                                start=(h % 8 == 0), stop=(h % 8 == 7)))
                                if hc % 8 == 7:
                                    tmk = sch.mark("T")
                                    trmk[tl] = tmk
                                    sch.wait("V", tmk)
                                    sch.op("V", lambda v, p_=p, t_=tl, gg=hc // 8:
                                           v.tensor_copy(
                                               AP(x2f, (gg * 8) * 512 + t_ * 128,
                                                  [[16 * 512, 128], [512, 8], [1, 128]]),
                                               p_[:, :]))
                                    pev[(tl, hc // 8)] = sch.mark("V")

                        for tl in range(4):
                            to = g * 4 + tl
                            if tl >= 2:
                                sch.wait("Y", nmks[tl - 2])
                            xtok = sch.dma("Y", xtie[tl % 2][:, :],
                                           xmid[to * 128:(to + 1) * 128, :],
                                           f"ex{tl % 2}")
                            sch.wait("V", xtok)
                            sch.wait("S", xtok)
                            if tl >= 2:
                                sch.wait("S", trmk[tl - 2])
                                sch.wait("V", trmk[tl - 2])
                            _emit_ln(sch, col, 8 * (tl % 2),
                                     xtie[tl % 2][:, :], lnsqs[tl % 2],
                                     out_bf=lnsqs[tl % 2])
                            nmks[tl] = sch.mark("V")
                            if tl >= 1:
                                e_transpose(tl - 1)
                        e_transpose(3)
                        sch.barrier()
                    with contextlib.ExitStack() as pc:
                        pg = [pc.enter_context(
                            nc.psum_tensor(f"pgE{g}{i}", [128, 512], F32))
                            for i in range(4)]
                        pu = [pc.enter_context(
                            nc.psum_tensor(f"puE{g}{i}", [128, 512], F32))
                            for i in range(4)]
                        ev, wtokg, wtoku, gmmk = {}, {}, {}, {}
                        for oc in range(64):
                            if oc >= 4:
                                sch.wait("Y", gmmk[oc - 4])
                            wtokg[oc] = sch.dma(
                                "Y", wg[oc % 4][:, :],
                                AP(wbuf_g, oc * GCH, [[NBLK, 128], [1, GCH]]),
                                f"eg{oc % 4}")
                            wtoku[oc] = sch.dma(
                                "Y", wu[oc % 4][:, :],
                                AP(wbuf_u, oc * GCH, [[NBLK, 128], [1, GCH]]),
                                f"eu{oc % 4}")
                            bkg = oc % 4
                            sch.wait("T", wtokg[oc])
                            sch.wait("T", wtoku[oc])
                            if oc >= 4:
                                sch.wait("T", ev[oc - 4])
                            for hc in range(16):
                                sch.op("T", lambda t, h=hc, o=oc, bi=bkg:
                                       t.matmul(
                                           pg[bi][:, :],
                                           wg[o % 4][:, h * 128:h * 128 + 128],
                                           x2f[:, h * 512:h * 512 + 512],
                                           start=(h == 0), stop=(h == 15)))
                                sch.op("T", lambda t, h=hc, o=oc, bi=bkg:
                                       t.matmul(
                                           pu[bi][:, :],
                                           wu[o % 4][:, h * 128:h * 128 + 128],
                                           x2f[:, h * 512:h * 512 + 512],
                                           start=(h == 0), stop=(h == 15)))
                            gmmk[oc] = gmm = sch.mark("T")
                            sch.wait("S", gmm)
                            if oc >= 4:
                                sch.wait("S", ev[oc - 4])
                            sch.op("S", lambda s, bi=bkg, o=oc:
                                   s.activation(sil[o % 4][:, :], pg[bi][:, :],
                                                AF.Silu, bias=0.0, scale=1.0))
                            smk = sch.mark("S")
                            sch.wait("V", smk)
                            sch.op("V", lambda v, o=oc, bi=bkg:
                                   v.tensor_mul(act[:, o * 512:o * 512 + 512],
                                                sil[o % 4][:, :], pu[bi][:, :]))
                            ev[oc] = sch.mark("V")
                        sch.barrier()
                    with contextlib.ExitStack() as pc:
                        pd = [pc.enter_context(
                            nc.psum_tensor(f"pdE{g}{i}", [128, 512], F32))
                            for i in range(4)]
                        ev, wtokd, dmmk = {}, {}, {}
                        for hcp in range(8):
                            bks = (hcp % 2) * 2
                            if hcp >= 2:
                                sch.wait("T", ev[hcp - 2])
                            for ic in range(64):
                                if ic % 8 == 0:
                                    icq = ic // 8
                                    prevq = (hcp, icq - 2) if icq >= 2 else \
                                            ((hcp - 1, 6 + icq) if hcp >= 1 else None)
                                    if prevq is not None:
                                        sch.wait("Y", dmmk.get(prevq))
                                    wtokd[(hcp, icq)] = sch.dma(
                                        "Y", wd[icq % 2][:, :],
                                        AP(wbuf_d, icq * 8 * D + hcp * 256,
                                           [[64 * D, 128], [D, 8], [1, 256]]),
                                        f"ed{icq % 2}")
                                    sch.wait("T", wtokd[(hcp, icq)])
                                for o2 in range(2):
                                    sch.op("T", lambda t, i_=ic, o=o2, bi=bks:
                                           t.matmul(
                                               pd[bi + o][:, :],
                                               wd[(i_ // 8) % 2][:, (i_ % 8) * 256 + o * 128:
                                                            (i_ % 8) * 256 + o * 128 + 128],
                                               act[:, i_ * 512:i_ * 512 + 512],
                                               start=(i_ == 0), stop=(i_ == 63)))
                                if ic % 8 == 7:
                                    dmmk[(hcp, ic // 8)] = sch.mark("T")
                            dmm = sch.mark("T")
                            sch.wait("S", dmm)
                            if hcp >= 2:
                                sch.wait("S", ev[hcp - 2])
                            for o2 in range(2):
                                sch.op("S", lambda s, h=hcp, o=o2, bi=bks:
                                       s.copy(y2f[:, (2 * h + o) * 512:
                                                  (2 * h + o) * 512 + 512],
                                              pd[bi + o][:, :]))
                            ev[hcp] = sch.mark("S")
                        sch.barrier()
                    with contextlib.ExitStack() as pc:
                        ptb = [pc.enter_context(
                            nc.psum_tensor(f"ptbE{g}{i}", [128, 1024], BF16))
                            for i in range(2)]
                        prev = {}
                        ost = {}
                        for tl in range(4):
                            to = g * 4 + tl
                            if tl >= 2:
                                sch.wait("Y", ost[tl - 2])
                            xtok = sch.dma("Y", xtie[tl % 2][:, :],
                                           xmid[to * 128:(to + 1) * 128, :],
                                           f"e4x{tl % 2}")
                            for ocx in range(16):
                                p = ptb[ocx // 8]
                                if ocx % 8 == 0 and tl >= 1:
                                    sch.wait("T", prev.get((tl - 1, ocx // 8)))
                                sch.op("T", lambda t, p_=p, o=ocx, l=tl:
                                       t.matmul(p_[:, (o % 8) * 128:(o % 8) * 128 + 128],
                                                y2f[:, o * 512 + l * 128:
                                                    o * 512 + l * 128 + 128],
                                                ident[:, :], is_transpose=True,
                                                start=(o % 8 == 0), stop=(o % 8 == 7)))
                                if ocx % 8 == 7:
                                    tmk = sch.mark("T")
                                    sch.wait("S", tmk)
                                    if tl >= 1:
                                        sch.wait("S", prev[tl - 1])
                                    sch.op("S", lambda s, p_=p, gg=ocx // 8:
                                           s.copy(ytoke[:, gg * 1024:(gg + 1) * 1024],
                                                  p_[:, :]))
                                    prev[(tl, ocx // 8)] = sch.mark("S")
                            sch.wait("V", prev[(tl, 0)])
                            sch.wait("V", prev[(tl, 1)])
                            _emit_ln(sch, col, 8 * (tl % 2), ytoke[:, :], lnsq,
                                     out_f32=lnoe)
                            sch.wait("V", xtok)
                            sch.op("V", lambda v, l=tl:
                                   v.tensor_add(xtie[l % 2][:, :],
                                                xtie[l % 2][:, :], lnoe[:, :]))
                            fmk = sch.mark("V")
                            prev[tl] = fmk
                            sch.wait("Y", fmk)
                            ost[tl] = sch.dma(
                                "Y", out_d[to * 128:(to + 1) * 128, :],
                                xtie[tl % 2][:, :], f"out{tl % 2}")

        sch.finish("Y")
        sch.emit(nc, ctx)

    return nc


_TPR = {}


def _perm_gu():
    if "gu" not in _TPR:
        p = np.arange(GCH).reshape(GCH // 16, 16).T.ravel()
        _TPR["gu"] = (np.arange(NBLK).reshape(-1, GCH)[:, p]).ravel()
    return _TPR["gu"]


def _perm_dn():
    if "dn" not in _TPR:
        _TPR["dn"] = (np.arange(D)[None, :] * 64
                      + np.arange(64)[:, None]).ravel()
    return _TPR["dn"]


def _tau_tiled():
    cst = _consts()
    tp = cst["taupow"]
    return np.concatenate([np.tile(tp[0:4], (4, 1)), np.tile(tp[4:8], (4, 1))])


def _tpr0():
    if 0 not in _TPR:
        _TPR[0] = np.ascontiguousarray(_tau_tiled()[:, _perm_gu()])
    return _TPR[0]


def _tpr1():
    if 1 not in _TPR:
        _TPR[1] = np.ascontiguousarray(_tau_tiled()[:, _perm_dn()])
    return _TPR[1]


def _get_program():
    global _PROG
    if _PROG is None:
        _PROG = build_program()
    return _PROG


def kernel(L0_attn_x, L0_attn_w_qkv, L0_attn_b_qkv, L0_attn_w_out,
           L0_attn_b_out, L0_ffn_gate_cp, L0_ffn_up_cp, L0_ffn_down_cp):
    cst = _consts()
    nc = _get_program()

    x = np.ascontiguousarray(np.asarray(L0_attn_x, dtype=np.float32))
    wqkvT = np.ascontiguousarray(np.asarray(L0_attn_w_qkv, np.float32).T)
    woutT = np.ascontiguousarray(np.asarray(L0_attn_w_out, np.float32).T)
    bqkv = np.ascontiguousarray(
        np.asarray(L0_attn_b_qkv, np.float32).reshape(48, 128).T)
    bout_rep = np.ascontiguousarray(
        np.broadcast_to(np.asarray(L0_attn_b_out, np.float32)[None, :],
                        (128, D)))
    cps = [np.asarray(c, np.float32) for c in
           (L0_ffn_gate_cp, L0_ffn_up_cp, L0_ffn_down_cp)]
    taps = [c[cst["tapidx"]] for c in cps]
    tap_gu = np.ascontiguousarray(
        np.concatenate([taps[0], taps[1]])[:, _perm_gu()])
    tap_dn = np.ascontiguousarray(taps[2][:, _perm_dn()])
    wgm_gu = np.ascontiguousarray(cst["wgmask"][:, _perm_gu()])
    wgm_dn = np.ascontiguousarray(cst["wgmask"][:, _perm_dn()])

    in_maps = []
    for c in range(NCORES):
        b, j = c // 2, c % 2
        gk = [j, 1 - j, 2 + j, 3 - j]
        xr = np.ascontiguousarray(x[b].reshape(4, 512, D)[gk].reshape(S, D))
        in_maps.append(dict(
            x_row=xr, wqkvT=wqkvT, bqkv=bqkv, woutT=woutT,
            bout_rep=bout_rep, tap_gu=tap_gu, tap_dn=tap_dn,
            tau_gu=_tpr0(), tau_dn=_tpr1(),
            atilde=cst["atilde"], wgmask_gu=wgm_gu, wgmask_dn=wgm_dn,
            cmask=np.ascontiguousarray(cst["cmask"][j]),
            ident=cst["ident"], ones=cst["ones"],
        ))

    res = run_bass_kernel_spmd(nc, in_maps, core_ids=list(range(NCORES)))

    out = np.empty((B, S, D), dtype=np.float32)
    for c in range(NCORES):
        b, j = c // 2, c % 2
        o = res.results[c]["out"]
        out[b, j * 512:(j + 1) * 512] = o[0:512]
        out[b, (2 + j) * 512:(3 + j) * 512] = o[512:1024]
    return out
